# version 10
# baseline (speedup 1.0000x reference)
import math
import numpy as np
import concourse.bass as bass
import concourse.mybir as mybir
from concourse.bass_utils import run_bass_kernel_spmd
from concourse.alu_op_type import AluOpType as OP
from contextlib import ExitStack

F32 = mybir.dt.float32
BF16 = mybir.dt.bfloat16
I32 = mybir.dt.int32
AF = mybir.ActivationFunctionType
AX = mybir.AxisListType

DEBUG = False

D = 2048
H = 16
KVH = 4
HD = 128
NKEY = 2560
NT_KEY = 20
NT_OWN = 16
NG = 2
OWN_G = 8
KEY_G = 12
EPS_LN = 1e-5
EPS_RMS = 1.1920928955078125e-07
TW = NT_KEY * 64

ENGS = ("sync", "scalar", "vector", "tensor", "gpsimd")
DMA_ENGS = {"sync", "gpsimd"}
DMA_SLOTS = 6


class Prog:

    def __init__(self, nc, sems):
        self.nc = nc
        self.sems = sems
        self.ops = {e: [] for e in ENGS}
        self.cnt = {k: 0 for k in sems}
        for e in ENGS:
            self.cnt.setdefault(e, 0)
        self.wmax = {e: {} for e in ENGS}
        self.dirty = {e: False for e in ENGS}
        self.dma_i = {q: 0 for q in DMA_ENGS}

    def _waits(self, eng, waits):
        for tok in waits:
            if tok is None:
                continue
            pe, v = tok
            if self.wmax[eng].get(pe, 0) >= v:
                continue
            self.wmax[eng][pe] = v
            sem = self.sems[pe]
            self.ops[eng].append(lambda E, sem=sem, v=v: E.wait_ge(sem, v))

    def op(self, eng, fn, waits=(), inc=False):
        self._waits(eng, waits)
        if inc:
            self.cnt[eng] += 1
            sem = self.sems[eng]
            self.ops[eng].append(lambda E, fn=fn, sem=sem: fn(E).then_inc(sem, 1))
            self.dirty[eng] = False
            return (eng, self.cnt[eng])
        self.ops[eng].append(fn)
        self.dirty[eng] = True
        return None

    def dma(self, eng, out, in_, waits=(), slow=False):
        i = self.dma_i[eng]
        self.dma_i[eng] = i + 1
        slot, cyc = i % DMA_SLOTS, i // DMA_SLOTS
        key = f"{eng}{slot}"
        w = list(waits)
        if cyc > 0:
            w.append((key, 16 * cyc))
        self._waits(eng, w)
        self.cnt[key] = 16 * (cyc + 1)
        sem = self.sems[key]
        self.ops[eng].append(lambda E, o=out, i_=in_, sem=sem, slow=slow: E.dma_start(
            out=o, in_=i_, allow_slow_non_contiguous=slow).then_inc(sem, 16))
        return (key, 16 * (cyc + 1))

    def tok(self, eng):
        assert not self.dirty[eng], f"{eng} has uncounted trailing ops"
        return (eng, self.cnt[eng])

    def barrier(self):
        for e in ENGS:
            assert not self.dirty[e], f"barrier with uncounted ops on {e}"
        toks = [(k, v) for k, v in self.cnt.items() if v > 0]
        for e in ENGS:
            self._waits(e, toks)

    def emit(self, block):
        for eng in ENGS:
            def section(E, eng=eng):
                for f in self.ops[eng]:
                    f(E)
            getattr(block, eng)(section)


def build_program():
    nc = bass.Bass("TRN2", target_bir_lowering=False,
                   detect_race_conditions=False)
    d = {}
    def din(name, shape, dtype=F32):
        d[name] = nc.dram_tensor(name, shape, dtype, kind="ExternalInput").ap()
        return d[name]
    def dint(name, shape, dtype):
        d[name] = nc.dram_tensor(name, shape, dtype, kind="Internal").ap()
        return d[name]

    xin = din("xin", [NKEY, D])
    wqT = din("wqT", [D, D])
    wkT = din("wkT", [D, KVH * HD])
    wvT = din("wvT", [D, KVH * HD])
    woT = din("woT", [D, D])
    wff = din("wff", [D, D])
    lnw_h = din("lnw", [128, 16])
    lnb_h = din("lnb", [128, 16])
    qgain_h = din("qgain", [H])
    cosT_h = din("cosTh", [128, TW])
    sinT_h = din("sinTh", [128, TW])
    flag_h = din("flag", [128])
    m0_h = din("m0", [128, 128])
    m4_h = din("m4", [128, 128])
    idm_h = din("idm", [128, 128])
    yout = nc.dram_tensor("yout", [NT_OWN * 128, D], F32, kind="ExternalOutput").ap()

    s_wq = dint("s_wq", [D, D], BF16)
    s_wo = dint("s_wo", [D, D], BF16)
    s_weff = dint("s_weff", [D, D], BF16)

    dbg = {}
    if DEBUG:
        for nm, shp, dt in [("dbg_small", [128, 64], F32), ("dbg_xn", [128, D], BF16),
                            ("dbg_kbf", [128, 512], BF16), ("dbg_vtok", [128, 512], BF16),
                            ("dbg_qbf", [128, D], BF16), ("dbg_pT", [128, 640], BF16),
                            ("dbg_y", [128, 128], BF16), ("dbg_xn2a", [128, D], BF16),
                            ("dbg_cbf", [1, D], BF16), ("dbg_invf", [128, 64], F32),
                            ("dbg_weff", [128, D], BF16), ("dbg_costile", [128, 64], BF16),
                            ("dbg_sintile", [128, 64], BF16), ("dbg_xnfm", [128, 128], BF16),
                            ("dbg_lns", [128, 16], F32), ("dbg_qfm", [128, 2048], BF16)]:
            dbg[nm] = nc.dram_tensor(nm, shp, dt, kind="ExternalOutput").ap()

    sems = {}
    stack = ExitStack()
    for e in ("scalar", "vector", "tensor"):
        sems[e] = stack.enter_context(nc.semaphore(f"sem_{e}"))
    for q in DMA_ENGS:
        for i in range(DMA_SLOTS):
            sems[f"{q}{i}"] = stack.enter_context(nc.semaphore(f"sem_{q}{i}"))
    P = Prog(nc, sems)
    E = stack.enter_context

    cst = E(nc.sbuf_tensor("cst", [128, 4], F32))
    sc128 = E(nc.sbuf_tensor("sc128", [128, 1], F32))
    negB = E(nc.sbuf_tensor("negB", [128, H], F32))
    grep = E(nc.sbuf_tensor("grep", [128, H], F32))
    flag_sb = E(nc.sbuf_tensor("flag_sb", [128, 1], F32))
    m0_bf = E(nc.sbuf_tensor("m0_bf", [128, 128], BF16))
    m0f_bf = E(nc.sbuf_tensor("m0f_bf", [128, 128], BF16))
    m4_bf = E(nc.sbuf_tensor("m4_bf", [128, 128], BF16))
    idm_bf = E(nc.sbuf_tensor("idm_bf", [128, 128], BF16))
    ones_r_bf = E(nc.sbuf_tensor("ones_r_bf", [1, 128], BF16))
    ones_r_f = E(nc.sbuf_tensor("ones_r_f", [1, 128], F32))
    ones_c_f = E(nc.sbuf_tensor("ones_c_f", [128, 1], F32))
    ones_c_bf = E(nc.sbuf_tensor("ones_c_bf", [128, 1], BF16))
    c_bf = E(nc.sbuf_tensor("c_bf", [1, D], BF16))
    cosT = E(nc.sbuf_tensor("cosT", [128, TW], BF16))
    sinT = E(nc.sbuf_tensor("sinT", [128, TW], BF16))

    with ExitStack() as p0:
        e0 = p0.enter_context
        sm1 = e0(nc.sbuf_tensor("sm1", [1, 128], F32))
        sm2 = e0(nc.sbuf_tensor("sm2", [1, 128], F32))
        sm3 = e0(nc.sbuf_tensor("sm3", [1, 128], F32))
        mload = e0(nc.sbuf_tensor("mload", [128, 3 * 128], F32))
        tbl = [e0(nc.sbuf_tensor(f"tbl{i}", [128, TW], F32)) for i in range(2)]
        ps_rep = e0(nc.psum_tensor("ps_rep", [128, 64], F32))

        def memsets(V):
            V.memset(cst[:, 0:1], EPS_LN)
            V.memset(cst[:, 1:2], EPS_RMS)
            V.memset(cst[:, 2:3], 128.0)
            V.memset(cst[:, 3:4], 10000.0)
            V.memset(ones_r_bf[:], 1.0)
            V.memset(ones_r_f[:], 1.0)
            V.memset(ones_c_f[:], 1.0)
            return V.memset(ones_c_bf[:], 1.0)
        P.op("vector", memsets, inc=True)
        t_con = P.tok("vector")

        tks = []
        t_ctl = P.dma("sync", tbl[0][:], cosT_h)
        t_stl = P.dma("sync", tbl[1][:], sinT_h)
        tks.append(P.op("vector", lambda V: V.tensor_copy(cosT[:], tbl[0][:]),
                        waits=[t_ctl], inc=True))
        tks.append(P.op("scalar", lambda S: S.copy(sinT[:], tbl[1][:]),
                        waits=[t_stl], inc=True))
        tks.append(P.dma("sync", sm2[0:1, 0:16], qgain_h.unsqueeze(0)))
        tks.append(P.dma("sync", flag_sb[:, 0:1], flag_h.unsqueeze(1)))
        tks.append(P.dma("sync", mload[:, 0:128], m0_h))
        tks.append(P.dma("sync", mload[:, 128:256], m4_h))
        tks.append(P.dma("sync", mload[:, 256:384], idm_h))

        def casts0(V):
            V.tensor_copy(m0_bf[:], mload[:, 0:128])
            V.tensor_copy(m4_bf[:], mload[:, 128:256])
            V.tensor_copy(idm_bf[:], mload[:, 256:384])
            return V.tensor_scalar_mul(m0f_bf[:], mload[:, 0:128], flag_sb[:, 0:1])
        P.op("vector", casts0, waits=tks + [t_con], inc=True)

        def sconsts(S):
            S.activation(sm3[0:1, 0:1], cst[0:1, 2:3], AF.Sqrt)
            return S.activation(sm3[0:1, 16:32], sm2[0:1, 0:16], AF.Abs)
        P.op("scalar", sconsts, waits=[t_con] + tks, inc=True)
        t_sc0 = P.tok("scalar")
        P.op("vector", lambda V: V.tensor_scalar_mul(sm1[0:1, 16:32], sm3[0:1, 16:32],
                                                     sm3[0:1, 0:1]), waits=[t_sc0],
             inc=True)
        t_v1a = P.tok("vector")
        def v1b(V):
            V.tensor_scalar_mul(sm1[0:1, 16:32], sm1[0:1, 16:32], -1.0)
            return V.reciprocal(sm1[0:1, 0:1], sm3[0:1, 0:1])
        P.op("vector", v1b, waits=[t_v1a], inc=True)
        t_v1 = P.tok("vector")
        def repl2(T):
            T.matmul(ps_rep[:, 0:16], ones_r_f[:], sm1[0:1, 16:32], start=True, stop=True)
            T.matmul(ps_rep[:, 16:32], ones_r_f[:], sm2[0:1, 0:16], start=True, stop=True)
            return T.matmul(ps_rep[:, 32:33], ones_r_f[:], sm1[0:1, 0:1],
                            start=True, stop=True)
        P.op("tensor", repl2, waits=[t_v1], inc=True)
        t_mm1 = P.tok("tensor")
        def cop2(V):
            V.tensor_copy(negB[:], ps_rep[:, 0:16])
            V.tensor_copy(grep[:], ps_rep[:, 16:32])
            return V.tensor_copy(sc128[:], ps_rep[:, 32:33])
        P.op("vector", cop2, waits=[t_mm1], inc=True)
        t_grep = P.tok("vector")
        if DEBUG:
            P.dma("gpsimd", dbg["dbg_small"][:, 0:1], sc128[:], slow=True)
            P.dma("gpsimd", dbg["dbg_small"][:, 1:17], negB[:], slow=True)
            P.dma("gpsimd", dbg["dbg_small"][:, 17:33], grep[:], slow=True)
            P.dma("gpsimd", dbg["dbg_small"][:, 33:37], cst[:], slow=True)
            P.dma("gpsimd", dbg["dbg_small"][:, 38:39], flag_sb[:], slow=True)
        P.barrier()

    with ExitStack() as pg:
        eg = pg.enter_context
        y_g = eg(nc.sbuf_tensor("y_g", [128, OWN_G * H * 128], BF16))

        def emit_group(g):
            p12 = ExitStack()
            xnfm_g = p12.enter_context(
                nc.sbuf_tensor(f"xnfm_g{g}", [128, OWN_G * 16 * 128], BF16))
            kfm_g = p12.enter_context(
                nc.sbuf_tensor(f"kfm_g{g}", [128, KVH * KEY_G * 128], BF16))
            vtok_g = p12.enter_context(
                nc.sbuf_tensor(f"vtok_g{g}", [128, KEY_G * 516], BF16))

            with ExitStack() as p1:
                e1 = p1.enter_context
                wk_bf = e1(nc.sbuf_tensor(f"wk_bf_g{g}", [128, 16 * 512], BF16))
                wv_bf = e1(nc.sbuf_tensor(f"wv_bf_g{g}", [128, 16 * 512], BF16))
                xld = [e1(nc.sbuf_tensor(f"xld{i}_g{g}", [128, D], F32)) for i in range(2)]
                xn_bf = e1(nc.sbuf_tensor(f"xn_bf_g{g}", [128, 2 * D], BF16))
                sq_scr = e1(nc.sbuf_tensor(f"sq_scr_g{g}", [128, D], BF16))
                xnfm_t = e1(nc.sbuf_tensor(f"xnfm_t_g{g}", [128, 16 * 128], BF16))
                k_bf = e1(nc.sbuf_tensor(f"k_bf_g{g}", [128, 512], BF16))
                sA = e1(nc.sbuf_tensor(f"sA_g{g}", [128, 64], F32))
                sB = e1(nc.sbuf_tensor(f"sB_g{g}", [128, 64], F32))
                sC = e1(nc.sbuf_tensor(f"sC_g{g}", [128, 64], F32))
                sD = e1(nc.sbuf_tensor(f"sD_g{g}", [128, 64], F32))
                lns = e1(nc.sbuf_tensor(f"lns_g{g}", [128, 32], F32))
                ps_k = e1(nc.psum_tensor(f"ps_k_g{g}", [128, 512], F32))
                ps_v = e1(nc.psum_tensor(f"ps_v_g{g}", [128, 512], F32))
                ps_tp = [e1(nc.psum_tensor(f"ps_tp{i}_g{g}", [128, 128], BF16))
                         for i in range(2)]

                kstg = [e1(nc.sbuf_tensor(f"kstg{i}_g{g}", [128, 512], F32))
                        for i in range(2)]
                t_wkv = []
                kprev = [None] * 4
                for c in range(16):
                    for idx, (srcT, dstB) in enumerate(((wkT, wk_bf), (wvT, wv_bf))):
                        sl = (2 * c + idx) % 2
                        t_ld = P.dma("gpsimd", kstg[sl][:], srcT[128 * c:128 * c + 128, :],
                                     waits=[kprev[sl]] if kprev[sl] else [])
                        if (2 * c + idx) % 2 == 0:
                            t_cv = P.op("scalar", lambda S, sl=sl, c=c, dstB=dstB: S.copy(
                                dstB[:, 512 * c:512 * c + 512], kstg[sl][:]),
                                waits=[t_ld], inc=True)
                        else:
                            t_cv = P.op("vector", lambda V, sl=sl, c=c, dstB=dstB:
                                        V.tensor_copy(dstB[:, 512 * c:512 * c + 512],
                                                      kstg[sl][:]), waits=[t_ld], inc=True)
                        kprev[sl] = t_cv
                        t_wkv.append(t_cv)

                items = []
                if g == 0:
                    wstg = [e1(nc.sbuf_tensor(f"wstg{i}", [128, D], F32)) for i in range(2)]
                    t_scrP = e1(nc.sbuf_tensor("t_scrP", [128, D], F32))
                    tern_bf = [e1(nc.sbuf_tensor(f"tern_bf{i}", [128, D], BF16))
                               for i in range(2)]
                    weff_bf0 = e1(nc.sbuf_tensor("weff_bf0", [128, D], BF16))
                    lnw_sb = e1(nc.sbuf_tensor("lnw_sb", [128, 16], F32))
                    lnb_sb = e1(nc.sbuf_tensor("lnb_sb", [128, 16], F32))
                    lnb_bf = e1(nc.sbuf_tensor("lnb_bf", [128, 16], BF16))
                    asum = e1(nc.sbuf_tensor("asum", [128, 20], F32))
                    ps_c = e1(nc.psum_tensor("ps_c", [128, D], F32))
                    tk_lw = P.dma("sync", lnw_sb[:], lnw_h)
                    tk_lb = P.dma("sync", lnb_sb[:], lnb_h)
                    t_lnbbf = P.op("vector", lambda V: V.tensor_copy(lnb_bf[:], lnb_sb[:]),
                                   waits=[tk_lb], inc=True)
                    pst = {"wi": 0, "ci": 0, "wfree": [None, None],
                           "cfree": [None, None], "absred": [], "t_scale": None,
                           "t_tern_mm": None, "t_weff_st": None,
                           "mm_pend": None, "mm_tok": [None, None]}

                    def abs_item(c):
                        def run():
                            sl = pst["wi"] % 2
                            pst["wi"] += 1
                            t_ld = P.dma("gpsimd", wstg[sl][:], wff[128 * c:128 * c + 128, :],
                                         waits=[pst["wfree"][sl]] if pst["wfree"][sl] else [])
                            t_r = P.op("vector", lambda V, sl=sl, c=c: V.tensor_reduce(
                                asum[:, c:c + 1], wstg[sl][:], AX.X, OP.add,
                                apply_absolute_value=True), waits=[t_ld], inc=True)
                            pst["wfree"][sl] = t_r
                            pst["absred"].append(t_r)
                        return run

                    def scale_item():
                        def run():
                            t1 = P.op("vector", lambda V: V.tensor_reduce(
                                asum[:, 16:17], asum[:, 0:16], AX.X, OP.add),
                                waits=pst["absred"], inc=True)
                            t2 = P.op("tensor", lambda T: T.matmul(
                                ps_c[0:1, 0:1], asum[:, 16:17], ones_c_f[:],
                                start=True, stop=True, skip_group_check=True),
                                waits=[t1], inc=True)
                            t3 = P.op("vector", lambda V: V.reciprocal(
                                asum[0:1, 17:18], ps_c[0:1, 0:1]), waits=[t2], inc=True)
                            t4 = P.op("vector", lambda V: V.tensor_scalar_mul(
                                asum[0:1, 18:19], asum[0:1, 17:18], float(D * D)),
                                waits=[t3], inc=True)
                            t5 = P.op("tensor", lambda T: T.matmul(
                                ps_c[:, 4:5], ones_r_f[:], asum[0:1, 18:19],
                                start=True, stop=True, skip_group_check=True),
                                waits=[t4], inc=True)
                            pst["t_scale"] = P.op("vector", lambda V: V.tensor_copy(
                                asum[:, 19:20], ps_c[:, 4:5]), waits=[t5], inc=True)
                        return run

                    def flush_tern_mm():
                        if pst["mm_pend"] is None:
                            return
                        c, par, t_t = pst["mm_pend"]
                        pst["mm_pend"] = None
                        for s5 in range(4):
                            P.op("tensor", lambda T, c=c, s5=s5, par=par: T.matmul(
                                ps_c[0:1, 512 * s5:512 * s5 + 512], lnb_bf[:, c:c + 1],
                                tern_bf[par][:, 512 * s5:512 * s5 + 512],
                                start=(c == 0), stop=(c == 15), skip_group_check=True),
                                waits=[t_t, t_lnbbf] if s5 == 0 else [], inc=(s5 == 3))
                        pst["t_tern_mm"] = pst["mm_tok"][par] = P.tok("tensor")

                    def tern_item(c):
                        def run():
                            sl = pst["wi"] % 2
                            par = c % 2
                            pst["wi"] += 1
                            t_ld = P.dma("gpsimd", wstg[sl][:], wff[128 * c:128 * c + 128, :],
                                         waits=[pst["wfree"][sl]] if pst["wfree"][sl] else [])
                            aw = [t_ld, pst["t_scale"]]
                            if pst.get("prev_tt"):
                                aw.append(pst["prev_tt"])
                            t_a = P.op("scalar", lambda S, sl=sl: S.activation(
                                t_scrP[:], wstg[sl][:], AF.Identity,
                                scale=asum[:, 19:20]),
                                waits=aw, inc=True)
                            tw = [t_a]
                            if pst["mm_tok"][par]:
                                tw.append(pst["mm_tok"][par])
                            def tern_chunk(V, sl=sl, par=par):
                                V.tensor_single_scalar(wstg[sl][:], t_scrP[:], -0.5, OP.is_lt)
                                return V.scalar_tensor_tensor(tern_bf[par][:], t_scrP[:], 0.5,
                                                              wstg[sl][:],
                                                              OP.is_gt, OP.subtract)
                            t_t = P.op("vector", lambda V, f=tern_chunk: f(V),
                                       waits=tw, inc=True)
                            pst["wfree"][sl] = t_t
                            pst["prev_tt"] = t_t
                            ww = [t_t, tk_lw]
                            if pst["t_weff_st"]:
                                ww.append(pst["t_weff_st"])
                            t_w = P.op("scalar", lambda S, c=c, par=par: S.activation(
                                weff_bf0[:], tern_bf[par][:], AF.Identity,
                                scale=lnw_sb[:, c:c + 1]), waits=ww, inc=True)
                            pst["t_weff_st"] = P.dma("gpsimd",
                                                     s_weff[128 * c:128 * c + 128, :],
                                                     weff_bf0[:], waits=[t_w])
                            flush_tern_mm()
                            pst["mm_pend"] = (c, par, t_t)
                        return run

                    def cfin_item():
                        def run():
                            flush_tern_mm()
                            P.op("vector", lambda V: V.tensor_copy(c_bf[0:1, :], ps_c[0:1, :]),
                                 waits=[pst["t_tern_mm"], pst["t_weff_st"]], inc=True)
                        return run

                    for c in range(16):
                        items.append(abs_item(c))
                    items.append(scale_item())
                    for c in range(16):
                        items.append(tern_item(c))
                    items.append(cfin_item())

                t_xfree = [None, None]
                t_xld = [None, None]
                t_tp_ev = [None, None]
                t_xtp_last = [None, None]
                t_kepi = None
                t_vevac = None
                P.op("vector", lambda V: V.memset(vtok_g[:], 1.0), inc=True)
                t_xn_arr = [None, None]

                def ln_emit(s):
                    kt = 8 * g + s
                    b = s % 2
                    lb = 16 * b
                    xb = D * b
                    t_xld[b] = P.dma("sync", xld[b][:], xin[128 * kt:128 * kt + 128, :],
                                     waits=[t_xfree[b]] if t_xfree[b] else [])
                    t_s1 = P.op("vector", lambda V, b=b, lb=lb: V.tensor_reduce(
                        lns[:, lb + 0:lb + 1], xld[b][:], AX.X, OP.add), waits=[t_xld[b]],
                        inc=True)
                    t_s2 = P.op("scalar", lambda S, b=b, lb=lb: S.activation(
                        sq_scr[:], xld[b][:], AF.Square, accum_out=lns[:, lb + 1:lb + 2]),
                        waits=[t_xld[b]], inc=True)
                    def ln_a(V, lb=lb):
                        V.tensor_scalar_mul(lns[:, lb + 2:lb + 3], lns[:, lb + 0:lb + 1], 1.0 / D)
                        return V.scalar_tensor_tensor(lns[:, lb + 3:lb + 4], lns[:, lb + 0:lb + 1],
                                                      1.0 / (D * D),
                                                      lns[:, lb + 0:lb + 1], OP.mult, OP.mult)
                    t_lna = P.op("vector", ln_a, waits=[t_s1], inc=True)
                    t_var = P.op("vector", lambda V, lb=lb: V.scalar_tensor_tensor(
                        lns[:, lb + 4:lb + 5], lns[:, lb + 1:lb + 2], 1.0 / D, lns[:, lb + 3:lb + 4],
                        OP.mult, OP.subtract), waits=[t_lna, t_s2], inc=True)
                    t_std = P.op("scalar", lambda S, lb=lb: S.activation(
                        lns[:, lb + 5:lb + 6], lns[:, lb + 4:lb + 5], AF.Sqrt, bias=cst[:, 0:1]),
                        waits=[t_var], inc=True)
                    t_rstd = P.op("vector", lambda V, lb=lb: V.reciprocal(
                        lns[:, lb + 6:lb + 7], lns[:, lb + 5:lb + 6]), waits=[t_std], inc=True)
                    t_nb = P.op("vector", lambda V, lb=lb: V.scalar_tensor_tensor(
                        lns[:, lb + 7:lb + 8], lns[:, lb + 2:lb + 3], -1.0, lns[:, lb + 6:lb + 7],
                        OP.mult, OP.mult), waits=[t_rstd], inc=True)
                    t_xn = P.op("scalar", lambda S, b=b, lb=lb, xb=xb: S.activation(
                        xn_bf[:, xb:xb + D], xld[b][:], AF.Identity, bias=lns[:, lb + 7:lb + 8],
                        scale=lns[:, lb + 6:lb + 7]),
                        waits=[t_nb] + ([t_xtp_last[b]] if t_xtp_last[b] else []), inc=True)
                    t_xfree[b] = t_xn
                    t_xn_arr[b] = t_xn

                ln_emit(0)
                for s in range(KEY_G):
                    kt = 8 * g + s
                    b = s % 2
                    lb = 16 * b
                    xb = D * b
                    own = kt >= 4 and (kt - 4) // 8 == g
                    ot_local = (kt - 4) - 8 * g if own else None
                    if s + 1 < KEY_G:
                        ln_emit(s + 1)
                    t_xn = t_xn_arr[b]
                    dst = xnfm_g if own else xnfm_t
                    base = (ot_local * 16 * 128) if own else 0
                    for c in range(16):
                        i = c % 2
                        w = [t_xn]
                        if t_tp_ev[i]:
                            w.append(t_tp_ev[i])
                        tk = P.op("tensor", lambda T, c=c, i=i, xb=xb: T.transpose(
                            ps_tp[i][:], xn_bf[:, xb + 128 * c:xb + 128 * c + 128], idm_bf[:]),
                            waits=w, inc=True)
                        if c == 15:
                            t_xtp_last[b] = tk
                        if i == 0:
                            t_tp_ev[i] = P.op("scalar", lambda S, c=c, i=i, dst=dst, base=base:
                                              S.copy(dst[:, base + 128 * c:base + 128 * c + 128],
                                                     ps_tp[i][:]), waits=[tk], inc=True)
                        else:
                            t_tp_ev[i] = P.op("vector", lambda V, c=c, i=i, dst=dst, base=base:
                                              V.tensor_copy(dst[:, base + 128 * c:base + 128 * c + 128],
                                                            ps_tp[i][:]), waits=[tk], inc=True)
                    kv_waits = [t_tp_ev[0], t_tp_ev[1]] + t_wkv
                    if t_kepi:
                        kv_waits.append(t_kepi)
                    for c in range(16):
                        P.op("tensor", lambda T, c=c, dst=dst, base=base: T.matmul(
                            ps_k[:], dst[:, base + 128 * c:base + 128 * c + 128],
                            wk_bf[:, 512 * c:512 * c + 512],
                            start=(c == 0), stop=(c == 15), skip_group_check=True),
                            waits=kv_waits if c == 0 else [], inc=(c == 15))
                    t_kmm = P.tok("tensor")
                    for c in range(16):
                        P.op("tensor", lambda T, c=c, dst=dst, base=base: T.matmul(
                            ps_v[:], dst[:, base + 128 * c:base + 128 * c + 128],
                            wv_bf[:, 512 * c:512 * c + 512],
                            start=(c == 0), stop=(c == 15), skip_group_check=True),
                            waits=[t_vevac] if (c == 0 and t_vevac) else [],
                            inc=(c == 15))
                    t_vmm = P.tok("tensor")
                    def v_evac(V, s=s):
                        for hk in range(KVH - 1):
                            V.tensor_copy(vtok_g[:, 516 * s + 129 * hk:516 * s + 129 * hk + 128],
                                          ps_v[:, 128 * hk:128 * hk + 128])
                        return V.tensor_copy(
                            vtok_g[:, 516 * s + 129 * 3:516 * s + 129 * 3 + 128],
                            ps_v[:, 384:512])
                    t_vevac = P.op("vector", v_evac, waits=[t_vmm], inc=True)
                    def k_sq(S, lb=lb):
                        r = None
                        for hk in range(KVH):
                            r = S.activation(sq_scr[:, 0:128],
                                             ps_k[:, 128 * hk:128 * hk + 128],
                                             AF.Square, accum_out=lns[:, lb + 8 + hk:lb + 9 + hk])
                        return r
                    t_ksq = P.op("scalar", k_sq, waits=[t_kmm], inc=True)
                    t_kstd = P.op("scalar", lambda S, lb=lb: S.activation(
                        lns[:, lb + 8:lb + 12], lns[:, lb + 8:lb + 12], AF.Sqrt, bias=cst[:, 1:2],
                        scale=1.0 / HD), waits=[t_ksq], inc=True)
                    t_krs = P.op("vector", lambda V, lb=lb: V.reciprocal(
                        lns[:, lb + 8:lb + 12], lns[:, lb + 8:lb + 12]), waits=[t_kstd], inc=True)
                    t_krs2 = P.op("scalar", lambda S, lb=lb: S.copy(
                        lns[:, lb + 12:lb + 16], lns[:, lb + 8:lb + 12]),
                        waits=[t_krs], inc=True)
                    ctile = slice(64 * kt, 64 * kt + 64)
                    for hk in range(KVH):
                        s1 = slice(128 * hk, 128 * hk + 64)
                        s2 = slice(128 * hk + 64, 128 * hk + 128)
                        def k_rope_m(V, hk=hk, s1=s1, s2=s2, ctile=ctile, lb=lb):
                            rs = lns[:, lb + 12 + hk:lb + 13 + hk]
                            V.scalar_tensor_tensor(sA[:], ps_k[:, s1], rs, cosT[:, ctile],
                                                   OP.mult, OP.mult)
                            V.scalar_tensor_tensor(sB[:], ps_k[:, s2], rs, sinT[:, ctile],
                                                   OP.mult, OP.mult)
                            V.scalar_tensor_tensor(sC[:], ps_k[:, s1], rs, sinT[:, ctile],
                                                   OP.mult, OP.mult)
                            return V.scalar_tensor_tensor(sD[:], ps_k[:, s2], rs,
                                                          cosT[:, ctile], OP.mult, OP.mult)
                        t_km = P.op("vector", lambda V, f=k_rope_m: f(V),
                                    waits=[t_krs2] if hk == 0 else [], inc=True)
                        def k_rope_a(V, s1=s1, s2=s2):
                            V.tensor_add(k_bf[:, s1], sA[:], sB[:])
                            return V.tensor_sub(k_bf[:, s2], sD[:], sC[:])
                        P.op("vector", lambda V, f=k_rope_a: f(V), waits=[t_km],
                             inc=(hk == KVH - 1))
                    t_kepi = P.tok("vector")
                    if DEBUG and g == 0 and kt == 4:
                        P.dma("gpsimd", dbg["dbg_xn"][:], xn_bf[:], waits=[t_xn], slow=True)
                        P.dma("gpsimd", dbg["dbg_kbf"][:], k_bf[:], waits=[t_kepi], slow=True)
                        P.dma("gpsimd", dbg["dbg_vtok"][:], vtok_g[:, 512 * s:512 * s + 512],
                              waits=[t_vevac], slow=True)
                        P.dma("gpsimd", dbg["dbg_xnfm"][:], xnfm_g[:, 0:128],
                              waits=[t_tp_ev[0], t_tp_ev[1]], slow=True)
                        P.dma("gpsimd", dbg["dbg_lns"][:], lns[:], waits=[t_kepi], slow=True)
                    for hk in range(KVH):
                        i = hk % 2
                        tk = P.op("tensor", lambda T, hk=hk, i=i: T.transpose(
                            ps_tp[i][:], k_bf[:, 128 * hk:128 * hk + 128], idm_bf[:]),
                            waits=[t_kepi, t_tp_ev[i]], inc=True)
                        if i == 0:
                            t_tp_ev[i] = P.op("scalar", lambda S, hk=hk, i=i, s=s: S.copy(
                                kfm_g[:, (hk * KEY_G + s) * 128:(hk * KEY_G + s) * 128 + 128],
                                ps_tp[i][:]), waits=[tk], inc=True)
                        else:
                            t_tp_ev[i] = P.op("vector", lambda V, hk=hk, i=i, s=s: V.tensor_copy(
                                kfm_g[:, (hk * KEY_G + s) * 128:(hk * KEY_G + s) * 128 + 128],
                                ps_tp[i][:]), waits=[tk], inc=True)
                    lo = (len(items) * s) // KEY_G
                    hi = (len(items) * (s + 1)) // KEY_G
                    for it in items[lo:hi]:
                        it()
                P.barrier()

            with ExitStack() as p2:
                e2 = p2.enter_context
                wq_bf = e2(nc.sbuf_tensor(f"wq_bf_g{g}", [128, 16 * D], BF16))
                q_bf = e2(nc.sbuf_tensor(f"q_bf_g{g}", [128, D], BF16))
                qfm = e2(nc.sbuf_tensor(f"qfm_g{g}", [128, 16 * 128], BF16))
                pT = [e2(nc.sbuf_tensor(f"pT{i}_g{g}", [128, 640], BF16)) for i in range(2)]
                qs = e2(nc.sbuf_tensor(f"qs_g{g}", [128, 48], F32))
                sA2 = e2(nc.sbuf_tensor(f"sA2_g{g}", [128, 64], F32))
                sB2 = e2(nc.sbuf_tensor(f"sB2_g{g}", [128, 64], F32))
                sC2 = e2(nc.sbuf_tensor(f"sC2_g{g}", [128, 64], F32))
                sD2 = e2(nc.sbuf_tensor(f"sD2_g{g}", [128, 64], F32))
                sq2 = e2(nc.sbuf_tensor(f"sq2_g{g}", [128, 128], BF16))
                rsum = e2(nc.sbuf_tensor(f"rsum_g{g}", [128, 2], F32))
                ps_q = [e2(nc.psum_tensor(f"ps_q{i}_g{g}", [128, 512], F32))
                        for i in range(2)]
                ps_tp2 = [e2(nc.psum_tensor(f"ps_tp2{i}_g{g}", [128, 128], BF16))
                          for i in range(2)]
                ps_att = [e2(nc.psum_tensor(f"ps_att{i}_g{g}", [128, 769], F32))
                          for i in range(2)]

                t_wq = []
                t_wqc = {}
                items2 = []
                if g == 0:
                    wstg2 = [e2(nc.sbuf_tensor(f"wstg2{i}", [128, D], F32))
                             for i in range(3)]
                    cws2 = [e2(nc.sbuf_tensor(f"cws2{i}", [128, D], BF16))
                            for i in range(2)]
                    pst2 = {"wi": 0, "ci": 0, "wfree": [None] * 3,
                            "cfree": [None, None]}

                    def wqcast_item(c):
                        def run():
                            sl = pst2["wi"] % 3
                            pst2["wi"] += 1
                            t_ld = P.dma("sync", wstg2[sl][:], wqT[128 * c:128 * c + 128, :],
                                         waits=[pst2["wfree"][sl]] if pst2["wfree"][sl] else [])
                            if c % 2 == 0:
                                t_cv = P.op("scalar", lambda S, sl=sl, c=c: S.copy(
                                    wq_bf[:, D * c:D * c + D], wstg2[sl][:]),
                                    waits=[t_ld], inc=True)
                            else:
                                t_cv = P.op("vector", lambda V, sl=sl, c=c: V.tensor_copy(
                                    wq_bf[:, D * c:D * c + D], wstg2[sl][:]),
                                    waits=[t_ld], inc=True)
                            pst2["wfree"][sl] = t_cv
                            t_wqc[c] = t_cv
                            P.dma("sync", s_wq[128 * c:128 * c + 128, :],
                                  wq_bf[:, D * c:D * c + D], waits=[t_cv])
                        return run

                    def wocast_item(c):
                        def run():
                            sl = pst2["wi"] % 3
                            ob = pst2["ci"] % 2
                            pst2["wi"] += 1
                            pst2["ci"] += 1
                            t_ld = P.dma("gpsimd", wstg2[sl][:], woT[128 * c:128 * c + 128, :],
                                         waits=[pst2["wfree"][sl]] if pst2["wfree"][sl] else [])
                            cw = [t_ld]
                            if pst2["cfree"][ob]:
                                cw.append(pst2["cfree"][ob])
                            if c % 2 == 0:
                                t_cv = P.op("scalar", lambda S, sl=sl, ob=ob: S.copy(
                                    cws2[ob][:], wstg2[sl][:]), waits=cw, inc=True)
                            else:
                                t_cv = P.op("vector", lambda V, sl=sl, ob=ob: V.tensor_copy(
                                    cws2[ob][:], wstg2[sl][:]), waits=cw, inc=True)
                            pst2["wfree"][sl] = t_cv
                            pst2["cfree"][ob] = P.dma("gpsimd",
                                                      s_wo[128 * c:128 * c + 128, :],
                                                      cws2[ob][:], waits=[t_cv])
                        return run

                    items2 = [wocast_item(c) for c in range(16)]
                    for c in range(16):
                        wqcast_item(c)()
                else:
                    for c in range(16):
                        t_wq.append(P.dma("sync", wq_bf[:, D * c:D * c + D],
                                          s_wq[128 * c:128 * c + 128, :]))

                t_q_ev = [None, None]
                t_tp_ev2 = [None, None]
                t_exp = {}
                t_av = {}
                t_yev = {}
                for qt in range(OWN_G):
                    ot = 8 * g + qt
                    kt0 = qt
                    base = qt * 16 * 128
                    ctile = slice(64 * (ot + 4), 64 * (ot + 4) + 64)
                    if items2:
                        lo = (len(items2) * qt) // OWN_G
                        hi = (len(items2) * (qt + 1)) // OWN_G
                        for it in items2[lo:hi]:
                            it()
                    for s4 in range(4):
                        bq = s4 % 2
                        w0 = list(t_wq) if (qt == 0 and s4 == 0) else []
                        if t_q_ev[bq]:
                            w0.append(t_q_ev[bq])
                        for c in range(16):
                            cw = list(w0) if c == 0 else []
                            if g == 0 and qt == 0:
                                cw.append(t_wqc[c])
                            P.op("tensor", lambda T, c=c, s4=s4, bq=bq, base=base: T.matmul(
                                ps_q[bq][:], xnfm_g[:, base + 128 * c:base + 128 * c + 128],
                                wq_bf[:, D * c + 512 * s4:D * c + 512 * s4 + 512],
                                start=(c == 0), stop=(c == 15), skip_group_check=True),
                                waits=cw, inc=(c == 15))
                        t_qmm = P.tok("tensor")
                        def q_sq(S, s4=s4, bq=bq):
                            r = None
                            for j in range(4):
                                r = S.activation(sq2[:], ps_q[bq][:, 128 * j:128 * j + 128],
                                                 AF.Square,
                                                 accum_out=qs[:, 4 * s4 + j:4 * s4 + j + 1])
                            return r
                        t_qsq = P.op("scalar", q_sq, waits=[t_qmm], inc=True)
                        t_qstd = P.op("scalar", lambda S, s4=s4: S.activation(
                            qs[:, 16 + 4 * s4:20 + 4 * s4], qs[:, 4 * s4:4 * s4 + 4],
                            AF.Sqrt, bias=cst[:, 1:2], scale=1.0 / HD),
                            waits=[t_qsq], inc=True)
                        t_qrs = P.op("vector", lambda V, s4=s4: V.reciprocal(
                            qs[:, 16 + 4 * s4:20 + 4 * s4], qs[:, 16 + 4 * s4:20 + 4 * s4]),
                            waits=[t_qstd], inc=True)
                        t_qrg0 = P.op("vector", lambda V, s4=s4: V.tensor_mul(
                            qs[:, 16 + 4 * s4:20 + 4 * s4], qs[:, 16 + 4 * s4:20 + 4 * s4],
                            grep[:, 4 * s4:4 * s4 + 4]), waits=[t_qrs], inc=True)
                        t_qrg = P.op("scalar", lambda S, s4=s4: S.copy(
                            qs[:, 32 + 4 * s4:36 + 4 * s4], qs[:, 16 + 4 * s4:20 + 4 * s4]),
                            waits=[t_qrg0], inc=True)
                        for j in range(4):
                            hq = 4 * s4 + j
                            s1 = slice(128 * j, 128 * j + 64)
                            s2 = slice(128 * j + 64, 128 * j + 128)
                            o1 = slice(128 * hq, 128 * hq + 64)
                            o2 = slice(128 * hq + 64, 128 * hq + 128)
                            def q_rope_m(V, bq=bq, hq=hq, s1=s1, s2=s2, ctile=ctile):
                                rs = qs[:, 32 + hq:33 + hq]
                                V.scalar_tensor_tensor(sA2[:], ps_q[bq][:, s1], rs,
                                                       cosT[:, ctile], OP.mult, OP.mult)
                                V.scalar_tensor_tensor(sB2[:], ps_q[bq][:, s2], rs,
                                                       sinT[:, ctile], OP.mult, OP.mult)
                                V.scalar_tensor_tensor(sC2[:], ps_q[bq][:, s1], rs,
                                                       sinT[:, ctile], OP.mult, OP.mult)
                                return V.scalar_tensor_tensor(sD2[:], ps_q[bq][:, s2], rs,
                                                              cosT[:, ctile],
                                                              OP.mult, OP.mult)
                            t_qm = P.op("vector", lambda V, f=q_rope_m: f(V),
                                        waits=[t_qrg] if j == 0 else [], inc=True)
                            def q_rope_a(V, o1=o1, o2=o2):
                                V.tensor_add(q_bf[:, o1], sA2[:], sB2[:])
                                return V.tensor_sub(q_bf[:, o2], sD2[:], sC2[:])
                            P.op("vector", lambda V, f=q_rope_a: f(V), waits=[t_qm],
                                 inc=(j == 3))
                        t_q_ev[bq] = P.tok("vector")
                    t_qrope = P.tok("vector")
                    for c in range(16):
                        i = c % 2
                        w = [t_qrope]
                        if t_tp_ev2[i]:
                            w.append(t_tp_ev2[i])
                        tk = P.op("tensor", lambda T, c=c, i=i: T.transpose(
                            ps_tp2[i][:], q_bf[:, 128 * c:128 * c + 128], idm_bf[:]),
                            waits=w, inc=True)
                        if i == 0:
                            t_tp_ev2[i] = P.op("scalar", lambda S, c=c, i=i: S.copy(
                                qfm[:, 128 * c:128 * c + 128], ps_tp2[i][:]),
                                waits=[tk], inc=True)
                        else:
                            t_tp_ev2[i] = P.op("vector", lambda V, c=c, i=i: V.tensor_copy(
                                qfm[:, 128 * c:128 * c + 128], ps_tp2[i][:]),
                                waits=[tk], inc=True)
                    t_qfm = [t_tp_ev2[0], t_tp_ev2[1]]
                    if DEBUG and g == 0 and qt == 4:
                        P.dma("gpsimd", dbg["dbg_qbf"][:], q_bf[:], waits=[t_qrope], slow=True)
                        P.dma("gpsimd", dbg["dbg_qfm"][:], qfm[:], waits=list(t_qfm), slow=True)

                    t_mask_d = {}

                    def head_front(h):
                        ba = h % 2
                        hk = h // KVH
                        sw = list(t_qfm)
                        if h - 2 in t_exp:
                            sw.append(t_exp[h - 2])
                        for j in range(5):
                            kslot = kt0 + j
                            P.op("tensor", lambda T, j=j, ba=ba, hk=hk, kslot=kslot, h=h:
                                 T.matmul(ps_att[ba][:, 128 * j:128 * j + 128],
                                          kfm_g[:, (hk * KEY_G + kslot) * 128:
                                                (hk * KEY_G + kslot) * 128 + 128],
                                          qfm[:, 128 * h:128 * h + 128],
                                          start=True, stop=True, skip_group_check=True),
                                 waits=sw if j == 0 else [], inc=(j == 4))
                        t_sc = P.tok("tensor")
                        ew = [t_sc]
                        if h - 2 in t_av:
                            ew.append(t_av[h - 2])
                        P.op("scalar", lambda S, ba=ba, h=h: S.activation(
                            pT[ba][:, 0:640], ps_att[ba][:, 0:640], AF.Exp,
                            bias=negB[:, h:h + 1], scale=sc128[:, 0:1]),
                            waits=ew, inc=True)
                        t_exp[h] = P.tok("scalar")
                        def masks(V, ba=ba, qt=qt, g=g):
                            m0use = m0f_bf if (8 * g + qt) <= 3 else m0_bf
                            V.tensor_mul(pT[ba][:, 0:128], pT[ba][:, 0:128], m0use[:])
                            r = V.tensor_mul(pT[ba][:, 512:640], pT[ba][:, 512:640],
                                             m4_bf[:])
                            for j in range(1, 5):
                                if 8 * g + qt + j <= 3:
                                    r = V.tensor_scalar_mul(
                                        pT[ba][:, 128 * j:128 * j + 128],
                                        pT[ba][:, 128 * j:128 * j + 128],
                                        flag_sb[:, 0:1])
                            return r
                        P.op("vector", masks, waits=[t_exp[h]], inc=True)
                        t_mask_d[h] = P.tok("vector")

                    def head_back(h):
                        ba = h % 2
                        hk = h // KVH
                        aw = [t_mask_d[h]]
                        if h - 2 in t_yev:
                            aw.append(t_yev[h - 2])
                        for j in range(5):
                            kslot = kt0 + j
                            P.op("tensor", lambda T, j=j, ba=ba, hk=hk, kslot=kslot:
                                 T.matmul(ps_att[ba][:, 640:769],
                                          pT[ba][:, 128 * j:128 * j + 128],
                                          vtok_g[:, 516 * kslot + 129 * hk:
                                                 516 * kslot + 129 * hk + 129],
                                          start=(j == 0), stop=(j == 4),
                                          skip_group_check=True),
                                 waits=aw if j == 0 else [], inc=(j == 4))
                        t_av[h] = P.tok("tensor")
                        t_rsum = P.op("vector", lambda V, ba=ba, h=h: V.reciprocal(
                            rsum[:, h % 2:h % 2 + 1], ps_att[ba][:, 768:769]),
                            waits=[t_av[h]], inc=True)
                        P.op("scalar", lambda S, ba=ba, qt=qt, h=h: S.activation(
                            y_g[:, (qt * H + h) * 128:(qt * H + h) * 128 + 128],
                            ps_att[ba][:, 640:768], AF.Identity,
                            scale=rsum[:, h % 2:h % 2 + 1]), waits=[t_rsum], inc=True)
                        t_yev[h] = P.tok("scalar")

                    for h in range(H):
                        head_front(h)
                        if h > 0:
                            head_back(h - 1)
                    head_back(H - 1)
                    t_av = {h - H: t for h, t in t_av.items() if h >= H - 2}
                    t_yev = {h - H: t for h, t in t_yev.items() if h >= H - 2}
                    t_exp = {}
                P.barrier()

            p12.close()
            with ExitStack() as p3:
                e3 = p3.enter_context
                wo_bf = e3(nc.sbuf_tensor(f"wo_bf_g{g}", [128, 16 * D], BF16))
                weff_sb = e3(nc.sbuf_tensor(f"weff_sb_g{g}", [128, 16 * D], BF16))
                xld3 = [e3(nc.sbuf_tensor(f"xld3{i}_g{g}", [128, D], F32)) for i in range(2)]
                xn2a = e3(nc.sbuf_tensor(f"xn2a_g{g}", [128, D], BF16))
                sq_scr3 = xn2a
                xnfm3 = e3(nc.sbuf_tensor(f"xnfm3_g{g}", [128, 16 * 128], BF16))
                yfm3 = e3(nc.sbuf_tensor(f"yfm3_g{g}", [128, 16 * 128], BF16))
                obuf = [e3(nc.sbuf_tensor(f"obuf{i}_g{g}", [128, 512], F32)) for i in range(2)]
                lns3 = e3(nc.sbuf_tensor(f"lns3_g{g}", [128, 16], F32))
                ps_o = [e3(nc.psum_tensor(f"ps_o{i}_g{g}", [128, 512], F32)) for i in range(2)]
                ps_tp3 = [e3(nc.psum_tensor(f"ps_tp3{i}_g{g}", [128, 128], BF16))
                          for i in range(2)]

                t_wo = []
                for c in range(16):
                    t_wo.append(P.dma("gpsimd", wo_bf[:, D * c:D * c + D],
                                      s_wo[128 * c:128 * c + 128, :]))
                    t_wo.append(P.dma("gpsimd", weff_sb[:, D * c:D * c + D],
                                      s_weff[128 * c:128 * c + 128, :]))

                t_xfree3 = [None, None]
                t_xld3 = [None, None]
                t_tp_ev3 = [None, None]
                t_o_ev = [None, None]
                t_ob_st = [None, None]
                for tt in range(OWN_G):
                    ot = 8 * g + tt
                    kt = ot + 4
                    b = tt % 2
                    t_xld3[b] = P.dma("sync", xld3[b][:], xin[128 * kt:128 * kt + 128, :],
                                      waits=[t_xfree3[b]] if t_xfree3[b] else [])
                    t_s1 = P.op("vector", lambda V, b=b: V.tensor_reduce(
                        lns3[:, 0:1], xld3[b][:], AX.X, OP.add), waits=[t_xld3[b]],
                        inc=True)
                    t_s2 = P.op("scalar", lambda S, b=b: S.activation(
                        sq_scr3[:], xld3[b][:], AF.Square, accum_out=lns3[:, 1:2]),
                        waits=[t_xld3[b]], inc=True)
                    def ln_a3(V):
                        V.tensor_scalar_mul(lns3[:, 2:3], lns3[:, 0:1], 1.0 / D)
                        return V.scalar_tensor_tensor(lns3[:, 3:4], lns3[:, 0:1],
                                                      1.0 / (D * D),
                                                      lns3[:, 0:1], OP.mult, OP.mult)
                    t_lna = P.op("vector", ln_a3, waits=[t_s1], inc=True)
                    t_var = P.op("vector", lambda V: V.scalar_tensor_tensor(
                        lns3[:, 4:5], lns3[:, 1:2], 1.0 / D, lns3[:, 3:4],
                        OP.mult, OP.subtract), waits=[t_lna, t_s2], inc=True)
                    t_std = P.op("scalar", lambda S: S.activation(
                        lns3[:, 5:6], lns3[:, 4:5], AF.Sqrt, bias=cst[:, 0:1]),
                        waits=[t_var], inc=True)
                    t_rstd = P.op("vector", lambda V: V.reciprocal(
                        lns3[:, 6:7], lns3[:, 5:6]), waits=[t_std], inc=True)
                    t_r2 = P.op("vector", lambda V: V.tensor_mul(
                        lns3[:, 7:8], lns3[:, 6:7], lns3[:, 6:7]), waits=[t_rstd],
                        inc=True)
                    t_v2 = P.op("vector", lambda V: V.tensor_mul(
                        lns3[:, 8:9], lns3[:, 7:8], lns3[:, 4:5]), waits=[t_r2],
                        inc=True)
                    t_std2 = P.op("scalar", lambda S: S.activation(
                        lns3[:, 9:10], lns3[:, 8:9], AF.Sqrt, bias=cst[:, 0:1]),
                        waits=[t_v2], inc=True)
                    t_rstd2 = P.op("vector", lambda V: V.reciprocal(
                        lns3[:, 10:11], lns3[:, 9:10]), waits=[t_std2], inc=True)
                    t_sc1 = P.op("vector", lambda V: V.tensor_mul(
                        lns3[:, 11:12], lns3[:, 6:7], lns3[:, 10:11]), waits=[t_rstd2],
                        inc=True)
                    t_ab = P.op("vector", lambda V: V.scalar_tensor_tensor(
                        lns3[:, 12:13], lns3[:, 2:3], -1.0, lns3[:, 11:12],
                        OP.mult, OP.mult), waits=[t_sc1], inc=True)
                    t_xn2a = P.op("scalar", lambda S, b=b: S.activation(
                        xn2a[:], xld3[b][:], AF.Identity, bias=lns3[:, 12:13],
                        scale=lns3[:, 11:12]), waits=[t_ab], inc=True)
                    if DEBUG and g == 0 and tt == 0:
                        P.dma("gpsimd", dbg["dbg_xn2a"][:], xn2a[:], waits=[t_xn2a], slow=True)
                    for c in range(16):
                        i = c % 2
                        w = [t_xn2a]
                        if t_tp_ev3[i]:
                            w.append(t_tp_ev3[i])
                        tk = P.op("tensor", lambda T, c=c, i=i: T.transpose(
                            ps_tp3[i][:], xn2a[:, 128 * c:128 * c + 128], idm_bf[:]),
                            waits=w, inc=True)
                        if i == 0:
                            t_tp_ev3[i] = P.op("scalar", lambda S, c=c, i=i: S.copy(
                                xnfm3[:, 128 * c:128 * c + 128], ps_tp3[i][:]),
                                waits=[tk], inc=True)
                        else:
                            t_tp_ev3[i] = P.op("vector", lambda V, c=c, i=i: V.tensor_copy(
                                xnfm3[:, 128 * c:128 * c + 128], ps_tp3[i][:]),
                                waits=[tk], inc=True)
                    for hh in range(16):
                        i = hh % 2
                        tk = P.op("tensor", lambda T, hh=hh, i=i, tt=tt: T.transpose(
                            ps_tp3[i][:], y_g[:, (tt * H + hh) * 128:(tt * H + hh) * 128 + 128],
                            idm_bf[:]), waits=[t_tp_ev3[i]], inc=True)
                        if i == 0:
                            t_tp_ev3[i] = P.op("scalar", lambda S, hh=hh, i=i: S.copy(
                                yfm3[:, 128 * hh:128 * hh + 128], ps_tp3[i][:]),
                                waits=[tk], inc=True)
                        else:
                            t_tp_ev3[i] = P.op("vector", lambda V, hh=hh, i=i: V.tensor_copy(
                                yfm3[:, 128 * hh:128 * hh + 128], ps_tp3[i][:]),
                                waits=[tk], inc=True)
                    t_fm_ready = [t_tp_ev3[0], t_tp_ev3[1]]
                    for s4 in range(4):
                        bo = s4 % 2
                        w0 = list(t_fm_ready) + (t_wo if (tt == 0 and s4 == 0) else [])
                        if t_o_ev[bo]:
                            w0.append(t_o_ev[bo])
                        for c in range(16):
                            P.op("tensor", lambda T, c=c, s4=s4, bo=bo: T.matmul(
                                ps_o[bo][:], yfm3[:, 128 * c:128 * c + 128],
                                wo_bf[:, D * c + 512 * s4:D * c + 512 * s4 + 512],
                                start=(c == 0), stop=False, skip_group_check=True),
                                waits=w0 if c == 0 else [])
                        for c in range(16):
                            P.op("tensor", lambda T, c=c, s4=s4, bo=bo: T.matmul(
                                ps_o[bo][:], xnfm3[:, 128 * c:128 * c + 128],
                                weff_sb[:, D * c + 512 * s4:D * c + 512 * s4 + 512],
                                start=False, stop=False, skip_group_check=True))
                        P.op("tensor", lambda T, s4=s4, bo=bo: T.matmul(
                            ps_o[bo][:], ones_r_bf[:], c_bf[0:1, 512 * s4:512 * s4 + 512],
                            start=False, stop=True, skip_group_check=True), inc=True)
                        t_omm = P.tok("tensor")
                        ww = [t_omm]
                        if t_ob_st[bo]:
                            ww.append(t_ob_st[bo])
                        t_o_ev[bo] = P.op("vector", lambda V, s4=s4, bo=bo, b=b: V.tensor_add(
                            obuf[bo][:], ps_o[bo][:], xld3[b][:, 512 * s4:512 * s4 + 512]),
                            waits=ww, inc=True)
                        t_ob_st[bo] = P.dma("sync",
                                            yout[128 * ot:128 * ot + 128,
                                                 512 * s4:512 * s4 + 512],
                                            obuf[bo][:], waits=[t_o_ev[bo]])
                        if s4 == 3:
                            t_xfree3[b] = t_o_ev[bo]
                P.barrier()

        for g in range(NG):
            emit_group(g)

    with nc.Block() as block:
        P.emit(block)
    stack.close()
    return nc


_prog_cache = {}


def kernel(L0_attn_x, L0_attn_Wq, L0_attn_Wk, L0_attn_Wv, L0_attn_Wo,
           L0_attn_q_gain, L0_ffn_W, L0_ffn_ln_w, L0_ffn_ln_b):
    x = np.ascontiguousarray(np.asarray(L0_attn_x, np.float32))
    B, S, _ = x.shape
    wqT = np.ascontiguousarray(np.asarray(L0_attn_Wq, np.float32).T)
    wkT = np.ascontiguousarray(np.asarray(L0_attn_Wk, np.float32).T)
    wvT = np.ascontiguousarray(np.asarray(L0_attn_Wv, np.float32).T)
    woT = np.ascontiguousarray(np.asarray(L0_attn_Wo, np.float32).T)
    wff = np.ascontiguousarray(np.asarray(L0_ffn_W, np.float32))
    lnw = np.ascontiguousarray(np.asarray(L0_ffn_ln_w, np.float32).reshape(16, 128).T)
    lnb = np.ascontiguousarray(np.asarray(L0_ffn_ln_b, np.float32).reshape(16, 128).T)
    qgain = np.asarray(L0_attn_q_gain, np.float32)
    iq = np.arange(128)[None, :]
    ik = np.arange(128)[:, None]
    m0 = (ik >= iq + 1).astype(np.float32)
    m4 = (ik <= iq).astype(np.float32)
    idm = np.eye(128, dtype=np.float32)

    if "tabs" not in _prog_cache:
        import jax
        import jax.numpy as jnp
        with jax.default_device(jax.devices("cpu")[0]):
            half = HD // 2
            inv_freq = 10000.0 ** (-jnp.arange(0, half, dtype=jnp.float32) / half)
            freqs = jnp.arange(S, dtype=jnp.float32)[:, None] * inv_freq[None, :]
            _prog_cache["tabs"] = (np.asarray(jnp.cos(freqs)), np.asarray(jnp.sin(freqs)))
    cos_full, sin_full = _prog_cache["tabs"]

    if "nc" not in _prog_cache:
        _prog_cache["nc"] = build_program()
    nc = _prog_cache["nc"]

    in_maps = []
    for c in range(8):
        b, h = c // 2, c % 2
        own = x[b, 2048 * h:2048 * h + 2048]
        halo = x[b, 2048 * h - 512:2048 * h] if h == 1 else np.zeros((512, 2048), np.float32)
        xin = np.concatenate([halo, own], axis=0)
        rows = np.clip(np.arange(2048 * h - 512, 2048 * h + 2048), 0, S - 1)
        cosTh = np.ascontiguousarray(
            cos_full[rows].reshape(NT_KEY, 128, 64).transpose(1, 0, 2).reshape(128, TW))
        sinTh = np.ascontiguousarray(
            sin_full[rows].reshape(NT_KEY, 128, 64).transpose(1, 0, 2).reshape(128, TW))
        flag = np.full(128, float(h), np.float32)
        in_maps.append(dict(xin=xin, wqT=wqT, wkT=wkT, wvT=wvT, woT=woT, wff=wff,
                            lnw=lnw, lnb=lnb, qgain=qgain, cosTh=cosTh, sinTh=sinTh,
                            flag=flag, m0=m0, m4=m4, idm=idm))
    res = run_bass_kernel_spmd(nc, in_maps, core_ids=list(range(8)))
    out = np.empty((B, S, D), res.results[0]["yout"].dtype)
    for c in range(8):
        b, h = c // 2, c % 2
        out[b, 2048 * h:2048 * h + 2048] = res.results[c]["yout"]
    return out


# version 11
# speedup vs baseline: 1.0199x; 1.0199x over previous
import math
import numpy as np
import concourse.bass as bass
import concourse.mybir as mybir
from concourse.bass_utils import run_bass_kernel_spmd
from concourse.alu_op_type import AluOpType as OP
from contextlib import ExitStack

F32 = mybir.dt.float32
BF16 = mybir.dt.bfloat16
I32 = mybir.dt.int32
AF = mybir.ActivationFunctionType
AX = mybir.AxisListType

DEBUG = False

D = 2048
H = 16
KVH = 4
HD = 128
NKEY = 2560
NT_KEY = 20
NT_OWN = 16
NG = 2
OWN_G = 8
KEY_G = 12
EPS_LN = 1e-5
EPS_RMS = 1.1920928955078125e-07
TW = NT_KEY * 64

ENGS = ("sync", "scalar", "vector", "tensor", "gpsimd")
DMA_ENGS = {"sync", "gpsimd"}
DMA_SLOTS = 6


class Prog:

    def __init__(self, nc, sems):
        self.nc = nc
        self.sems = sems
        self.ops = {e: [] for e in ENGS}
        self.cnt = {k: 0 for k in sems}
        for e in ENGS:
            self.cnt.setdefault(e, 0)
        self.wmax = {e: {} for e in ENGS}
        self.dirty = {e: False for e in ENGS}
        self.dma_i = {q: 0 for q in DMA_ENGS}

    def _waits(self, eng, waits):
        for tok in waits:
            if tok is None:
                continue
            pe, v = tok
            if self.wmax[eng].get(pe, 0) >= v:
                continue
            self.wmax[eng][pe] = v
            sem = self.sems[pe]
            self.ops[eng].append(lambda E, sem=sem, v=v: E.wait_ge(sem, v))

    def op(self, eng, fn, waits=(), inc=False):
        self._waits(eng, waits)
        if inc:
            self.cnt[eng] += 1
            sem = self.sems[eng]
            self.ops[eng].append(lambda E, fn=fn, sem=sem: fn(E).then_inc(sem, 1))
            self.dirty[eng] = False
            return (eng, self.cnt[eng])
        self.ops[eng].append(fn)
        self.dirty[eng] = True
        return None

    def dma(self, eng, out, in_, waits=(), slow=False):
        i = self.dma_i[eng]
        self.dma_i[eng] = i + 1
        slot, cyc = i % DMA_SLOTS, i // DMA_SLOTS
        key = f"{eng}{slot}"
        w = list(waits)
        if cyc > 0:
            w.append((key, 16 * cyc))
        self._waits(eng, w)
        self.cnt[key] = 16 * (cyc + 1)
        sem = self.sems[key]
        self.ops[eng].append(lambda E, o=out, i_=in_, sem=sem, slow=slow: E.dma_start(
            out=o, in_=i_, allow_slow_non_contiguous=slow).then_inc(sem, 16))
        return (key, 16 * (cyc + 1))

    def tok(self, eng):
        assert not self.dirty[eng], f"{eng} has uncounted trailing ops"
        return (eng, self.cnt[eng])

    def barrier(self):
        for e in ENGS:
            assert not self.dirty[e], f"barrier with uncounted ops on {e}"
        toks = [(k, v) for k, v in self.cnt.items() if v > 0]
        for e in ENGS:
            self._waits(e, toks)

    def emit(self, block):
        for eng in ENGS:
            def section(E, eng=eng):
                for f in self.ops[eng]:
                    f(E)
            getattr(block, eng)(section)


def build_program():
    nc = bass.Bass("TRN2", target_bir_lowering=False,
                   detect_race_conditions=False)
    d = {}
    def din(name, shape, dtype=F32):
        d[name] = nc.dram_tensor(name, shape, dtype, kind="ExternalInput").ap()
        return d[name]
    def dint(name, shape, dtype):
        d[name] = nc.dram_tensor(name, shape, dtype, kind="Internal").ap()
        return d[name]

    xin = din("xin", [NKEY, D])
    wqT = din("wqT", [D, D])
    wkT = din("wkT", [D, KVH * HD])
    wvT = din("wvT", [D, KVH * HD])
    woT = din("woT", [D, D])
    wff = din("wff", [D, D])
    lnw_h = din("lnw", [128, 16])
    lnb_h = din("lnb", [128, 16])
    qgain_h = din("qgain", [H])
    cosT_h = din("cosTh", [128, TW])
    sinT_h = din("sinTh", [128, TW])
    flag_h = din("flag", [128])
    m0_h = din("m0", [128, 128])
    m4_h = din("m4", [128, 128])
    idm_h = din("idm", [128, 128])
    yout = nc.dram_tensor("yout", [NT_OWN * 128, D], F32, kind="ExternalOutput").ap()

    s_wq = dint("s_wq", [D, D], BF16)
    s_wo = dint("s_wo", [D, D], BF16)
    s_weff = dint("s_weff", [D, D], BF16)

    dbg = {}
    if DEBUG:
        for nm, shp, dt in [("dbg_small", [128, 64], F32), ("dbg_xn", [128, D], BF16),
                            ("dbg_kbf", [128, 512], BF16), ("dbg_vtok", [128, 512], BF16),
                            ("dbg_qbf", [128, D], BF16), ("dbg_pT", [128, 640], BF16),
                            ("dbg_y", [128, 128], BF16), ("dbg_xn2a", [128, D], BF16),
                            ("dbg_cbf", [1, D], BF16), ("dbg_invf", [128, 64], F32),
                            ("dbg_weff", [128, D], BF16), ("dbg_costile", [128, 64], BF16),
                            ("dbg_sintile", [128, 64], BF16), ("dbg_xnfm", [128, 128], BF16),
                            ("dbg_lns", [128, 16], F32), ("dbg_qfm", [128, 2048], BF16)]:
            dbg[nm] = nc.dram_tensor(nm, shp, dt, kind="ExternalOutput").ap()

    sems = {}
    stack = ExitStack()
    for e in ("scalar", "vector", "tensor"):
        sems[e] = stack.enter_context(nc.semaphore(f"sem_{e}"))
    for q in DMA_ENGS:
        for i in range(DMA_SLOTS):
            sems[f"{q}{i}"] = stack.enter_context(nc.semaphore(f"sem_{q}{i}"))
    P = Prog(nc, sems)
    E = stack.enter_context

    cst = E(nc.sbuf_tensor("cst", [128, 4], F32))
    sc128 = E(nc.sbuf_tensor("sc128", [128, 1], F32))
    negB = E(nc.sbuf_tensor("negB", [128, H], F32))
    grep = E(nc.sbuf_tensor("grep", [128, H], F32))
    flag_sb = E(nc.sbuf_tensor("flag_sb", [128, 1], F32))
    m0_bf = E(nc.sbuf_tensor("m0_bf", [128, 128], BF16))
    m0f_bf = E(nc.sbuf_tensor("m0f_bf", [128, 128], BF16))
    m4_bf = E(nc.sbuf_tensor("m4_bf", [128, 128], BF16))
    idm_bf = E(nc.sbuf_tensor("idm_bf", [128, 128], BF16))
    ones_r_bf = E(nc.sbuf_tensor("ones_r_bf", [1, 128], BF16))
    ones_r_f = E(nc.sbuf_tensor("ones_r_f", [1, 128], F32))
    ones_c_f = E(nc.sbuf_tensor("ones_c_f", [128, 1], F32))
    ones_c_bf = E(nc.sbuf_tensor("ones_c_bf", [128, 1], BF16))
    c_bf = E(nc.sbuf_tensor("c_bf", [1, D], BF16))
    cosT = E(nc.sbuf_tensor("cosT", [128, TW], BF16))
    sinT = E(nc.sbuf_tensor("sinT", [128, TW], BF16))

    with ExitStack() as p0:
        e0 = p0.enter_context
        sm1 = e0(nc.sbuf_tensor("sm1", [1, 128], F32))
        sm2 = e0(nc.sbuf_tensor("sm2", [1, 128], F32))
        sm3 = e0(nc.sbuf_tensor("sm3", [1, 128], F32))
        mload = e0(nc.sbuf_tensor("mload", [128, 3 * 128], F32))
        tbl = [e0(nc.sbuf_tensor(f"tbl{i}", [128, TW], F32)) for i in range(2)]
        ps_rep = e0(nc.psum_tensor("ps_rep", [128, 64], F32))

        def memsets(V):
            V.memset(cst[:, 0:1], EPS_LN)
            V.memset(cst[:, 1:2], EPS_RMS)
            V.memset(cst[:, 2:3], 128.0)
            V.memset(cst[:, 3:4], 10000.0)
            V.memset(ones_r_bf[:], 1.0)
            V.memset(ones_r_f[:], 1.0)
            V.memset(ones_c_f[:], 1.0)
            return V.memset(ones_c_bf[:], 1.0)
        P.op("vector", memsets, inc=True)
        t_con = P.tok("vector")

        tks = []
        t_ctl = P.dma("sync", tbl[0][:], cosT_h)
        t_stl = P.dma("sync", tbl[1][:], sinT_h)
        tks.append(P.op("vector", lambda V: V.tensor_copy(cosT[:], tbl[0][:]),
                        waits=[t_ctl], inc=True))
        tks.append(P.op("scalar", lambda S: S.copy(sinT[:], tbl[1][:]),
                        waits=[t_stl], inc=True))
        tks.append(P.dma("sync", sm2[0:1, 0:16], qgain_h.unsqueeze(0)))
        tks.append(P.dma("sync", flag_sb[:, 0:1], flag_h.unsqueeze(1)))
        tks.append(P.dma("sync", mload[:, 0:128], m0_h))
        tks.append(P.dma("sync", mload[:, 128:256], m4_h))
        tks.append(P.dma("sync", mload[:, 256:384], idm_h))

        def casts0(V):
            V.tensor_copy(m0_bf[:], mload[:, 0:128])
            V.tensor_copy(m4_bf[:], mload[:, 128:256])
            V.tensor_copy(idm_bf[:], mload[:, 256:384])
            return V.tensor_scalar_mul(m0f_bf[:], mload[:, 0:128], flag_sb[:, 0:1])
        P.op("vector", casts0, waits=tks + [t_con], inc=True)

        def sconsts(S):
            S.activation(sm3[0:1, 0:1], cst[0:1, 2:3], AF.Sqrt)
            return S.activation(sm3[0:1, 16:32], sm2[0:1, 0:16], AF.Abs)
        P.op("scalar", sconsts, waits=[t_con] + tks, inc=True)
        t_sc0 = P.tok("scalar")
        P.op("vector", lambda V: V.tensor_scalar_mul(sm1[0:1, 16:32], sm3[0:1, 16:32],
                                                     sm3[0:1, 0:1]), waits=[t_sc0],
             inc=True)
        t_v1a = P.tok("vector")
        def v1b(V):
            V.tensor_scalar_mul(sm1[0:1, 16:32], sm1[0:1, 16:32], -1.0)
            return V.reciprocal(sm1[0:1, 0:1], sm3[0:1, 0:1])
        P.op("vector", v1b, waits=[t_v1a], inc=True)
        t_v1 = P.tok("vector")
        def repl2(T):
            T.matmul(ps_rep[:, 0:16], ones_r_f[:], sm1[0:1, 16:32], start=True, stop=True)
            T.matmul(ps_rep[:, 16:32], ones_r_f[:], sm2[0:1, 0:16], start=True, stop=True)
            return T.matmul(ps_rep[:, 32:33], ones_r_f[:], sm1[0:1, 0:1],
                            start=True, stop=True)
        P.op("tensor", repl2, waits=[t_v1], inc=True)
        t_mm1 = P.tok("tensor")
        def cop2(V):
            V.tensor_copy(negB[:], ps_rep[:, 0:16])
            V.tensor_copy(grep[:], ps_rep[:, 16:32])
            return V.tensor_copy(sc128[:], ps_rep[:, 32:33])
        P.op("vector", cop2, waits=[t_mm1], inc=True)
        t_grep = P.tok("vector")
        if DEBUG:
            P.dma("gpsimd", dbg["dbg_small"][:, 0:1], sc128[:], slow=True)
            P.dma("gpsimd", dbg["dbg_small"][:, 1:17], negB[:], slow=True)
            P.dma("gpsimd", dbg["dbg_small"][:, 17:33], grep[:], slow=True)
            P.dma("gpsimd", dbg["dbg_small"][:, 33:37], cst[:], slow=True)
            P.dma("gpsimd", dbg["dbg_small"][:, 38:39], flag_sb[:], slow=True)
        P.barrier()

    with ExitStack() as pg:
        eg = pg.enter_context
        y_g = eg(nc.sbuf_tensor("y_g", [128, OWN_G * H * 128], BF16))

        def emit_group(g):
            p12 = ExitStack()
            xnfm_g = p12.enter_context(
                nc.sbuf_tensor(f"xnfm_g{g}", [128, OWN_G * 16 * 128], BF16))
            kfm_g = p12.enter_context(
                nc.sbuf_tensor(f"kfm_g{g}", [128, KVH * KEY_G * 128], BF16))
            vtok_g = p12.enter_context(
                nc.sbuf_tensor(f"vtok_g{g}", [128, KEY_G * 516], BF16))

            with ExitStack() as p1:
                e1 = p1.enter_context
                wk_bf = e1(nc.sbuf_tensor(f"wk_bf_g{g}", [128, 16 * 512], BF16))
                wv_bf = e1(nc.sbuf_tensor(f"wv_bf_g{g}", [128, 16 * 512], BF16))
                xld = [e1(nc.sbuf_tensor(f"xld{i}_g{g}", [128, D], F32)) for i in range(2)]
                xn_bf = e1(nc.sbuf_tensor(f"xn_bf_g{g}", [128, 2 * D], BF16))
                sq_scr = e1(nc.sbuf_tensor(f"sq_scr_g{g}", [128, D], BF16))
                xnfm_t = e1(nc.sbuf_tensor(f"xnfm_t_g{g}", [128, 16 * 128], BF16))
                k_bf = e1(nc.sbuf_tensor(f"k_bf_g{g}", [128, 512], BF16))
                sA = e1(nc.sbuf_tensor(f"sA_g{g}", [128, 64], F32))
                sB = e1(nc.sbuf_tensor(f"sB_g{g}", [128, 64], F32))
                sC = e1(nc.sbuf_tensor(f"sC_g{g}", [128, 64], F32))
                sD = e1(nc.sbuf_tensor(f"sD_g{g}", [128, 64], F32))
                lns = e1(nc.sbuf_tensor(f"lns_g{g}", [128, 32], F32))
                ps_k = e1(nc.psum_tensor(f"ps_k_g{g}", [128, 512], F32))
                ps_v = e1(nc.psum_tensor(f"ps_v_g{g}", [128, 512], F32))
                ps_tp = [e1(nc.psum_tensor(f"ps_tp{i}_g{g}", [128, 128], BF16))
                         for i in range(2)]

                kstg = [e1(nc.sbuf_tensor(f"kstg{i}_g{g}", [128, 512], F32))
                        for i in range(3)]
                t_wkv = []
                kprev = [None] * 3

                def emit_wkv():
                    for c in range(16):
                        for idx, (srcT, dstB) in enumerate(((wkT, wk_bf), (wvT, wv_bf))):
                            sl = (2 * c + idx) % 3
                            t_ld = P.dma("gpsimd", kstg[sl][:], srcT[128 * c:128 * c + 128, :],
                                         waits=[kprev[sl]] if kprev[sl] else [])
                            if (2 * c + idx) % 2 == 0:
                                t_cv = P.op("scalar", lambda S, sl=sl, c=c, dstB=dstB: S.copy(
                                    dstB[:, 512 * c:512 * c + 512], kstg[sl][:]),
                                    waits=[t_ld], inc=True)
                            else:
                                t_cv = P.op("vector", lambda V, sl=sl, c=c, dstB=dstB:
                                            V.tensor_copy(dstB[:, 512 * c:512 * c + 512],
                                                          kstg[sl][:]), waits=[t_ld], inc=True)
                            kprev[sl] = t_cv
                            t_wkv.append(t_cv)

                items = []
                if g == 0:
                    wstg = [e1(nc.sbuf_tensor(f"wstg{i}", [128, D], F32)) for i in range(2)]
                    t_scrP = e1(nc.sbuf_tensor("t_scrP", [128, D], F32))
                    tern_bf = [e1(nc.sbuf_tensor(f"tern_bf{i}", [128, D], BF16))
                               for i in range(2)]
                    weff_bf0 = e1(nc.sbuf_tensor("weff_bf0", [128, D], BF16))
                    lnw_sb = e1(nc.sbuf_tensor("lnw_sb", [128, 16], F32))
                    lnb_sb = e1(nc.sbuf_tensor("lnb_sb", [128, 16], F32))
                    lnb_bf = e1(nc.sbuf_tensor("lnb_bf", [128, 16], BF16))
                    asum = e1(nc.sbuf_tensor("asum", [128, 20], F32))
                    ps_c = e1(nc.psum_tensor("ps_c", [128, D], F32))
                    tk_lw = P.dma("sync", lnw_sb[:], lnw_h)
                    tk_lb = P.dma("sync", lnb_sb[:], lnb_h)
                    t_lnbbf = P.op("vector", lambda V: V.tensor_copy(lnb_bf[:], lnb_sb[:]),
                                   waits=[tk_lb], inc=True)
                    pst = {"wi": 0, "ci": 0, "wfree": [None, None],
                           "cfree": [None, None], "absred": [], "t_scale": None,
                           "t_tern_mm": None, "t_weff_st": None,
                           "mm_pend": None, "mm_tok": [None, None]}

                    def abs_item(c):
                        def run():
                            sl = pst["wi"] % 2
                            pst["wi"] += 1
                            t_ld = P.dma("gpsimd", wstg[sl][:], wff[128 * c:128 * c + 128, :],
                                         waits=[pst["wfree"][sl]] if pst["wfree"][sl] else [])
                            t_r = P.op("vector", lambda V, sl=sl, c=c: V.tensor_reduce(
                                asum[:, c:c + 1], wstg[sl][:], AX.X, OP.add,
                                apply_absolute_value=True), waits=[t_ld], inc=True)
                            pst["wfree"][sl] = t_r
                            pst["absred"].append(t_r)
                        return run

                    def scale_item():
                        def run():
                            t1 = P.op("vector", lambda V: V.tensor_reduce(
                                asum[:, 16:17], asum[:, 0:16], AX.X, OP.add),
                                waits=pst["absred"], inc=True)
                            t2 = P.op("tensor", lambda T: T.matmul(
                                ps_c[0:1, 0:1], asum[:, 16:17], ones_c_f[:],
                                start=True, stop=True, skip_group_check=True),
                                waits=[t1], inc=True)
                            t3 = P.op("vector", lambda V: V.reciprocal(
                                asum[0:1, 17:18], ps_c[0:1, 0:1]), waits=[t2], inc=True)
                            t4 = P.op("vector", lambda V: V.tensor_scalar_mul(
                                asum[0:1, 18:19], asum[0:1, 17:18], float(D * D)),
                                waits=[t3], inc=True)
                            t5 = P.op("tensor", lambda T: T.matmul(
                                ps_c[:, 4:5], ones_r_f[:], asum[0:1, 18:19],
                                start=True, stop=True, skip_group_check=True),
                                waits=[t4], inc=True)
                            pst["t_scale"] = P.op("vector", lambda V: V.tensor_copy(
                                asum[:, 19:20], ps_c[:, 4:5]), waits=[t5], inc=True)
                        return run

                    def flush_tern_mm():
                        if pst["mm_pend"] is None:
                            return
                        c, par, t_t = pst["mm_pend"]
                        pst["mm_pend"] = None
                        for s5 in range(4):
                            P.op("tensor", lambda T, c=c, s5=s5, par=par: T.matmul(
                                ps_c[0:1, 512 * s5:512 * s5 + 512], lnb_bf[:, c:c + 1],
                                tern_bf[par][:, 512 * s5:512 * s5 + 512],
                                start=(c == 0), stop=(c == 15), skip_group_check=True),
                                waits=[t_t, t_lnbbf] if s5 == 0 else [], inc=(s5 == 3))
                        pst["t_tern_mm"] = pst["mm_tok"][par] = P.tok("tensor")

                    def tern_item(c):
                        def run():
                            sl = pst["wi"] % 2
                            par = c % 2
                            pst["wi"] += 1
                            t_ld = P.dma("gpsimd", wstg[sl][:], wff[128 * c:128 * c + 128, :],
                                         waits=[pst["wfree"][sl]] if pst["wfree"][sl] else [])
                            aw = [t_ld, pst["t_scale"]]
                            if pst.get("prev_tt"):
                                aw.append(pst["prev_tt"])
                            t_a = P.op("scalar", lambda S, sl=sl: S.activation(
                                t_scrP[:], wstg[sl][:], AF.Identity,
                                scale=asum[:, 19:20]),
                                waits=aw, inc=True)
                            tw = [t_a]
                            if pst["mm_tok"][par]:
                                tw.append(pst["mm_tok"][par])
                            def tern_chunk(V, sl=sl, par=par):
                                V.tensor_single_scalar(wstg[sl][:], t_scrP[:], -0.5, OP.is_lt)
                                return V.scalar_tensor_tensor(tern_bf[par][:], t_scrP[:], 0.5,
                                                              wstg[sl][:],
                                                              OP.is_gt, OP.subtract)
                            t_t = P.op("vector", lambda V, f=tern_chunk: f(V),
                                       waits=tw, inc=True)
                            pst["wfree"][sl] = t_t
                            pst["prev_tt"] = t_t
                            ww = [t_t, tk_lw]
                            if pst["t_weff_st"]:
                                ww.append(pst["t_weff_st"])
                            t_w = P.op("scalar", lambda S, c=c, par=par: S.activation(
                                weff_bf0[:], tern_bf[par][:], AF.Identity,
                                scale=lnw_sb[:, c:c + 1]), waits=ww, inc=True)
                            pst["t_weff_st"] = P.dma("gpsimd",
                                                     s_weff[128 * c:128 * c + 128, :],
                                                     weff_bf0[:], waits=[t_w])
                            flush_tern_mm()
                            pst["mm_pend"] = (c, par, t_t)
                        return run

                    def cfin_item():
                        def run():
                            flush_tern_mm()
                            P.op("vector", lambda V: V.tensor_copy(c_bf[0:1, :], ps_c[0:1, :]),
                                 waits=[pst["t_tern_mm"], pst["t_weff_st"]], inc=True)
                        return run

                    for c in range(16):
                        items.append(abs_item(c))
                    items.append(scale_item())
                    for c in range(16):
                        items.append(tern_item(c))
                    items.append(cfin_item())

                t_xfree = [None, None]
                t_xld = [None, None]
                t_tp_ev = [None, None]
                t_xtp_last = [None, None]
                t_kepi = None
                t_vevac = None
                P.op("vector", lambda V: V.memset(vtok_g[:], 1.0), inc=True)
                t_xn_arr = [None, None]

                def ln_emit(s):
                    kt = 8 * g + s
                    b = s % 2
                    lb = 16 * b
                    xb = D * b
                    t_xld[b] = P.dma("sync", xld[b][:], xin[128 * kt:128 * kt + 128, :],
                                     waits=[t_xfree[b]] if t_xfree[b] else [])
                    t_s1 = P.op("vector", lambda V, b=b, lb=lb: V.tensor_reduce(
                        lns[:, lb + 0:lb + 1], xld[b][:], AX.X, OP.add), waits=[t_xld[b]],
                        inc=True)
                    t_s2 = P.op("scalar", lambda S, b=b, lb=lb: S.activation(
                        sq_scr[:], xld[b][:], AF.Square, accum_out=lns[:, lb + 1:lb + 2]),
                        waits=[t_xld[b]], inc=True)
                    def ln_a(V, lb=lb):
                        V.tensor_scalar_mul(lns[:, lb + 2:lb + 3], lns[:, lb + 0:lb + 1], 1.0 / D)
                        return V.scalar_tensor_tensor(lns[:, lb + 3:lb + 4], lns[:, lb + 0:lb + 1],
                                                      1.0 / (D * D),
                                                      lns[:, lb + 0:lb + 1], OP.mult, OP.mult)
                    t_lna = P.op("vector", ln_a, waits=[t_s1], inc=True)
                    t_var = P.op("vector", lambda V, lb=lb: V.scalar_tensor_tensor(
                        lns[:, lb + 4:lb + 5], lns[:, lb + 1:lb + 2], 1.0 / D, lns[:, lb + 3:lb + 4],
                        OP.mult, OP.subtract), waits=[t_lna, t_s2], inc=True)
                    t_std = P.op("scalar", lambda S, lb=lb: S.activation(
                        lns[:, lb + 5:lb + 6], lns[:, lb + 4:lb + 5], AF.Sqrt, bias=cst[:, 0:1]),
                        waits=[t_var], inc=True)
                    t_rstd = P.op("vector", lambda V, lb=lb: V.reciprocal(
                        lns[:, lb + 6:lb + 7], lns[:, lb + 5:lb + 6]), waits=[t_std], inc=True)
                    t_nb = P.op("vector", lambda V, lb=lb: V.scalar_tensor_tensor(
                        lns[:, lb + 7:lb + 8], lns[:, lb + 2:lb + 3], -1.0, lns[:, lb + 6:lb + 7],
                        OP.mult, OP.mult), waits=[t_rstd], inc=True)
                    t_xn = P.op("scalar", lambda S, b=b, lb=lb, xb=xb: S.activation(
                        xn_bf[:, xb:xb + D], xld[b][:], AF.Identity, bias=lns[:, lb + 7:lb + 8],
                        scale=lns[:, lb + 6:lb + 7]),
                        waits=[t_nb] + ([t_xtp_last[b]] if t_xtp_last[b] else []), inc=True)
                    t_xfree[b] = t_xn
                    t_xn_arr[b] = t_xn

                ln_emit(0)
                emit_wkv()
                for s in range(KEY_G):
                    kt = 8 * g + s
                    b = s % 2
                    lb = 16 * b
                    xb = D * b
                    own = kt >= 4 and (kt - 4) // 8 == g
                    ot_local = (kt - 4) - 8 * g if own else None
                    if s + 1 < KEY_G:
                        ln_emit(s + 1)
                    t_xn = t_xn_arr[b]
                    dst = xnfm_g if own else xnfm_t
                    base = (ot_local * 16 * 128) if own else 0
                    for c in range(16):
                        i = c % 2
                        w = [t_xn]
                        if t_tp_ev[i]:
                            w.append(t_tp_ev[i])
                        tk = P.op("tensor", lambda T, c=c, i=i, xb=xb: T.transpose(
                            ps_tp[i][:], xn_bf[:, xb + 128 * c:xb + 128 * c + 128], idm_bf[:]),
                            waits=w, inc=True)
                        if c == 15:
                            t_xtp_last[b] = tk
                        if i == 0:
                            t_tp_ev[i] = P.op("scalar", lambda S, c=c, i=i, dst=dst, base=base:
                                              S.copy(dst[:, base + 128 * c:base + 128 * c + 128],
                                                     ps_tp[i][:]), waits=[tk], inc=True)
                        else:
                            t_tp_ev[i] = P.op("vector", lambda V, c=c, i=i, dst=dst, base=base:
                                              V.tensor_copy(dst[:, base + 128 * c:base + 128 * c + 128],
                                                            ps_tp[i][:]), waits=[tk], inc=True)
                    kv_waits = [t_tp_ev[0], t_tp_ev[1]]
                    if t_kepi:
                        kv_waits.append(t_kepi)
                    for c in range(16):
                        P.op("tensor", lambda T, c=c, dst=dst, base=base: T.matmul(
                            ps_k[:], dst[:, base + 128 * c:base + 128 * c + 128],
                            wk_bf[:, 512 * c:512 * c + 512],
                            start=(c == 0), stop=(c == 15), skip_group_check=True),
                            waits=(kv_waits if c == 0 else []) + [t_wkv[2 * c]],
                            inc=(c == 15))
                    t_kmm = P.tok("tensor")
                    for c in range(16):
                        P.op("tensor", lambda T, c=c, dst=dst, base=base: T.matmul(
                            ps_v[:], dst[:, base + 128 * c:base + 128 * c + 128],
                            wv_bf[:, 512 * c:512 * c + 512],
                            start=(c == 0), stop=(c == 15), skip_group_check=True),
                            waits=([t_vevac] if (c == 0 and t_vevac) else []) + [t_wkv[2 * c + 1]],
                            inc=(c == 15))
                    t_vmm = P.tok("tensor")
                    def v_evac(V, s=s):
                        for hk in range(KVH - 1):
                            V.tensor_copy(vtok_g[:, 516 * s + 129 * hk:516 * s + 129 * hk + 128],
                                          ps_v[:, 128 * hk:128 * hk + 128])
                        return V.tensor_copy(
                            vtok_g[:, 516 * s + 129 * 3:516 * s + 129 * 3 + 128],
                            ps_v[:, 384:512])
                    t_vevac = P.op("vector", v_evac, waits=[t_vmm], inc=True)
                    def k_sq(S, lb=lb):
                        r = None
                        for hk in range(KVH):
                            r = S.activation(sq_scr[:, 0:128],
                                             ps_k[:, 128 * hk:128 * hk + 128],
                                             AF.Square, accum_out=lns[:, lb + 8 + hk:lb + 9 + hk])
                        return r
                    t_ksq = P.op("scalar", k_sq, waits=[t_kmm], inc=True)
                    t_kstd = P.op("scalar", lambda S, lb=lb: S.activation(
                        lns[:, lb + 8:lb + 12], lns[:, lb + 8:lb + 12], AF.Sqrt, bias=cst[:, 1:2],
                        scale=1.0 / HD), waits=[t_ksq], inc=True)
                    t_krs = P.op("vector", lambda V, lb=lb: V.reciprocal(
                        lns[:, lb + 8:lb + 12], lns[:, lb + 8:lb + 12]), waits=[t_kstd], inc=True)
                    t_krs2 = P.op("scalar", lambda S, lb=lb: S.copy(
                        lns[:, lb + 12:lb + 16], lns[:, lb + 8:lb + 12]),
                        waits=[t_krs], inc=True)
                    ctile = slice(64 * kt, 64 * kt + 64)
                    for hk in range(KVH):
                        s1 = slice(128 * hk, 128 * hk + 64)
                        s2 = slice(128 * hk + 64, 128 * hk + 128)
                        def k_rope_m(V, hk=hk, s1=s1, s2=s2, ctile=ctile, lb=lb):
                            rs = lns[:, lb + 12 + hk:lb + 13 + hk]
                            V.scalar_tensor_tensor(sA[:], ps_k[:, s1], rs, cosT[:, ctile],
                                                   OP.mult, OP.mult)
                            V.scalar_tensor_tensor(sB[:], ps_k[:, s2], rs, sinT[:, ctile],
                                                   OP.mult, OP.mult)
                            V.scalar_tensor_tensor(sC[:], ps_k[:, s1], rs, sinT[:, ctile],
                                                   OP.mult, OP.mult)
                            return V.scalar_tensor_tensor(sD[:], ps_k[:, s2], rs,
                                                          cosT[:, ctile], OP.mult, OP.mult)
                        t_km = P.op("vector", lambda V, f=k_rope_m: f(V),
                                    waits=[t_krs2] if hk == 0 else [], inc=True)
                        def k_rope_a(V, s1=s1, s2=s2):
                            V.tensor_add(k_bf[:, s1], sA[:], sB[:])
                            return V.tensor_sub(k_bf[:, s2], sD[:], sC[:])
                        P.op("vector", lambda V, f=k_rope_a: f(V), waits=[t_km],
                             inc=(hk == KVH - 1))
                    t_kepi = P.tok("vector")
                    if DEBUG and g == 0 and kt == 4:
                        P.dma("gpsimd", dbg["dbg_xn"][:], xn_bf[:], waits=[t_xn], slow=True)
                        P.dma("gpsimd", dbg["dbg_kbf"][:], k_bf[:], waits=[t_kepi], slow=True)
                        P.dma("gpsimd", dbg["dbg_vtok"][:], vtok_g[:, 512 * s:512 * s + 512],
                              waits=[t_vevac], slow=True)
                        P.dma("gpsimd", dbg["dbg_xnfm"][:], xnfm_g[:, 0:128],
                              waits=[t_tp_ev[0], t_tp_ev[1]], slow=True)
                        P.dma("gpsimd", dbg["dbg_lns"][:], lns[:], waits=[t_kepi], slow=True)
                    for hk in range(KVH):
                        i = hk % 2
                        tk = P.op("tensor", lambda T, hk=hk, i=i: T.transpose(
                            ps_tp[i][:], k_bf[:, 128 * hk:128 * hk + 128], idm_bf[:]),
                            waits=[t_kepi, t_tp_ev[i]], inc=True)
                        if i == 0:
                            t_tp_ev[i] = P.op("scalar", lambda S, hk=hk, i=i, s=s: S.copy(
                                kfm_g[:, (hk * KEY_G + s) * 128:(hk * KEY_G + s) * 128 + 128],
                                ps_tp[i][:]), waits=[tk], inc=True)
                        else:
                            t_tp_ev[i] = P.op("vector", lambda V, hk=hk, i=i, s=s: V.tensor_copy(
                                kfm_g[:, (hk * KEY_G + s) * 128:(hk * KEY_G + s) * 128 + 128],
                                ps_tp[i][:]), waits=[tk], inc=True)
                    lo = (len(items) * s) // KEY_G
                    hi = (len(items) * (s + 1)) // KEY_G
                    for it in items[lo:hi]:
                        it()
                P.barrier()

            with ExitStack() as p2:
                e2 = p2.enter_context
                wq_bf = e2(nc.sbuf_tensor(f"wq_bf_g{g}", [128, 16 * D], BF16))
                q_bf = e2(nc.sbuf_tensor(f"q_bf_g{g}", [128, D], BF16))
                qfm = e2(nc.sbuf_tensor(f"qfm_g{g}", [128, 16 * 128], BF16))
                pT = [e2(nc.sbuf_tensor(f"pT{i}_g{g}", [128, 640], BF16)) for i in range(2)]
                qs = e2(nc.sbuf_tensor(f"qs_g{g}", [128, 48], F32))
                sA2 = e2(nc.sbuf_tensor(f"sA2_g{g}", [128, 64], F32))
                sB2 = e2(nc.sbuf_tensor(f"sB2_g{g}", [128, 64], F32))
                sC2 = e2(nc.sbuf_tensor(f"sC2_g{g}", [128, 64], F32))
                sD2 = e2(nc.sbuf_tensor(f"sD2_g{g}", [128, 64], F32))
                sq2 = e2(nc.sbuf_tensor(f"sq2_g{g}", [128, 128], BF16))
                rsum = e2(nc.sbuf_tensor(f"rsum_g{g}", [128, 2], F32))
                ps_q = [e2(nc.psum_tensor(f"ps_q{i}_g{g}", [128, 512], F32))
                        for i in range(2)]
                ps_tp2 = [e2(nc.psum_tensor(f"ps_tp2{i}_g{g}", [128, 128], BF16))
                          for i in range(2)]
                ps_att = [e2(nc.psum_tensor(f"ps_att{i}_g{g}", [128, 769], F32))
                          for i in range(2)]

                t_wq = []
                t_wqc = {}
                items2 = []
                if g == 0:
                    wstg2 = [e2(nc.sbuf_tensor(f"wstg2{i}", [128, D], F32))
                             for i in range(3)]
                    cws2 = [e2(nc.sbuf_tensor(f"cws2{i}", [128, D], BF16))
                            for i in range(2)]
                    pst2 = {"wi": 0, "ci": 0, "wfree": [None] * 3,
                            "cfree": [None, None]}

                    def wqcast_item(c):
                        def run():
                            sl = pst2["wi"] % 3
                            pst2["wi"] += 1
                            t_ld = P.dma("sync", wstg2[sl][:], wqT[128 * c:128 * c + 128, :],
                                         waits=[pst2["wfree"][sl]] if pst2["wfree"][sl] else [])
                            if c % 2 == 0:
                                t_cv = P.op("scalar", lambda S, sl=sl, c=c: S.copy(
                                    wq_bf[:, D * c:D * c + D], wstg2[sl][:]),
                                    waits=[t_ld], inc=True)
                            else:
                                t_cv = P.op("vector", lambda V, sl=sl, c=c: V.tensor_copy(
                                    wq_bf[:, D * c:D * c + D], wstg2[sl][:]),
                                    waits=[t_ld], inc=True)
                            pst2["wfree"][sl] = t_cv
                            t_wqc[c] = t_cv
                            P.dma("sync", s_wq[128 * c:128 * c + 128, :],
                                  wq_bf[:, D * c:D * c + D], waits=[t_cv])
                        return run

                    def wocast_item(c):
                        def run():
                            sl = pst2["wi"] % 3
                            ob = pst2["ci"] % 2
                            pst2["wi"] += 1
                            pst2["ci"] += 1
                            t_ld = P.dma("gpsimd", wstg2[sl][:], woT[128 * c:128 * c + 128, :],
                                         waits=[pst2["wfree"][sl]] if pst2["wfree"][sl] else [])
                            cw = [t_ld]
                            if pst2["cfree"][ob]:
                                cw.append(pst2["cfree"][ob])
                            if c % 2 == 0:
                                t_cv = P.op("scalar", lambda S, sl=sl, ob=ob: S.copy(
                                    cws2[ob][:], wstg2[sl][:]), waits=cw, inc=True)
                            else:
                                t_cv = P.op("vector", lambda V, sl=sl, ob=ob: V.tensor_copy(
                                    cws2[ob][:], wstg2[sl][:]), waits=cw, inc=True)
                            pst2["wfree"][sl] = t_cv
                            pst2["cfree"][ob] = P.dma("gpsimd",
                                                      s_wo[128 * c:128 * c + 128, :],
                                                      cws2[ob][:], waits=[t_cv])
                        return run

                    items2 = [wocast_item(c) for c in range(16)]
                    for c in range(16):
                        wqcast_item(c)()
                else:
                    for c in range(16):
                        t_wq.append(P.dma("sync", wq_bf[:, D * c:D * c + D],
                                          s_wq[128 * c:128 * c + 128, :]))

                t_q_ev = [None, None]
                t_tp_ev2 = [None, None]
                t_exp = {}
                t_av = {}
                t_yev = {}
                for qt in range(OWN_G):
                    ot = 8 * g + qt
                    kt0 = qt
                    base = qt * 16 * 128
                    ctile = slice(64 * (ot + 4), 64 * (ot + 4) + 64)
                    if items2:
                        lo = (len(items2) * qt) // OWN_G
                        hi = (len(items2) * (qt + 1)) // OWN_G
                        for it in items2[lo:hi]:
                            it()
                    for s4 in range(4):
                        bq = s4 % 2
                        w0 = []
                        if t_q_ev[bq]:
                            w0.append(t_q_ev[bq])
                        for c in range(16):
                            cw = list(w0) if c == 0 else []
                            if qt == 0:
                                cw.append(t_wqc[c] if g == 0 else t_wq[c])
                            P.op("tensor", lambda T, c=c, s4=s4, bq=bq, base=base: T.matmul(
                                ps_q[bq][:], xnfm_g[:, base + 128 * c:base + 128 * c + 128],
                                wq_bf[:, D * c + 512 * s4:D * c + 512 * s4 + 512],
                                start=(c == 0), stop=(c == 15), skip_group_check=True),
                                waits=cw, inc=(c == 15))
                        t_qmm = P.tok("tensor")
                        def q_sq(S, s4=s4, bq=bq):
                            r = None
                            for j in range(4):
                                r = S.activation(sq2[:], ps_q[bq][:, 128 * j:128 * j + 128],
                                                 AF.Square,
                                                 accum_out=qs[:, 4 * s4 + j:4 * s4 + j + 1])
                            return r
                        t_qsq = P.op("scalar", q_sq, waits=[t_qmm], inc=True)
                        t_qstd = P.op("scalar", lambda S, s4=s4: S.activation(
                            qs[:, 16 + 4 * s4:20 + 4 * s4], qs[:, 4 * s4:4 * s4 + 4],
                            AF.Sqrt, bias=cst[:, 1:2], scale=1.0 / HD),
                            waits=[t_qsq], inc=True)
                        t_qrs = P.op("vector", lambda V, s4=s4: V.reciprocal(
                            qs[:, 16 + 4 * s4:20 + 4 * s4], qs[:, 16 + 4 * s4:20 + 4 * s4]),
                            waits=[t_qstd], inc=True)
                        t_qrg0 = P.op("vector", lambda V, s4=s4: V.tensor_mul(
                            qs[:, 16 + 4 * s4:20 + 4 * s4], qs[:, 16 + 4 * s4:20 + 4 * s4],
                            grep[:, 4 * s4:4 * s4 + 4]), waits=[t_qrs], inc=True)
                        t_qrg = P.op("scalar", lambda S, s4=s4: S.copy(
                            qs[:, 32 + 4 * s4:36 + 4 * s4], qs[:, 16 + 4 * s4:20 + 4 * s4]),
                            waits=[t_qrg0], inc=True)
                        for j in range(4):
                            hq = 4 * s4 + j
                            s1 = slice(128 * j, 128 * j + 64)
                            s2 = slice(128 * j + 64, 128 * j + 128)
                            o1 = slice(128 * hq, 128 * hq + 64)
                            o2 = slice(128 * hq + 64, 128 * hq + 128)
                            def q_rope_m(V, bq=bq, hq=hq, s1=s1, s2=s2, ctile=ctile):
                                rs = qs[:, 32 + hq:33 + hq]
                                V.scalar_tensor_tensor(sA2[:], ps_q[bq][:, s1], rs,
                                                       cosT[:, ctile], OP.mult, OP.mult)
                                V.scalar_tensor_tensor(sB2[:], ps_q[bq][:, s2], rs,
                                                       sinT[:, ctile], OP.mult, OP.mult)
                                V.scalar_tensor_tensor(sC2[:], ps_q[bq][:, s1], rs,
                                                       sinT[:, ctile], OP.mult, OP.mult)
                                return V.scalar_tensor_tensor(sD2[:], ps_q[bq][:, s2], rs,
                                                              cosT[:, ctile],
                                                              OP.mult, OP.mult)
                            t_qm = P.op("vector", lambda V, f=q_rope_m: f(V),
                                        waits=[t_qrg] if j == 0 else [], inc=True)
                            def q_rope_a(V, o1=o1, o2=o2):
                                V.tensor_add(q_bf[:, o1], sA2[:], sB2[:])
                                return V.tensor_sub(q_bf[:, o2], sD2[:], sC2[:])
                            P.op("vector", lambda V, f=q_rope_a: f(V), waits=[t_qm],
                                 inc=(j == 3))
                        t_q_ev[bq] = P.tok("vector")
                    t_qrope = P.tok("vector")
                    for c in range(16):
                        i = c % 2
                        w = [t_qrope]
                        if t_tp_ev2[i]:
                            w.append(t_tp_ev2[i])
                        tk = P.op("tensor", lambda T, c=c, i=i: T.transpose(
                            ps_tp2[i][:], q_bf[:, 128 * c:128 * c + 128], idm_bf[:]),
                            waits=w, inc=True)
                        if i == 0:
                            t_tp_ev2[i] = P.op("scalar", lambda S, c=c, i=i: S.copy(
                                qfm[:, 128 * c:128 * c + 128], ps_tp2[i][:]),
                                waits=[tk], inc=True)
                        else:
                            t_tp_ev2[i] = P.op("vector", lambda V, c=c, i=i: V.tensor_copy(
                                qfm[:, 128 * c:128 * c + 128], ps_tp2[i][:]),
                                waits=[tk], inc=True)
                    t_qfm = [t_tp_ev2[0], t_tp_ev2[1]]
                    if DEBUG and g == 0 and qt == 4:
                        P.dma("gpsimd", dbg["dbg_qbf"][:], q_bf[:], waits=[t_qrope], slow=True)
                        P.dma("gpsimd", dbg["dbg_qfm"][:], qfm[:], waits=list(t_qfm), slow=True)

                    t_mask_d = {}

                    def head_front(h):
                        ba = h % 2
                        hk = h // KVH
                        sw = list(t_qfm)
                        if h - 2 in t_exp:
                            sw.append(t_exp[h - 2])
                        for j in range(5):
                            kslot = kt0 + j
                            P.op("tensor", lambda T, j=j, ba=ba, hk=hk, kslot=kslot, h=h:
                                 T.matmul(ps_att[ba][:, 128 * j:128 * j + 128],
                                          kfm_g[:, (hk * KEY_G + kslot) * 128:
                                                (hk * KEY_G + kslot) * 128 + 128],
                                          qfm[:, 128 * h:128 * h + 128],
                                          start=True, stop=True, skip_group_check=True),
                                 waits=sw if j == 0 else [], inc=(j == 4))
                        t_sc = P.tok("tensor")
                        ew = [t_sc]
                        if h - 2 in t_av:
                            ew.append(t_av[h - 2])
                        P.op("scalar", lambda S, ba=ba, h=h: S.activation(
                            pT[ba][:, 0:640], ps_att[ba][:, 0:640], AF.Exp,
                            bias=negB[:, h:h + 1], scale=sc128[:, 0:1]),
                            waits=ew, inc=True)
                        t_exp[h] = P.tok("scalar")
                        def masks(V, ba=ba, qt=qt, g=g):
                            m0use = m0f_bf if (8 * g + qt) <= 3 else m0_bf
                            V.tensor_mul(pT[ba][:, 0:128], pT[ba][:, 0:128], m0use[:])
                            r = V.tensor_mul(pT[ba][:, 512:640], pT[ba][:, 512:640],
                                             m4_bf[:])
                            for j in range(1, 5):
                                if 8 * g + qt + j <= 3:
                                    r = V.tensor_scalar_mul(
                                        pT[ba][:, 128 * j:128 * j + 128],
                                        pT[ba][:, 128 * j:128 * j + 128],
                                        flag_sb[:, 0:1])
                            return r
                        P.op("vector", masks, waits=[t_exp[h]], inc=True)
                        t_mask_d[h] = P.tok("vector")

                    def head_back(h):
                        ba = h % 2
                        hk = h // KVH
                        aw = [t_mask_d[h]]
                        if h - 2 in t_yev:
                            aw.append(t_yev[h - 2])
                        for j in range(5):
                            kslot = kt0 + j
                            P.op("tensor", lambda T, j=j, ba=ba, hk=hk, kslot=kslot:
                                 T.matmul(ps_att[ba][:, 640:769],
                                          pT[ba][:, 128 * j:128 * j + 128],
                                          vtok_g[:, 516 * kslot + 129 * hk:
                                                 516 * kslot + 129 * hk + 129],
                                          start=(j == 0), stop=(j == 4),
                                          skip_group_check=True),
                                 waits=aw if j == 0 else [], inc=(j == 4))
                        t_av[h] = P.tok("tensor")
                        t_rsum = P.op("vector", lambda V, ba=ba, h=h: V.reciprocal(
                            rsum[:, h % 2:h % 2 + 1], ps_att[ba][:, 768:769]),
                            waits=[t_av[h]], inc=True)
                        P.op("scalar", lambda S, ba=ba, qt=qt, h=h: S.activation(
                            y_g[:, (qt * H + h) * 128:(qt * H + h) * 128 + 128],
                            ps_att[ba][:, 640:768], AF.Identity,
                            scale=rsum[:, h % 2:h % 2 + 1]), waits=[t_rsum], inc=True)
                        t_yev[h] = P.tok("scalar")

                    for h in range(H):
                        head_front(h)
                        if h > 0:
                            head_back(h - 1)
                    head_back(H - 1)
                    t_av = {h - H: t for h, t in t_av.items() if h >= H - 2}
                    t_yev = {h - H: t for h, t in t_yev.items() if h >= H - 2}
                    t_exp = {}
                P.barrier()

            p12.close()
            with ExitStack() as p3:
                e3 = p3.enter_context
                wo_bf = e3(nc.sbuf_tensor(f"wo_bf_g{g}", [128, 16 * D], BF16))
                weff_sb = e3(nc.sbuf_tensor(f"weff_sb_g{g}", [128, 16 * D], BF16))
                xld3 = [e3(nc.sbuf_tensor(f"xld3{i}_g{g}", [128, D], F32)) for i in range(2)]
                xn2a = e3(nc.sbuf_tensor(f"xn2a_g{g}", [128, 2 * D], BF16))
                sq_scr3 = xn2a
                xnfm3 = e3(nc.sbuf_tensor(f"xnfm3_g{g}", [128, 16 * 128], BF16))
                yfm3 = e3(nc.sbuf_tensor(f"yfm3_g{g}", [128, 16 * 128], BF16))
                obuf = [e3(nc.sbuf_tensor(f"obuf{i}_g{g}", [128, 512], F32)) for i in range(2)]
                lns3 = e3(nc.sbuf_tensor(f"lns3_g{g}", [128, 32], F32))
                ps_o = [e3(nc.psum_tensor(f"ps_o{i}_g{g}", [128, 512], F32)) for i in range(2)]
                ps_tp3 = [e3(nc.psum_tensor(f"ps_tp3{i}_g{g}", [128, 128], BF16))
                          for i in range(2)]

                t_wo = []
                for c in range(16):
                    t_wo.append(P.dma("gpsimd", wo_bf[:, D * c:D * c + D],
                                      s_wo[128 * c:128 * c + 128, :]))
                    t_wo.append(P.dma("gpsimd", weff_sb[:, D * c:D * c + D],
                                      s_weff[128 * c:128 * c + 128, :]))

                t_xfree3 = [None, None]
                t_xld3 = [None, None]
                t_tp_ev3 = [None, None]
                t_xtp3_last = [None, None]
                t_xn2a_arr = [None, None]
                t_o_ev = [None, None]
                t_ob_st = [None, None]

                def ln3_emit(tt):
                    ot = 8 * g + tt
                    kt = ot + 4
                    b = tt % 2
                    lb = 16 * b
                    xb = D * b
                    t_xld3[b] = P.dma("sync", xld3[b][:], xin[128 * kt:128 * kt + 128, :],
                                      waits=[t_xfree3[b]] if t_xfree3[b] else [])
                    t_s1 = P.op("vector", lambda V, b=b, lb=lb: V.tensor_reduce(
                        lns3[:, lb + 0:lb + 1], xld3[b][:], AX.X, OP.add), waits=[t_xld3[b]],
                        inc=True)
                    sqw = [t_xld3[b]]
                    if t_xtp3_last[b]:
                        sqw.append(t_xtp3_last[b])
                    t_s2 = P.op("scalar", lambda S, b=b, lb=lb, xb=xb: S.activation(
                        sq_scr3[:, xb:xb + D], xld3[b][:], AF.Square,
                        accum_out=lns3[:, lb + 1:lb + 2]),
                        waits=sqw, inc=True)
                    def ln_a3(V, lb=lb):
                        V.tensor_scalar_mul(lns3[:, lb + 2:lb + 3], lns3[:, lb + 0:lb + 1], 1.0 / D)
                        return V.scalar_tensor_tensor(lns3[:, lb + 3:lb + 4], lns3[:, lb + 0:lb + 1],
                                                      1.0 / (D * D),
                                                      lns3[:, lb + 0:lb + 1], OP.mult, OP.mult)
                    t_lna = P.op("vector", ln_a3, waits=[t_s1], inc=True)
                    t_var = P.op("vector", lambda V, lb=lb: V.scalar_tensor_tensor(
                        lns3[:, lb + 4:lb + 5], lns3[:, lb + 1:lb + 2], 1.0 / D,
                        lns3[:, lb + 3:lb + 4],
                        OP.mult, OP.subtract), waits=[t_lna, t_s2], inc=True)
                    t_std = P.op("scalar", lambda S, lb=lb: S.activation(
                        lns3[:, lb + 5:lb + 6], lns3[:, lb + 4:lb + 5], AF.Sqrt, bias=cst[:, 0:1]),
                        waits=[t_var], inc=True)
                    t_rstd = P.op("vector", lambda V, lb=lb: V.reciprocal(
                        lns3[:, lb + 6:lb + 7], lns3[:, lb + 5:lb + 6]), waits=[t_std], inc=True)
                    t_r2 = P.op("vector", lambda V, lb=lb: V.tensor_mul(
                        lns3[:, lb + 7:lb + 8], lns3[:, lb + 6:lb + 7], lns3[:, lb + 6:lb + 7]),
                        waits=[t_rstd], inc=True)
                    t_v2 = P.op("vector", lambda V, lb=lb: V.tensor_mul(
                        lns3[:, lb + 8:lb + 9], lns3[:, lb + 7:lb + 8], lns3[:, lb + 4:lb + 5]),
                        waits=[t_r2], inc=True)
                    t_std2 = P.op("scalar", lambda S, lb=lb: S.activation(
                        lns3[:, lb + 9:lb + 10], lns3[:, lb + 8:lb + 9], AF.Sqrt, bias=cst[:, 0:1]),
                        waits=[t_v2], inc=True)
                    t_rstd2 = P.op("vector", lambda V, lb=lb: V.reciprocal(
                        lns3[:, lb + 10:lb + 11], lns3[:, lb + 9:lb + 10]), waits=[t_std2], inc=True)
                    t_sc1 = P.op("vector", lambda V, lb=lb: V.tensor_mul(
                        lns3[:, lb + 11:lb + 12], lns3[:, lb + 6:lb + 7], lns3[:, lb + 10:lb + 11]),
                        waits=[t_rstd2], inc=True)
                    t_ab = P.op("vector", lambda V, lb=lb: V.scalar_tensor_tensor(
                        lns3[:, lb + 12:lb + 13], lns3[:, lb + 2:lb + 3], -1.0,
                        lns3[:, lb + 11:lb + 12],
                        OP.mult, OP.mult), waits=[t_sc1], inc=True)
                    t_xn2a_arr[b] = P.op("scalar", lambda S, b=b, lb=lb, xb=xb: S.activation(
                        xn2a[:, xb:xb + D], xld3[b][:], AF.Identity, bias=lns3[:, lb + 12:lb + 13],
                        scale=lns3[:, lb + 11:lb + 12]), waits=[t_ab], inc=True)

                ln3_emit(0)
                for tt in range(OWN_G):
                    ot = 8 * g + tt
                    kt = ot + 4
                    b = tt % 2
                    xb = D * b
                    if tt + 1 < OWN_G:
                        ln3_emit(tt + 1)
                    t_xn2a = t_xn2a_arr[b]
                    for c in range(16):
                        i = c % 2
                        w = [t_xn2a]
                        if t_tp_ev3[i]:
                            w.append(t_tp_ev3[i])
                        tk = P.op("tensor", lambda T, c=c, i=i, xb=xb: T.transpose(
                            ps_tp3[i][:], xn2a[:, xb + 128 * c:xb + 128 * c + 128], idm_bf[:]),
                            waits=w, inc=True)
                        if c == 15:
                            t_xtp3_last[b] = tk
                        if i == 0:
                            t_tp_ev3[i] = P.op("scalar", lambda S, c=c, i=i: S.copy(
                                xnfm3[:, 128 * c:128 * c + 128], ps_tp3[i][:]),
                                waits=[tk], inc=True)
                        else:
                            t_tp_ev3[i] = P.op("vector", lambda V, c=c, i=i: V.tensor_copy(
                                xnfm3[:, 128 * c:128 * c + 128], ps_tp3[i][:]),
                                waits=[tk], inc=True)
                    for hh in range(16):
                        i = hh % 2
                        tk = P.op("tensor", lambda T, hh=hh, i=i, tt=tt: T.transpose(
                            ps_tp3[i][:], y_g[:, (tt * H + hh) * 128:(tt * H + hh) * 128 + 128],
                            idm_bf[:]), waits=[t_tp_ev3[i]], inc=True)
                        if i == 0:
                            t_tp_ev3[i] = P.op("scalar", lambda S, hh=hh, i=i: S.copy(
                                yfm3[:, 128 * hh:128 * hh + 128], ps_tp3[i][:]),
                                waits=[tk], inc=True)
                        else:
                            t_tp_ev3[i] = P.op("vector", lambda V, hh=hh, i=i: V.tensor_copy(
                                yfm3[:, 128 * hh:128 * hh + 128], ps_tp3[i][:]),
                                waits=[tk], inc=True)
                    t_fm_ready = [t_tp_ev3[0], t_tp_ev3[1]]
                    for s4 in range(4):
                        bo = s4 % 2
                        w0 = list(t_fm_ready)
                        if t_o_ev[bo]:
                            w0.append(t_o_ev[bo])
                        for c in range(16):
                            cw = list(w0) if c == 0 else []
                            if tt == 0:
                                cw.append(t_wo[2 * c])
                            P.op("tensor", lambda T, c=c, s4=s4, bo=bo: T.matmul(
                                ps_o[bo][:], yfm3[:, 128 * c:128 * c + 128],
                                wo_bf[:, D * c + 512 * s4:D * c + 512 * s4 + 512],
                                start=(c == 0), stop=False, skip_group_check=True),
                                waits=cw)
                        for c in range(16):
                            P.op("tensor", lambda T, c=c, s4=s4, bo=bo: T.matmul(
                                ps_o[bo][:], xnfm3[:, 128 * c:128 * c + 128],
                                weff_sb[:, D * c + 512 * s4:D * c + 512 * s4 + 512],
                                start=False, stop=False, skip_group_check=True),
                                waits=[t_wo[2 * c + 1]] if tt == 0 else [])
                        P.op("tensor", lambda T, s4=s4, bo=bo: T.matmul(
                            ps_o[bo][:], ones_r_bf[:], c_bf[0:1, 512 * s4:512 * s4 + 512],
                            start=False, stop=True, skip_group_check=True), inc=True)
                        t_omm = P.tok("tensor")
                        ww = [t_omm]
                        if t_ob_st[bo]:
                            ww.append(t_ob_st[bo])
                        t_o_ev[bo] = P.op("vector", lambda V, s4=s4, bo=bo, b=b: V.tensor_add(
                            obuf[bo][:], ps_o[bo][:], xld3[b][:, 512 * s4:512 * s4 + 512]),
                            waits=ww, inc=True)
                        t_ob_st[bo] = P.dma("sync",
                                            yout[128 * ot:128 * ot + 128,
                                                 512 * s4:512 * s4 + 512],
                                            obuf[bo][:], waits=[t_o_ev[bo]])
                        if s4 == 3:
                            t_xfree3[b] = t_o_ev[bo]
                P.barrier()

        for g in range(NG):
            emit_group(g)

    with nc.Block() as block:
        P.emit(block)
    stack.close()
    return nc


_prog_cache = {}


def kernel(L0_attn_x, L0_attn_Wq, L0_attn_Wk, L0_attn_Wv, L0_attn_Wo,
           L0_attn_q_gain, L0_ffn_W, L0_ffn_ln_w, L0_ffn_ln_b):
    x = np.ascontiguousarray(np.asarray(L0_attn_x, np.float32))
    B, S, _ = x.shape
    wqT = np.ascontiguousarray(np.asarray(L0_attn_Wq, np.float32).T)
    wkT = np.ascontiguousarray(np.asarray(L0_attn_Wk, np.float32).T)
    wvT = np.ascontiguousarray(np.asarray(L0_attn_Wv, np.float32).T)
    woT = np.ascontiguousarray(np.asarray(L0_attn_Wo, np.float32).T)
    wff = np.ascontiguousarray(np.asarray(L0_ffn_W, np.float32))
    lnw = np.ascontiguousarray(np.asarray(L0_ffn_ln_w, np.float32).reshape(16, 128).T)
    lnb = np.ascontiguousarray(np.asarray(L0_ffn_ln_b, np.float32).reshape(16, 128).T)
    qgain = np.asarray(L0_attn_q_gain, np.float32)
    iq = np.arange(128)[None, :]
    ik = np.arange(128)[:, None]
    m0 = (ik >= iq + 1).astype(np.float32)
    m4 = (ik <= iq).astype(np.float32)
    idm = np.eye(128, dtype=np.float32)

    if "tabs" not in _prog_cache:
        import jax
        import jax.numpy as jnp
        with jax.default_device(jax.devices("cpu")[0]):
            half = HD // 2
            inv_freq = 10000.0 ** (-jnp.arange(0, half, dtype=jnp.float32) / half)
            freqs = jnp.arange(S, dtype=jnp.float32)[:, None] * inv_freq[None, :]
            _prog_cache["tabs"] = (np.asarray(jnp.cos(freqs)), np.asarray(jnp.sin(freqs)))
    cos_full, sin_full = _prog_cache["tabs"]

    if "nc" not in _prog_cache:
        _prog_cache["nc"] = build_program()
    nc = _prog_cache["nc"]

    in_maps = []
    for c in range(8):
        b, h = c // 2, c % 2
        own = x[b, 2048 * h:2048 * h + 2048]
        halo = x[b, 2048 * h - 512:2048 * h] if h == 1 else np.zeros((512, 2048), np.float32)
        xin = np.concatenate([halo, own], axis=0)
        rows = np.clip(np.arange(2048 * h - 512, 2048 * h + 2048), 0, S - 1)
        cosTh = np.ascontiguousarray(
            cos_full[rows].reshape(NT_KEY, 128, 64).transpose(1, 0, 2).reshape(128, TW))
        sinTh = np.ascontiguousarray(
            sin_full[rows].reshape(NT_KEY, 128, 64).transpose(1, 0, 2).reshape(128, TW))
        flag = np.full(128, float(h), np.float32)
        in_maps.append(dict(xin=xin, wqT=wqT, wkT=wkT, wvT=wvT, woT=woT, wff=wff,
                            lnw=lnw, lnb=lnb, qgain=qgain, cosTh=cosTh, sinTh=sinTh,
                            flag=flag, m0=m0, m4=m4, idm=idm))
    res = run_bass_kernel_spmd(nc, in_maps, core_ids=list(range(8)))
    out = np.empty((B, S, D), res.results[0]["yout"].dtype)
    for c in range(8):
        b, h = c // 2, c % 2
        out[b, 2048 * h:2048 * h + 2048] = res.results[c]["yout"]
    return out


# version 12
# speedup vs baseline: 1.0432x; 1.0229x over previous
import math
import numpy as np
import concourse.bass as bass
import concourse.mybir as mybir
from concourse.bass_utils import run_bass_kernel_spmd
from concourse.alu_op_type import AluOpType as OP
from contextlib import ExitStack

F32 = mybir.dt.float32
BF16 = mybir.dt.bfloat16
I32 = mybir.dt.int32
AF = mybir.ActivationFunctionType
AX = mybir.AxisListType

DEBUG = False

D = 2048
H = 16
KVH = 4
HD = 128
NKEY = 2560
NT_KEY = 20
NT_OWN = 16
NG = 2
OWN_G = 8
KEY_G = 12
EPS_LN = 1e-5
EPS_RMS = 1.1920928955078125e-07
TW = NT_KEY * 64

ENGS = ("sync", "scalar", "vector", "tensor", "gpsimd")
DMA_ENGS = {"sync", "gpsimd"}
DMA_SLOTS = 6


class Prog:

    def __init__(self, nc, sems):
        self.nc = nc
        self.sems = sems
        self.ops = {e: [] for e in ENGS}
        self.cnt = {k: 0 for k in sems}
        for e in ENGS:
            self.cnt.setdefault(e, 0)
        self.wmax = {e: {} for e in ENGS}
        self.dirty = {e: False for e in ENGS}
        self.dma_i = {q: 0 for q in DMA_ENGS}

    def _waits(self, eng, waits):
        for tok in waits:
            if tok is None:
                continue
            pe, v = tok
            if self.wmax[eng].get(pe, 0) >= v:
                continue
            self.wmax[eng][pe] = v
            sem = self.sems[pe]
            self.ops[eng].append(lambda E, sem=sem, v=v: E.wait_ge(sem, v))

    def op(self, eng, fn, waits=(), inc=False):
        self._waits(eng, waits)
        if inc:
            self.cnt[eng] += 1
            sem = self.sems[eng]
            self.ops[eng].append(lambda E, fn=fn, sem=sem: fn(E).then_inc(sem, 1))
            self.dirty[eng] = False
            return (eng, self.cnt[eng])
        self.ops[eng].append(fn)
        self.dirty[eng] = True
        return None

    def dma(self, eng, out, in_, waits=(), slow=False):
        i = self.dma_i[eng]
        self.dma_i[eng] = i + 1
        slot, cyc = i % DMA_SLOTS, i // DMA_SLOTS
        key = f"{eng}{slot}"
        w = list(waits)
        if cyc > 0:
            w.append((key, 16 * cyc))
        self._waits(eng, w)
        self.cnt[key] = 16 * (cyc + 1)
        sem = self.sems[key]
        self.ops[eng].append(lambda E, o=out, i_=in_, sem=sem, slow=slow: E.dma_start(
            out=o, in_=i_, allow_slow_non_contiguous=slow).then_inc(sem, 16))
        return (key, 16 * (cyc + 1))

    def tok(self, eng):
        assert not self.dirty[eng], f"{eng} has uncounted trailing ops"
        return (eng, self.cnt[eng])

    def barrier(self):
        for e in ENGS:
            assert not self.dirty[e], f"barrier with uncounted ops on {e}"
        toks = [(k, v) for k, v in self.cnt.items() if v > 0]
        for e in ENGS:
            self._waits(e, toks)

    def emit(self, block):
        for eng in ENGS:
            def section(E, eng=eng):
                for f in self.ops[eng]:
                    f(E)
            getattr(block, eng)(section)


def build_program():
    nc = bass.Bass("TRN2", target_bir_lowering=False,
                   detect_race_conditions=False)
    d = {}
    def din(name, shape, dtype=F32):
        d[name] = nc.dram_tensor(name, shape, dtype, kind="ExternalInput").ap()
        return d[name]
    def dint(name, shape, dtype):
        d[name] = nc.dram_tensor(name, shape, dtype, kind="Internal").ap()
        return d[name]

    xin = din("xin", [NKEY, D])
    wqT = din("wqT", [D, D])
    wkT = din("wkT", [D, KVH * HD])
    wvT = din("wvT", [D, KVH * HD])
    woT = din("woT", [D, D])
    wff = din("wff", [D, D])
    lnw_h = din("lnw", [128, 16])
    lnb_h = din("lnb", [128, 16])
    qgain_h = din("qgain", [H])
    cosT_h = din("cosTh", [128, TW])
    sinT_h = din("sinTh", [128, TW])
    flag_h = din("flag", [128])
    m0_h = din("m0", [128, 128])
    m4_h = din("m4", [128, 128])
    idm_h = din("idm", [128, 128])
    yout = nc.dram_tensor("yout", [NT_OWN * 128, D], F32, kind="ExternalOutput").ap()

    s_wq = dint("s_wq", [D, D], BF16)
    s_wo = dint("s_wo", [D, D], BF16)
    s_weff = dint("s_weff", [D, D], BF16)

    dbg = {}
    if DEBUG:
        for nm, shp, dt in [("dbg_small", [128, 64], F32), ("dbg_xn", [128, D], BF16),
                            ("dbg_kbf", [128, 512], BF16), ("dbg_vtok", [128, 512], BF16),
                            ("dbg_qbf", [128, D], BF16), ("dbg_pT", [128, 640], BF16),
                            ("dbg_y", [128, 128], BF16), ("dbg_xn2a", [128, D], BF16),
                            ("dbg_cbf", [1, D], BF16), ("dbg_invf", [128, 64], F32),
                            ("dbg_weff", [128, D], BF16), ("dbg_costile", [128, 64], BF16),
                            ("dbg_sintile", [128, 64], BF16), ("dbg_xnfm", [128, 128], BF16),
                            ("dbg_lns", [128, 16], F32), ("dbg_qfm", [128, 2048], BF16)]:
            dbg[nm] = nc.dram_tensor(nm, shp, dt, kind="ExternalOutput").ap()

    sems = {}
    stack = ExitStack()
    for e in ("scalar", "vector", "tensor"):
        sems[e] = stack.enter_context(nc.semaphore(f"sem_{e}"))
    for q in DMA_ENGS:
        for i in range(DMA_SLOTS):
            sems[f"{q}{i}"] = stack.enter_context(nc.semaphore(f"sem_{q}{i}"))
    P = Prog(nc, sems)
    E = stack.enter_context

    cst = E(nc.sbuf_tensor("cst", [128, 4], F32))
    sc128 = E(nc.sbuf_tensor("sc128", [128, 1], F32))
    negBmax = E(nc.sbuf_tensor("negBmax", [128, 1], F32))
    negB = E(nc.sbuf_tensor("negB", [128, H], F32))
    grep = E(nc.sbuf_tensor("grep", [128, H], F32))
    flag_sb = E(nc.sbuf_tensor("flag_sb", [128, 1], F32))
    m0_bf = E(nc.sbuf_tensor("m0_bf", [128, 128], BF16))
    m0f_bf = E(nc.sbuf_tensor("m0f_bf", [128, 128], BF16))
    m4_bf = E(nc.sbuf_tensor("m4_bf", [128, 128], BF16))
    idm_bf = E(nc.sbuf_tensor("idm_bf", [128, 128], BF16))
    ones_r_bf = E(nc.sbuf_tensor("ones_r_bf", [1, 128], BF16))
    ones_r_f = E(nc.sbuf_tensor("ones_r_f", [1, 128], F32))
    ones_c_f = E(nc.sbuf_tensor("ones_c_f", [128, 1], F32))
    ones_c_bf = E(nc.sbuf_tensor("ones_c_bf", [128, 1], BF16))
    c_bf = E(nc.sbuf_tensor("c_bf", [1, D], BF16))
    cosT = E(nc.sbuf_tensor("cosT", [128, TW], BF16))
    sinT = E(nc.sbuf_tensor("sinT", [128, TW], BF16))

    with ExitStack() as p0:
        e0 = p0.enter_context
        sm1 = e0(nc.sbuf_tensor("sm1", [1, 128], F32))
        sm2 = e0(nc.sbuf_tensor("sm2", [1, 128], F32))
        sm3 = e0(nc.sbuf_tensor("sm3", [1, 128], F32))
        mload = e0(nc.sbuf_tensor("mload", [128, 3 * 128], F32))
        tbl = [e0(nc.sbuf_tensor(f"tbl{i}", [128, TW], F32)) for i in range(2)]
        ps_rep = e0(nc.psum_tensor("ps_rep", [128, 64], F32))

        def memsets(V):
            V.memset(cst[:, 0:1], EPS_LN)
            V.memset(cst[:, 1:2], EPS_RMS)
            V.memset(cst[:, 2:3], 128.0)
            V.memset(cst[:, 3:4], 10000.0)
            V.memset(ones_r_bf[:], 1.0)
            V.memset(ones_r_f[:], 1.0)
            V.memset(ones_c_f[:], 1.0)
            return V.memset(ones_c_bf[:], 1.0)
        P.op("vector", memsets, inc=True)
        t_con = P.tok("vector")

        tks = []
        t_ctl = P.dma("sync", tbl[0][:], cosT_h)
        t_stl = P.dma("sync", tbl[1][:], sinT_h)
        tks.append(P.op("vector", lambda V: V.tensor_copy(cosT[:], tbl[0][:]),
                        waits=[t_ctl], inc=True))
        tks.append(P.op("scalar", lambda S: S.copy(sinT[:], tbl[1][:]),
                        waits=[t_stl], inc=True))
        tks.append(P.dma("sync", sm2[0:1, 0:16], qgain_h.unsqueeze(0)))
        tks.append(P.dma("sync", flag_sb[:, 0:1], flag_h.unsqueeze(1)))
        tks.append(P.dma("sync", mload[:, 0:128], m0_h))
        tks.append(P.dma("sync", mload[:, 128:256], m4_h))
        tks.append(P.dma("sync", mload[:, 256:384], idm_h))

        def casts0(V):
            V.tensor_copy(m0_bf[:], mload[:, 0:128])
            V.tensor_copy(m4_bf[:], mload[:, 128:256])
            V.tensor_copy(idm_bf[:], mload[:, 256:384])
            return V.tensor_scalar_mul(m0f_bf[:], mload[:, 0:128], flag_sb[:, 0:1])
        P.op("vector", casts0, waits=tks + [t_con], inc=True)

        def sconsts(S):
            S.activation(sm3[0:1, 0:1], cst[0:1, 2:3], AF.Sqrt)
            return S.activation(sm3[0:1, 16:32], sm2[0:1, 0:16], AF.Abs)
        P.op("scalar", sconsts, waits=[t_con] + tks, inc=True)
        t_sc0 = P.tok("scalar")
        P.op("vector", lambda V: V.tensor_scalar_mul(sm1[0:1, 16:32], sm3[0:1, 16:32],
                                                     sm3[0:1, 0:1]), waits=[t_sc0],
             inc=True)
        t_v1a = P.tok("vector")
        def v1b(V):
            V.tensor_scalar_mul(sm1[0:1, 16:32], sm1[0:1, 16:32], -1.0)
            V.tensor_reduce(sm1[0:1, 40:41], sm1[0:1, 16:32], AX.X, OP.min)
            return V.reciprocal(sm1[0:1, 0:1], sm3[0:1, 0:1])
        P.op("vector", v1b, waits=[t_v1a], inc=True)
        t_v1 = P.tok("vector")
        def repl2(T):
            T.matmul(ps_rep[:, 0:16], ones_r_f[:], sm1[0:1, 16:32], start=True, stop=True)
            T.matmul(ps_rep[:, 16:32], ones_r_f[:], sm2[0:1, 0:16], start=True, stop=True)
            T.matmul(ps_rep[:, 33:34], ones_r_f[:], sm1[0:1, 40:41],
                     start=True, stop=True)
            return T.matmul(ps_rep[:, 32:33], ones_r_f[:], sm1[0:1, 0:1],
                            start=True, stop=True)
        P.op("tensor", repl2, waits=[t_v1], inc=True)
        t_mm1 = P.tok("tensor")
        def cop2(V):
            V.tensor_copy(negB[:], ps_rep[:, 0:16])
            V.tensor_copy(grep[:], ps_rep[:, 16:32])
            V.tensor_copy(negBmax[:], ps_rep[:, 33:34])
            return V.tensor_copy(sc128[:], ps_rep[:, 32:33])
        P.op("vector", cop2, waits=[t_mm1], inc=True)
        t_grep = P.tok("vector")
        if DEBUG:
            P.dma("gpsimd", dbg["dbg_small"][:, 0:1], sc128[:], slow=True)
            P.dma("gpsimd", dbg["dbg_small"][:, 1:17], negB[:], slow=True)
            P.dma("gpsimd", dbg["dbg_small"][:, 17:33], grep[:], slow=True)
            P.dma("gpsimd", dbg["dbg_small"][:, 33:37], cst[:], slow=True)
            P.dma("gpsimd", dbg["dbg_small"][:, 38:39], flag_sb[:], slow=True)
        P.barrier()

    with ExitStack() as pg:
        eg = pg.enter_context
        y_g = eg(nc.sbuf_tensor("y_g", [128, OWN_G * H * 128], BF16))

        def emit_group(g):
            p12 = ExitStack()
            xnfm_g = p12.enter_context(
                nc.sbuf_tensor(f"xnfm_g{g}", [128, OWN_G * 16 * 128], BF16))
            kfm_g = p12.enter_context(
                nc.sbuf_tensor(f"kfm_g{g}", [128, KVH * KEY_G * 128], BF16))
            vtok_g = p12.enter_context(
                nc.sbuf_tensor(f"vtok_g{g}", [128, KEY_G * 516], BF16))

            with ExitStack() as p1:
                e1 = p1.enter_context
                wk_bf = e1(nc.sbuf_tensor(f"wk_bf_g{g}", [128, 16 * 512], BF16))
                wv_bf = e1(nc.sbuf_tensor(f"wv_bf_g{g}", [128, 16 * 512], BF16))
                xld = [e1(nc.sbuf_tensor(f"xld{i}_g{g}", [128, D], F32)) for i in range(2)]
                xn_bf = e1(nc.sbuf_tensor(f"xn_bf_g{g}", [128, 2 * D], BF16))
                sq_scr = e1(nc.sbuf_tensor(f"sq_scr_g{g}", [128, D], BF16))
                xnfm_t = e1(nc.sbuf_tensor(f"xnfm_t_g{g}", [128, 16 * 128], BF16))
                k_bf = e1(nc.sbuf_tensor(f"k_bf_g{g}", [128, 512], BF16))
                sA = e1(nc.sbuf_tensor(f"sA_g{g}", [128, 64], F32))
                sB = e1(nc.sbuf_tensor(f"sB_g{g}", [128, 64], F32))
                sC = e1(nc.sbuf_tensor(f"sC_g{g}", [128, 64], F32))
                sD = e1(nc.sbuf_tensor(f"sD_g{g}", [128, 64], F32))
                lns = e1(nc.sbuf_tensor(f"lns_g{g}", [128, 32], F32))
                ps_k = e1(nc.psum_tensor(f"ps_k_g{g}", [128, 512], F32))
                ps_v = e1(nc.psum_tensor(f"ps_v_g{g}", [128, 512], F32))
                ps_tp = [e1(nc.psum_tensor(f"ps_tp{i}_g{g}", [128, 128], BF16))
                         for i in range(2)]

                kstg = [e1(nc.sbuf_tensor(f"kstg{i}_g{g}", [128, 512], F32))
                        for i in range(3)]
                t_wkv = []
                kprev = [None] * 3

                def emit_wkv():
                    for c in range(16):
                        for idx, (srcT, dstB) in enumerate(((wkT, wk_bf), (wvT, wv_bf))):
                            sl = (2 * c + idx) % 3
                            t_ld = P.dma("gpsimd", kstg[sl][:], srcT[128 * c:128 * c + 128, :],
                                         waits=[kprev[sl]] if kprev[sl] else [])
                            if (2 * c + idx) % 2 == 0:
                                t_cv = P.op("scalar", lambda S, sl=sl, c=c, dstB=dstB: S.copy(
                                    dstB[:, 512 * c:512 * c + 512], kstg[sl][:]),
                                    waits=[t_ld], inc=True)
                            else:
                                t_cv = P.op("vector", lambda V, sl=sl, c=c, dstB=dstB:
                                            V.tensor_copy(dstB[:, 512 * c:512 * c + 512],
                                                          kstg[sl][:]), waits=[t_ld], inc=True)
                            kprev[sl] = t_cv
                            t_wkv.append(t_cv)

                items = []
                if g == 0:
                    wstg = [e1(nc.sbuf_tensor(f"wstg{i}", [128, D], F32)) for i in range(2)]
                    t_scrP = e1(nc.sbuf_tensor("t_scrP", [128, D], F32))
                    tern_bf = [e1(nc.sbuf_tensor(f"tern_bf{i}", [128, D], BF16))
                               for i in range(2)]
                    weff_bf0 = e1(nc.sbuf_tensor("weff_bf0", [128, D], BF16))
                    lnw_sb = e1(nc.sbuf_tensor("lnw_sb", [128, 16], F32))
                    lnb_sb = e1(nc.sbuf_tensor("lnb_sb", [128, 16], F32))
                    lnb_bf = e1(nc.sbuf_tensor("lnb_bf", [128, 16], BF16))
                    asum = e1(nc.sbuf_tensor("asum", [128, 20], F32))
                    ps_c = e1(nc.psum_tensor("ps_c", [128, D], F32))
                    tk_lw = P.dma("sync", lnw_sb[:], lnw_h)
                    tk_lb = P.dma("sync", lnb_sb[:], lnb_h)
                    t_lnbbf = P.op("vector", lambda V: V.tensor_copy(lnb_bf[:], lnb_sb[:]),
                                   waits=[tk_lb], inc=True)
                    pst = {"wi": 0, "ci": 0, "wfree": [None, None],
                           "cfree": [None, None], "absred": [], "t_scale": None,
                           "t_tern_mm": None, "t_weff_st": None,
                           "mm_pend": None, "mm_tok": [None, None]}

                    def abs_item(c):
                        def run():
                            sl = pst["wi"] % 2
                            pst["wi"] += 1
                            t_ld = P.dma("gpsimd", wstg[sl][:], wff[128 * c:128 * c + 128, :],
                                         waits=[pst["wfree"][sl]] if pst["wfree"][sl] else [])
                            t_r = P.op("vector", lambda V, sl=sl, c=c: V.tensor_reduce(
                                asum[:, c:c + 1], wstg[sl][:], AX.X, OP.add,
                                apply_absolute_value=True), waits=[t_ld], inc=True)
                            pst["wfree"][sl] = t_r
                            pst["absred"].append(t_r)
                        return run

                    def scale_item():
                        def run():
                            t1 = P.op("vector", lambda V: V.tensor_reduce(
                                asum[:, 16:17], asum[:, 0:16], AX.X, OP.add),
                                waits=pst["absred"], inc=True)
                            t2 = P.op("tensor", lambda T: T.matmul(
                                ps_c[0:1, 0:1], asum[:, 16:17], ones_c_f[:],
                                start=True, stop=True, skip_group_check=True),
                                waits=[t1], inc=True)
                            t3 = P.op("vector", lambda V: V.reciprocal(
                                asum[0:1, 17:18], ps_c[0:1, 0:1]), waits=[t2], inc=True)
                            t4 = P.op("vector", lambda V: V.tensor_scalar_mul(
                                asum[0:1, 18:19], asum[0:1, 17:18], float(D * D)),
                                waits=[t3], inc=True)
                            t5 = P.op("tensor", lambda T: T.matmul(
                                ps_c[:, 4:5], ones_r_f[:], asum[0:1, 18:19],
                                start=True, stop=True, skip_group_check=True),
                                waits=[t4], inc=True)
                            pst["t_scale"] = P.op("vector", lambda V: V.tensor_copy(
                                asum[:, 19:20], ps_c[:, 4:5]), waits=[t5], inc=True)
                        return run

                    def flush_tern_mm():
                        if pst["mm_pend"] is None:
                            return
                        c, par, t_t = pst["mm_pend"]
                        pst["mm_pend"] = None
                        for s5 in range(4):
                            P.op("tensor", lambda T, c=c, s5=s5, par=par: T.matmul(
                                ps_c[0:1, 512 * s5:512 * s5 + 512], lnb_bf[:, c:c + 1],
                                tern_bf[par][:, 512 * s5:512 * s5 + 512],
                                start=(c == 0), stop=(c == 15), skip_group_check=True),
                                waits=[t_t, t_lnbbf] if s5 == 0 else [], inc=(s5 == 3))
                        pst["t_tern_mm"] = pst["mm_tok"][par] = P.tok("tensor")

                    def tern_item(c):
                        def run():
                            sl = pst["wi"] % 2
                            par = c % 2
                            pst["wi"] += 1
                            t_ld = P.dma("gpsimd", wstg[sl][:], wff[128 * c:128 * c + 128, :],
                                         waits=[pst["wfree"][sl]] if pst["wfree"][sl] else [])
                            aw = [t_ld, pst["t_scale"]]
                            if pst.get("prev_tt"):
                                aw.append(pst["prev_tt"])
                            t_a = P.op("scalar", lambda S, sl=sl: S.activation(
                                t_scrP[:], wstg[sl][:], AF.Identity,
                                scale=asum[:, 19:20]),
                                waits=aw, inc=True)
                            tw = [t_a]
                            if pst["mm_tok"][par]:
                                tw.append(pst["mm_tok"][par])
                            def tern_chunk(V, sl=sl, par=par):
                                V.tensor_single_scalar(wstg[sl][:], t_scrP[:], -0.5, OP.is_lt)
                                return V.scalar_tensor_tensor(tern_bf[par][:], t_scrP[:], 0.5,
                                                              wstg[sl][:],
                                                              OP.is_gt, OP.subtract)
                            t_t = P.op("vector", lambda V, f=tern_chunk: f(V),
                                       waits=tw, inc=True)
                            pst["wfree"][sl] = t_t
                            pst["prev_tt"] = t_t
                            ww = [t_t, tk_lw]
                            if pst["t_weff_st"]:
                                ww.append(pst["t_weff_st"])
                            t_w = P.op("scalar", lambda S, c=c, par=par: S.activation(
                                weff_bf0[:], tern_bf[par][:], AF.Identity,
                                scale=lnw_sb[:, c:c + 1]), waits=ww, inc=True)
                            pst["t_weff_st"] = P.dma("gpsimd",
                                                     s_weff[128 * c:128 * c + 128, :],
                                                     weff_bf0[:], waits=[t_w])
                            flush_tern_mm()
                            pst["mm_pend"] = (c, par, t_t)
                        return run

                    def cfin_item():
                        def run():
                            flush_tern_mm()
                            P.op("vector", lambda V: V.tensor_copy(c_bf[0:1, :], ps_c[0:1, :]),
                                 waits=[pst["t_tern_mm"], pst["t_weff_st"]], inc=True)
                        return run

                    for c in range(16):
                        items.append(abs_item(c))
                    items.append(scale_item())
                    for c in range(16):
                        items.append(tern_item(c))
                    items.append(cfin_item())

                t_xfree = [None, None]
                t_xld = [None, None]
                t_tp_ev = [None, None]
                t_xtp_last = [None, None]
                t_kepi = None
                t_vevac = None
                P.op("vector", lambda V: V.memset(vtok_g[:], 1.0), inc=True)
                t_xn_arr = [None, None]

                def ln_emit(s):
                    kt = 8 * g + s
                    b = s % 2
                    lb = 16 * b
                    xb = D * b
                    t_xld[b] = P.dma("sync", xld[b][:], xin[128 * kt:128 * kt + 128, :],
                                     waits=[t_xfree[b]] if t_xfree[b] else [])
                    t_s1 = P.op("vector", lambda V, b=b, lb=lb: V.tensor_reduce(
                        lns[:, lb + 0:lb + 1], xld[b][:], AX.X, OP.add), waits=[t_xld[b]],
                        inc=True)
                    t_s2 = P.op("scalar", lambda S, b=b, lb=lb: S.activation(
                        sq_scr[:], xld[b][:], AF.Square, accum_out=lns[:, lb + 1:lb + 2]),
                        waits=[t_xld[b]], inc=True)
                    def ln_a(V, lb=lb):
                        V.tensor_scalar_mul(lns[:, lb + 2:lb + 3], lns[:, lb + 0:lb + 1], 1.0 / D)
                        return V.scalar_tensor_tensor(lns[:, lb + 3:lb + 4], lns[:, lb + 0:lb + 1],
                                                      1.0 / (D * D),
                                                      lns[:, lb + 0:lb + 1], OP.mult, OP.mult)
                    t_lna = P.op("vector", ln_a, waits=[t_s1], inc=True)
                    t_var = P.op("vector", lambda V, lb=lb: V.scalar_tensor_tensor(
                        lns[:, lb + 4:lb + 5], lns[:, lb + 1:lb + 2], 1.0 / D, lns[:, lb + 3:lb + 4],
                        OP.mult, OP.subtract), waits=[t_lna, t_s2], inc=True)
                    t_std = P.op("scalar", lambda S, lb=lb: S.activation(
                        lns[:, lb + 5:lb + 6], lns[:, lb + 4:lb + 5], AF.Sqrt, bias=cst[:, 0:1]),
                        waits=[t_var], inc=True)
                    t_rstd = P.op("vector", lambda V, lb=lb: V.reciprocal(
                        lns[:, lb + 6:lb + 7], lns[:, lb + 5:lb + 6]), waits=[t_std], inc=True)
                    t_nb = P.op("vector", lambda V, lb=lb: V.scalar_tensor_tensor(
                        lns[:, lb + 7:lb + 8], lns[:, lb + 2:lb + 3], -1.0, lns[:, lb + 6:lb + 7],
                        OP.mult, OP.mult), waits=[t_rstd], inc=True)
                    t_xn = P.op("scalar", lambda S, b=b, lb=lb, xb=xb: S.activation(
                        xn_bf[:, xb:xb + D], xld[b][:], AF.Identity, bias=lns[:, lb + 7:lb + 8],
                        scale=lns[:, lb + 6:lb + 7]),
                        waits=[t_nb] + ([t_xtp_last[b]] if t_xtp_last[b] else []), inc=True)
                    t_xfree[b] = t_xn
                    t_xn_arr[b] = t_xn

                ln_emit(0)
                emit_wkv()
                for s in range(KEY_G):
                    kt = 8 * g + s
                    b = s % 2
                    lb = 16 * b
                    xb = D * b
                    own = kt >= 4 and (kt - 4) // 8 == g
                    ot_local = (kt - 4) - 8 * g if own else None
                    if s + 1 < KEY_G:
                        ln_emit(s + 1)
                    t_xn = t_xn_arr[b]
                    dst = xnfm_g if own else xnfm_t
                    base = (ot_local * 16 * 128) if own else 0
                    for c in range(16):
                        i = c % 2
                        w = [t_xn]
                        if t_tp_ev[i]:
                            w.append(t_tp_ev[i])
                        tk = P.op("tensor", lambda T, c=c, i=i, xb=xb: T.transpose(
                            ps_tp[i][:], xn_bf[:, xb + 128 * c:xb + 128 * c + 128], idm_bf[:]),
                            waits=w, inc=True)
                        if c == 15:
                            t_xtp_last[b] = tk
                        if i == 0:
                            t_tp_ev[i] = P.op("scalar", lambda S, c=c, i=i, dst=dst, base=base:
                                              S.copy(dst[:, base + 128 * c:base + 128 * c + 128],
                                                     ps_tp[i][:]), waits=[tk], inc=True)
                        else:
                            t_tp_ev[i] = P.op("vector", lambda V, c=c, i=i, dst=dst, base=base:
                                              V.tensor_copy(dst[:, base + 128 * c:base + 128 * c + 128],
                                                            ps_tp[i][:]), waits=[tk], inc=True)
                    kv_waits = [t_tp_ev[0], t_tp_ev[1]]
                    if t_kepi:
                        kv_waits.append(t_kepi)
                    for c in range(16):
                        P.op("tensor", lambda T, c=c, dst=dst, base=base: T.matmul(
                            ps_k[:], dst[:, base + 128 * c:base + 128 * c + 128],
                            wk_bf[:, 512 * c:512 * c + 512],
                            start=(c == 0), stop=(c == 15), skip_group_check=True),
                            waits=(kv_waits if c == 0 else []) + [t_wkv[2 * c]],
                            inc=(c == 15))
                    t_kmm = P.tok("tensor")
                    for c in range(16):
                        P.op("tensor", lambda T, c=c, dst=dst, base=base: T.matmul(
                            ps_v[:], dst[:, base + 128 * c:base + 128 * c + 128],
                            wv_bf[:, 512 * c:512 * c + 512],
                            start=(c == 0), stop=(c == 15), skip_group_check=True),
                            waits=([t_vevac] if (c == 0 and t_vevac) else []) + [t_wkv[2 * c + 1]],
                            inc=(c == 15))
                    t_vmm = P.tok("tensor")
                    def v_evac(V, s=s):
                        for hk in range(KVH - 1):
                            V.tensor_copy(vtok_g[:, 516 * s + 129 * hk:516 * s + 129 * hk + 128],
                                          ps_v[:, 128 * hk:128 * hk + 128])
                        return V.tensor_copy(
                            vtok_g[:, 516 * s + 129 * 3:516 * s + 129 * 3 + 128],
                            ps_v[:, 384:512])
                    t_vevac = P.op("vector", v_evac, waits=[t_vmm], inc=True)
                    def k_sq(S, lb=lb):
                        r = None
                        for hk in range(KVH):
                            r = S.activation(sq_scr[:, 0:128],
                                             ps_k[:, 128 * hk:128 * hk + 128],
                                             AF.Square, accum_out=lns[:, lb + 8 + hk:lb + 9 + hk])
                        return r
                    t_ksq = P.op("scalar", k_sq, waits=[t_kmm], inc=True)
                    t_kstd = P.op("scalar", lambda S, lb=lb: S.activation(
                        lns[:, lb + 8:lb + 12], lns[:, lb + 8:lb + 12], AF.Sqrt, bias=cst[:, 1:2],
                        scale=1.0 / HD), waits=[t_ksq], inc=True)
                    t_krs = P.op("vector", lambda V, lb=lb: V.reciprocal(
                        lns[:, lb + 8:lb + 12], lns[:, lb + 8:lb + 12]), waits=[t_kstd], inc=True)
                    t_krs2 = P.op("scalar", lambda S, lb=lb: S.copy(
                        lns[:, lb + 12:lb + 16], lns[:, lb + 8:lb + 12]),
                        waits=[t_krs], inc=True)
                    ctile = slice(64 * kt, 64 * kt + 64)
                    for hk in range(KVH):
                        s1 = slice(128 * hk, 128 * hk + 64)
                        s2 = slice(128 * hk + 64, 128 * hk + 128)
                        def k_rope_m(V, hk=hk, s1=s1, s2=s2, ctile=ctile, lb=lb):
                            rs = lns[:, lb + 12 + hk:lb + 13 + hk]
                            V.scalar_tensor_tensor(sA[:], ps_k[:, s1], rs, cosT[:, ctile],
                                                   OP.mult, OP.mult)
                            V.scalar_tensor_tensor(sB[:], ps_k[:, s2], rs, sinT[:, ctile],
                                                   OP.mult, OP.mult)
                            V.scalar_tensor_tensor(sC[:], ps_k[:, s1], rs, sinT[:, ctile],
                                                   OP.mult, OP.mult)
                            return V.scalar_tensor_tensor(sD[:], ps_k[:, s2], rs,
                                                          cosT[:, ctile], OP.mult, OP.mult)
                        t_km = P.op("vector", lambda V, f=k_rope_m: f(V),
                                    waits=[t_krs2] if hk == 0 else [], inc=True)
                        def k_rope_a(V, s1=s1, s2=s2):
                            V.tensor_add(k_bf[:, s1], sA[:], sB[:])
                            return V.tensor_sub(k_bf[:, s2], sD[:], sC[:])
                        P.op("vector", lambda V, f=k_rope_a: f(V), waits=[t_km],
                             inc=(hk == KVH - 1))
                    t_kepi = P.tok("vector")
                    if DEBUG and g == 0 and kt == 4:
                        P.dma("gpsimd", dbg["dbg_xn"][:], xn_bf[:], waits=[t_xn], slow=True)
                        P.dma("gpsimd", dbg["dbg_kbf"][:], k_bf[:], waits=[t_kepi], slow=True)
                        P.dma("gpsimd", dbg["dbg_vtok"][:], vtok_g[:, 512 * s:512 * s + 512],
                              waits=[t_vevac], slow=True)
                        P.dma("gpsimd", dbg["dbg_xnfm"][:], xnfm_g[:, 0:128],
                              waits=[t_tp_ev[0], t_tp_ev[1]], slow=True)
                        P.dma("gpsimd", dbg["dbg_lns"][:], lns[:], waits=[t_kepi], slow=True)
                    for hk in range(KVH):
                        i = hk % 2
                        tk = P.op("tensor", lambda T, hk=hk, i=i: T.transpose(
                            ps_tp[i][:], k_bf[:, 128 * hk:128 * hk + 128], idm_bf[:]),
                            waits=[t_kepi, t_tp_ev[i]], inc=True)
                        if i == 0:
                            t_tp_ev[i] = P.op("scalar", lambda S, hk=hk, i=i, s=s: S.copy(
                                kfm_g[:, (hk * KEY_G + s) * 128:(hk * KEY_G + s) * 128 + 128],
                                ps_tp[i][:]), waits=[tk], inc=True)
                        else:
                            t_tp_ev[i] = P.op("vector", lambda V, hk=hk, i=i, s=s: V.tensor_copy(
                                kfm_g[:, (hk * KEY_G + s) * 128:(hk * KEY_G + s) * 128 + 128],
                                ps_tp[i][:]), waits=[tk], inc=True)
                    lo = (len(items) * s) // KEY_G
                    hi = (len(items) * (s + 1)) // KEY_G
                    for it in items[lo:hi]:
                        it()
                P.barrier()

            with ExitStack() as p2:
                e2 = p2.enter_context
                wq_bf = e2(nc.sbuf_tensor(f"wq_bf_g{g}", [128, 16 * D], BF16))
                q_bf = e2(nc.sbuf_tensor(f"q_bf_g{g}", [128, D], BF16))
                qfm = e2(nc.sbuf_tensor(f"qfm_g{g}", [128, 16 * 128], BF16))
                pT4 = [[e2(nc.sbuf_tensor(f"pT4_{p}_{i}_g{g}", [128, 512], BF16))
                        for i in range(5)] for p in range(2)]
                qs = e2(nc.sbuf_tensor(f"qs_g{g}", [128, 48], F32))
                sA2 = e2(nc.sbuf_tensor(f"sA2_g{g}", [128, 64], F32))
                sB2 = e2(nc.sbuf_tensor(f"sB2_g{g}", [128, 64], F32))
                sC2 = e2(nc.sbuf_tensor(f"sC2_g{g}", [128, 64], F32))
                sD2 = e2(nc.sbuf_tensor(f"sD2_g{g}", [128, 64], F32))
                sq2 = e2(nc.sbuf_tensor(f"sq2_g{g}", [128, 128], BF16))
                rsum = e2(nc.sbuf_tensor(f"rsum_g{g}", [128, 2], F32))
                ps_q = [e2(nc.psum_tensor(f"ps_q{i}_g{g}", [128, 512], F32))
                        for i in range(2)]
                ps_tp2 = [e2(nc.psum_tensor(f"ps_tp2{i}_g{g}", [128, 128], BF16))
                          for i in range(2)]
                ps_s = [e2(nc.psum_tensor(f"ps_s{i}_g{g}", [128, 512], F32))
                        for i in range(2)]
                ps_att = [e2(nc.psum_tensor(f"ps_att{i}_g{g}", [128, 129], F32))
                          for i in range(2)]

                t_wq = []
                t_wqc = {}
                items2 = []
                if g == 0:
                    wstg2 = [e2(nc.sbuf_tensor(f"wstg2{i}", [128, D], F32))
                             for i in range(2)]
                    cws2 = [e2(nc.sbuf_tensor(f"cws2{i}", [128, D], BF16))
                            for i in range(2)]
                    pst2 = {"wi": 0, "ci": 0, "wfree": [None] * 2,
                            "cfree": [None, None]}

                    def wqcast_item(c):
                        def run():
                            sl = pst2["wi"] % 2
                            pst2["wi"] += 1
                            t_ld = P.dma("sync", wstg2[sl][:], wqT[128 * c:128 * c + 128, :],
                                         waits=[pst2["wfree"][sl]] if pst2["wfree"][sl] else [])
                            if c % 2 == 0:
                                t_cv = P.op("scalar", lambda S, sl=sl, c=c: S.copy(
                                    wq_bf[:, D * c:D * c + D], wstg2[sl][:]),
                                    waits=[t_ld], inc=True)
                            else:
                                t_cv = P.op("vector", lambda V, sl=sl, c=c: V.tensor_copy(
                                    wq_bf[:, D * c:D * c + D], wstg2[sl][:]),
                                    waits=[t_ld], inc=True)
                            pst2["wfree"][sl] = t_cv
                            t_wqc[c] = t_cv
                            P.dma("sync", s_wq[128 * c:128 * c + 128, :],
                                  wq_bf[:, D * c:D * c + D], waits=[t_cv])
                        return run

                    def wocast_item(c):
                        def run():
                            sl = pst2["wi"] % 2
                            ob = pst2["ci"] % 2
                            pst2["wi"] += 1
                            pst2["ci"] += 1
                            t_ld = P.dma("gpsimd", wstg2[sl][:], woT[128 * c:128 * c + 128, :],
                                         waits=[pst2["wfree"][sl]] if pst2["wfree"][sl] else [])
                            cw = [t_ld]
                            if pst2["cfree"][ob]:
                                cw.append(pst2["cfree"][ob])
                            if c % 2 == 0:
                                t_cv = P.op("scalar", lambda S, sl=sl, ob=ob: S.copy(
                                    cws2[ob][:], wstg2[sl][:]), waits=cw, inc=True)
                            else:
                                t_cv = P.op("vector", lambda V, sl=sl, ob=ob: V.tensor_copy(
                                    cws2[ob][:], wstg2[sl][:]), waits=cw, inc=True)
                            pst2["wfree"][sl] = t_cv
                            pst2["cfree"][ob] = P.dma("gpsimd",
                                                      s_wo[128 * c:128 * c + 128, :],
                                                      cws2[ob][:], waits=[t_cv])
                        return run

                    items2 = [wocast_item(c) for c in range(16)]
                    for c in range(16):
                        wqcast_item(c)()
                else:
                    for c in range(16):
                        t_wq.append(P.dma("sync", wq_bf[:, D * c:D * c + D],
                                          s_wq[128 * c:128 * c + 128, :]))

                t_q_ev = [None, None]
                t_tp_ev2 = [None, None]
                t_expb = [None, None]
                t_yevb = [None, None]
                t_pfree = [None, None]
                mk_d = [None, None]
                sct = [0]
                for qt in range(OWN_G):
                    ot = 8 * g + qt
                    kt0 = qt
                    base = qt * 16 * 128
                    ctile = slice(64 * (ot + 4), 64 * (ot + 4) + 64)
                    if items2:
                        lo = (len(items2) * qt) // OWN_G
                        hi = (len(items2) * (qt + 1)) // OWN_G
                        for it in items2[lo:hi]:
                            it()
                    for s4 in range(4):
                        bq = s4 % 2
                        w0 = []
                        if t_q_ev[bq]:
                            w0.append(t_q_ev[bq])
                        for c in range(16):
                            cw = list(w0) if c == 0 else []
                            if qt == 0:
                                cw.append(t_wqc[c] if g == 0 else t_wq[c])
                            P.op("tensor", lambda T, c=c, s4=s4, bq=bq, base=base: T.matmul(
                                ps_q[bq][:], xnfm_g[:, base + 128 * c:base + 128 * c + 128],
                                wq_bf[:, D * c + 512 * s4:D * c + 512 * s4 + 512],
                                start=(c == 0), stop=(c == 15), skip_group_check=True),
                                waits=cw, inc=(c == 15))
                        t_qmm = P.tok("tensor")
                        def q_sq(S, s4=s4, bq=bq):
                            r = None
                            for j in range(4):
                                r = S.activation(sq2[:], ps_q[bq][:, 128 * j:128 * j + 128],
                                                 AF.Square,
                                                 accum_out=qs[:, 4 * s4 + j:4 * s4 + j + 1])
                            return r
                        t_qsq = P.op("scalar", q_sq, waits=[t_qmm], inc=True)
                        t_qstd = P.op("scalar", lambda S, s4=s4: S.activation(
                            qs[:, 16 + 4 * s4:20 + 4 * s4], qs[:, 4 * s4:4 * s4 + 4],
                            AF.Sqrt, bias=cst[:, 1:2], scale=1.0 / HD),
                            waits=[t_qsq], inc=True)
                        t_qrs = P.op("vector", lambda V, s4=s4: V.reciprocal(
                            qs[:, 16 + 4 * s4:20 + 4 * s4], qs[:, 16 + 4 * s4:20 + 4 * s4]),
                            waits=[t_qstd], inc=True)
                        t_qrg0 = P.op("vector", lambda V, s4=s4: V.tensor_mul(
                            qs[:, 16 + 4 * s4:20 + 4 * s4], qs[:, 16 + 4 * s4:20 + 4 * s4],
                            grep[:, 4 * s4:4 * s4 + 4]), waits=[t_qrs], inc=True)
                        t_qrg = P.op("scalar", lambda S, s4=s4: S.copy(
                            qs[:, 32 + 4 * s4:36 + 4 * s4], qs[:, 16 + 4 * s4:20 + 4 * s4]),
                            waits=[t_qrg0], inc=True)
                        for j in range(4):
                            hq = 4 * s4 + j
                            s1 = slice(128 * j, 128 * j + 64)
                            s2 = slice(128 * j + 64, 128 * j + 128)
                            o1 = slice(128 * hq, 128 * hq + 64)
                            o2 = slice(128 * hq + 64, 128 * hq + 128)
                            def q_rope_m(V, bq=bq, hq=hq, s1=s1, s2=s2, ctile=ctile):
                                rs = qs[:, 32 + hq:33 + hq]
                                V.scalar_tensor_tensor(sA2[:], ps_q[bq][:, s1], rs,
                                                       cosT[:, ctile], OP.mult, OP.mult)
                                V.scalar_tensor_tensor(sB2[:], ps_q[bq][:, s2], rs,
                                                       sinT[:, ctile], OP.mult, OP.mult)
                                V.scalar_tensor_tensor(sC2[:], ps_q[bq][:, s1], rs,
                                                       sinT[:, ctile], OP.mult, OP.mult)
                                return V.scalar_tensor_tensor(sD2[:], ps_q[bq][:, s2], rs,
                                                              cosT[:, ctile],
                                                              OP.mult, OP.mult)
                            t_qm = P.op("vector", lambda V, f=q_rope_m: f(V),
                                        waits=[t_qrg] if j == 0 else [], inc=True)
                            def q_rope_a(V, o1=o1, o2=o2):
                                V.tensor_add(q_bf[:, o1], sA2[:], sB2[:])
                                return V.tensor_sub(q_bf[:, o2], sD2[:], sC2[:])
                            P.op("vector", lambda V, f=q_rope_a: f(V), waits=[t_qm],
                                 inc=(j == 3))
                        t_q_ev[bq] = P.tok("vector")
                    t_qrope = P.tok("vector")
                    for c in range(16):
                        i = c % 2
                        w = [t_qrope]
                        if t_tp_ev2[i]:
                            w.append(t_tp_ev2[i])
                        tk = P.op("tensor", lambda T, c=c, i=i: T.transpose(
                            ps_tp2[i][:], q_bf[:, 128 * c:128 * c + 128], idm_bf[:]),
                            waits=w, inc=True)
                        if i == 0:
                            t_tp_ev2[i] = P.op("scalar", lambda S, c=c, i=i: S.copy(
                                qfm[:, 128 * c:128 * c + 128], ps_tp2[i][:]),
                                waits=[tk], inc=True)
                        else:
                            t_tp_ev2[i] = P.op("vector", lambda V, c=c, i=i: V.tensor_copy(
                                qfm[:, 128 * c:128 * c + 128], ps_tp2[i][:]),
                                waits=[tk], inc=True)
                    t_qfm = [t_tp_ev2[0], t_tp_ev2[1]]
                    if DEBUG and g == 0 and qt == 4:
                        P.dma("gpsimd", dbg["dbg_qbf"][:], q_bf[:], waits=[t_qrope], slow=True)
                        P.dma("gpsimd", dbg["dbg_qfm"][:], qfm[:], waits=list(t_qfm), slow=True)

                    def emit_scores(hk):
                        par = hk % 2
                        mk = [None] * 5
                        for j in range(5):
                            ba = sct[0] % 2
                            sct[0] += 1
                            kslot = kt0 + j
                            sw = list(t_qfm)
                            if t_expb[ba]:
                                sw.append(t_expb[ba])
                            t_smm = P.op("tensor", lambda T, hk=hk, kslot=kslot, ba=ba:
                                         T.matmul(ps_s[ba][:],
                                                  kfm_g[:, (hk * KEY_G + kslot) * 128:
                                                        (hk * KEY_G + kslot) * 128 + 128],
                                                  qfm[:, 512 * hk:512 * hk + 512],
                                                  start=True, stop=True,
                                                  skip_group_check=True),
                                         waits=sw, inc=True)
                            ew = [t_smm]
                            if t_pfree[par]:
                                ew.append(t_pfree[par])
                            t_expb[ba] = P.op("scalar", lambda S, ba=ba, par=par, j=j:
                                              S.activation(pT4[par][j][:], ps_s[ba][:],
                                                           AF.Exp, bias=negBmax[:, 0:1],
                                                           scale=sc128[:, 0:1]),
                                              waits=ew, inc=True)
                            tok = t_expb[ba]
                            if j == 0:
                                m0use = m0f_bf if (8 * g + qt) <= 3 else m0_bf
                                tok = P.op("vector", lambda V, par=par, m0use=m0use:
                                           [V.tensor_mul(pT4[par][0][:, 128 * u:128 * u + 128],
                                                         pT4[par][0][:, 128 * u:128 * u + 128],
                                                         m0use[:]) for u in range(4)][-1],
                                           waits=[tok], inc=True)
                            if j == 4:
                                tok = P.op("vector", lambda V, par=par:
                                           [V.tensor_mul(pT4[par][4][:, 128 * u:128 * u + 128],
                                                         pT4[par][4][:, 128 * u:128 * u + 128],
                                                         m4_bf[:]) for u in range(4)][-1],
                                           waits=[tok], inc=True)
                            if 0 < j < 4 and 8 * g + qt + j <= 3:
                                tok = P.op("vector", lambda V, par=par, j=j:
                                           V.tensor_scalar_mul(pT4[par][j][:], pT4[par][j][:],
                                                               flag_sb[:, 0:1]),
                                           waits=[tok], inc=True)
                            mk[j] = tok
                        mk_d[par] = mk

                    def emit_av(hk):
                        par = hk % 2
                        for hl in range(4):
                            h = 4 * hk + hl
                            bv = h % 2
                            aw = list(mk_d[par])
                            if t_yevb[bv]:
                                aw.append(t_yevb[bv])
                            for j in range(5):
                                kslot = kt0 + j
                                P.op("tensor", lambda T, j=j, bv=bv, hk=hk, hl=hl,
                                     kslot=kslot, par=par:
                                     T.matmul(ps_att[bv][:],
                                              pT4[par][j][:, 128 * hl:128 * hl + 128],
                                              vtok_g[:, 516 * kslot + 129 * hk:
                                                     516 * kslot + 129 * hk + 129],
                                              start=(j == 0), stop=(j == 4),
                                              skip_group_check=True),
                                     waits=aw if j == 0 else [], inc=(j == 4))
                            t_av = P.tok("tensor")
                            if hl == 3:
                                t_pfree[par] = t_av
                            t_rsum = P.op("vector", lambda V, bv=bv: V.reciprocal(
                                rsum[:, bv:bv + 1], ps_att[bv][:, 128:129]),
                                waits=[t_av], inc=True)
                            P.op("scalar", lambda S, bv=bv, qt=qt, h=h: S.activation(
                                y_g[:, (qt * H + h) * 128:(qt * H + h) * 128 + 128],
                                ps_att[bv][:, 0:128], AF.Identity,
                                scale=rsum[:, bv:bv + 1]), waits=[t_rsum], inc=True)
                            t_yevb[bv] = P.tok("scalar")

                    for hk in range(KVH):
                        emit_scores(hk)
                        if hk > 0:
                            emit_av(hk - 1)
                    emit_av(KVH - 1)
                P.barrier()

            p12.close()
            with ExitStack() as p3:
                e3 = p3.enter_context
                wo_bf = e3(nc.sbuf_tensor(f"wo_bf_g{g}", [128, 16 * D], BF16))
                weff_sb = e3(nc.sbuf_tensor(f"weff_sb_g{g}", [128, 16 * D], BF16))
                xld3 = [e3(nc.sbuf_tensor(f"xld3{i}_g{g}", [128, D], F32)) for i in range(2)]
                xn2a = e3(nc.sbuf_tensor(f"xn2a_g{g}", [128, 2 * D], BF16))
                sq_scr3 = xn2a
                xnfm3 = e3(nc.sbuf_tensor(f"xnfm3_g{g}", [128, 16 * 128], BF16))
                yfm3 = e3(nc.sbuf_tensor(f"yfm3_g{g}", [128, 16 * 128], BF16))
                obuf = [e3(nc.sbuf_tensor(f"obuf{i}_g{g}", [128, 512], F32)) for i in range(2)]
                lns3 = e3(nc.sbuf_tensor(f"lns3_g{g}", [128, 32], F32))
                ps_o = [e3(nc.psum_tensor(f"ps_o{i}_g{g}", [128, 512], F32)) for i in range(2)]
                ps_tp3 = [e3(nc.psum_tensor(f"ps_tp3{i}_g{g}", [128, 128], BF16))
                          for i in range(2)]

                t_wo = []
                for c in range(16):
                    t_wo.append(P.dma("gpsimd", wo_bf[:, D * c:D * c + D],
                                      s_wo[128 * c:128 * c + 128, :]))
                    t_wo.append(P.dma("gpsimd", weff_sb[:, D * c:D * c + D],
                                      s_weff[128 * c:128 * c + 128, :]))

                t_xfree3 = [None, None]
                t_xld3 = [None, None]
                t_tp_ev3 = [None, None]
                t_xtp3_last = [None, None]
                t_xn2a_arr = [None, None]
                t_o_ev = [None, None]
                t_ob_st = [None, None]

                def ln3_emit(tt):
                    ot = 8 * g + tt
                    kt = ot + 4
                    b = tt % 2
                    lb = 16 * b
                    xb = D * b
                    t_xld3[b] = P.dma("sync", xld3[b][:], xin[128 * kt:128 * kt + 128, :],
                                      waits=[t_xfree3[b]] if t_xfree3[b] else [])
                    t_s1 = P.op("vector", lambda V, b=b, lb=lb: V.tensor_reduce(
                        lns3[:, lb + 0:lb + 1], xld3[b][:], AX.X, OP.add), waits=[t_xld3[b]],
                        inc=True)
                    sqw = [t_xld3[b]]
                    if t_xtp3_last[b]:
                        sqw.append(t_xtp3_last[b])
                    t_s2 = P.op("scalar", lambda S, b=b, lb=lb, xb=xb: S.activation(
                        sq_scr3[:, xb:xb + D], xld3[b][:], AF.Square,
                        accum_out=lns3[:, lb + 1:lb + 2]),
                        waits=sqw, inc=True)
                    def ln_a3(V, lb=lb):
                        V.tensor_scalar_mul(lns3[:, lb + 2:lb + 3], lns3[:, lb + 0:lb + 1], 1.0 / D)
                        return V.scalar_tensor_tensor(lns3[:, lb + 3:lb + 4], lns3[:, lb + 0:lb + 1],
                                                      1.0 / (D * D),
                                                      lns3[:, lb + 0:lb + 1], OP.mult, OP.mult)
                    t_lna = P.op("vector", ln_a3, waits=[t_s1], inc=True)
                    t_var = P.op("vector", lambda V, lb=lb: V.scalar_tensor_tensor(
                        lns3[:, lb + 4:lb + 5], lns3[:, lb + 1:lb + 2], 1.0 / D,
                        lns3[:, lb + 3:lb + 4],
                        OP.mult, OP.subtract), waits=[t_lna, t_s2], inc=True)
                    t_std = P.op("scalar", lambda S, lb=lb: S.activation(
                        lns3[:, lb + 5:lb + 6], lns3[:, lb + 4:lb + 5], AF.Sqrt, bias=cst[:, 0:1]),
                        waits=[t_var], inc=True)
                    t_rstd = P.op("vector", lambda V, lb=lb: V.reciprocal(
                        lns3[:, lb + 6:lb + 7], lns3[:, lb + 5:lb + 6]), waits=[t_std], inc=True)
                    t_r2 = P.op("vector", lambda V, lb=lb: V.tensor_mul(
                        lns3[:, lb + 7:lb + 8], lns3[:, lb + 6:lb + 7], lns3[:, lb + 6:lb + 7]),
                        waits=[t_rstd], inc=True)
                    t_v2 = P.op("vector", lambda V, lb=lb: V.tensor_mul(
                        lns3[:, lb + 8:lb + 9], lns3[:, lb + 7:lb + 8], lns3[:, lb + 4:lb + 5]),
                        waits=[t_r2], inc=True)
                    t_std2 = P.op("scalar", lambda S, lb=lb: S.activation(
                        lns3[:, lb + 9:lb + 10], lns3[:, lb + 8:lb + 9], AF.Sqrt, bias=cst[:, 0:1]),
                        waits=[t_v2], inc=True)
                    t_rstd2 = P.op("vector", lambda V, lb=lb: V.reciprocal(
                        lns3[:, lb + 10:lb + 11], lns3[:, lb + 9:lb + 10]), waits=[t_std2], inc=True)
                    t_sc1 = P.op("vector", lambda V, lb=lb: V.tensor_mul(
                        lns3[:, lb + 11:lb + 12], lns3[:, lb + 6:lb + 7], lns3[:, lb + 10:lb + 11]),
                        waits=[t_rstd2], inc=True)
                    t_ab = P.op("vector", lambda V, lb=lb: V.scalar_tensor_tensor(
                        lns3[:, lb + 12:lb + 13], lns3[:, lb + 2:lb + 3], -1.0,
                        lns3[:, lb + 11:lb + 12],
                        OP.mult, OP.mult), waits=[t_sc1], inc=True)
                    t_xn2a_arr[b] = P.op("scalar", lambda S, b=b, lb=lb, xb=xb: S.activation(
                        xn2a[:, xb:xb + D], xld3[b][:], AF.Identity, bias=lns3[:, lb + 12:lb + 13],
                        scale=lns3[:, lb + 11:lb + 12]), waits=[t_ab], inc=True)

                ln3_emit(0)
                for tt in range(OWN_G):
                    ot = 8 * g + tt
                    kt = ot + 4
                    b = tt % 2
                    xb = D * b
                    if tt + 1 < OWN_G:
                        ln3_emit(tt + 1)
                    t_xn2a = t_xn2a_arr[b]
                    for c in range(16):
                        i = c % 2
                        w = [t_xn2a]
                        if t_tp_ev3[i]:
                            w.append(t_tp_ev3[i])
                        tk = P.op("tensor", lambda T, c=c, i=i, xb=xb: T.transpose(
                            ps_tp3[i][:], xn2a[:, xb + 128 * c:xb + 128 * c + 128], idm_bf[:]),
                            waits=w, inc=True)
                        if c == 15:
                            t_xtp3_last[b] = tk
                        if i == 0:
                            t_tp_ev3[i] = P.op("scalar", lambda S, c=c, i=i: S.copy(
                                xnfm3[:, 128 * c:128 * c + 128], ps_tp3[i][:]),
                                waits=[tk], inc=True)
                        else:
                            t_tp_ev3[i] = P.op("vector", lambda V, c=c, i=i: V.tensor_copy(
                                xnfm3[:, 128 * c:128 * c + 128], ps_tp3[i][:]),
                                waits=[tk], inc=True)
                    for hh in range(16):
                        i = hh % 2
                        tk = P.op("tensor", lambda T, hh=hh, i=i, tt=tt: T.transpose(
                            ps_tp3[i][:], y_g[:, (tt * H + hh) * 128:(tt * H + hh) * 128 + 128],
                            idm_bf[:]), waits=[t_tp_ev3[i]], inc=True)
                        if i == 0:
                            t_tp_ev3[i] = P.op("scalar", lambda S, hh=hh, i=i: S.copy(
                                yfm3[:, 128 * hh:128 * hh + 128], ps_tp3[i][:]),
                                waits=[tk], inc=True)
                        else:
                            t_tp_ev3[i] = P.op("vector", lambda V, hh=hh, i=i: V.tensor_copy(
                                yfm3[:, 128 * hh:128 * hh + 128], ps_tp3[i][:]),
                                waits=[tk], inc=True)
                    t_fm_ready = [t_tp_ev3[0], t_tp_ev3[1]]
                    for s4 in range(4):
                        bo = s4 % 2
                        w0 = list(t_fm_ready)
                        if t_o_ev[bo]:
                            w0.append(t_o_ev[bo])
                        for c in range(16):
                            cw = list(w0) if c == 0 else []
                            if tt == 0:
                                cw.append(t_wo[2 * c])
                            P.op("tensor", lambda T, c=c, s4=s4, bo=bo: T.matmul(
                                ps_o[bo][:], yfm3[:, 128 * c:128 * c + 128],
                                wo_bf[:, D * c + 512 * s4:D * c + 512 * s4 + 512],
                                start=(c == 0), stop=False, skip_group_check=True),
                                waits=cw)
                        for c in range(16):
                            P.op("tensor", lambda T, c=c, s4=s4, bo=bo: T.matmul(
                                ps_o[bo][:], xnfm3[:, 128 * c:128 * c + 128],
                                weff_sb[:, D * c + 512 * s4:D * c + 512 * s4 + 512],
                                start=False, stop=False, skip_group_check=True),
                                waits=[t_wo[2 * c + 1]] if tt == 0 else [])
                        P.op("tensor", lambda T, s4=s4, bo=bo: T.matmul(
                            ps_o[bo][:], ones_r_bf[:], c_bf[0:1, 512 * s4:512 * s4 + 512],
                            start=False, stop=True, skip_group_check=True), inc=True)
                        t_omm = P.tok("tensor")
                        ww = [t_omm]
                        if t_ob_st[bo]:
                            ww.append(t_ob_st[bo])
                        t_o_ev[bo] = P.op("vector", lambda V, s4=s4, bo=bo, b=b: V.tensor_add(
                            obuf[bo][:], ps_o[bo][:], xld3[b][:, 512 * s4:512 * s4 + 512]),
                            waits=ww, inc=True)
                        t_ob_st[bo] = P.dma("sync",
                                            yout[128 * ot:128 * ot + 128,
                                                 512 * s4:512 * s4 + 512],
                                            obuf[bo][:], waits=[t_o_ev[bo]])
                        if s4 == 3:
                            t_xfree3[b] = t_o_ev[bo]
                P.barrier()

        for g in range(NG):
            emit_group(g)

    with nc.Block() as block:
        P.emit(block)
    stack.close()
    return nc


_prog_cache = {}


def kernel(L0_attn_x, L0_attn_Wq, L0_attn_Wk, L0_attn_Wv, L0_attn_Wo,
           L0_attn_q_gain, L0_ffn_W, L0_ffn_ln_w, L0_ffn_ln_b):
    x = np.ascontiguousarray(np.asarray(L0_attn_x, np.float32))
    B, S, _ = x.shape
    wqT = np.ascontiguousarray(np.asarray(L0_attn_Wq, np.float32).T)
    wkT = np.ascontiguousarray(np.asarray(L0_attn_Wk, np.float32).T)
    wvT = np.ascontiguousarray(np.asarray(L0_attn_Wv, np.float32).T)
    woT = np.ascontiguousarray(np.asarray(L0_attn_Wo, np.float32).T)
    wff = np.ascontiguousarray(np.asarray(L0_ffn_W, np.float32))
    lnw = np.ascontiguousarray(np.asarray(L0_ffn_ln_w, np.float32).reshape(16, 128).T)
    lnb = np.ascontiguousarray(np.asarray(L0_ffn_ln_b, np.float32).reshape(16, 128).T)
    qgain = np.asarray(L0_attn_q_gain, np.float32)
    iq = np.arange(128)[None, :]
    ik = np.arange(128)[:, None]
    m0 = (ik >= iq + 1).astype(np.float32)
    m4 = (ik <= iq).astype(np.float32)
    idm = np.eye(128, dtype=np.float32)

    if "tabs" not in _prog_cache:
        import jax
        import jax.numpy as jnp
        with jax.default_device(jax.devices("cpu")[0]):
            half = HD // 2
            inv_freq = 10000.0 ** (-jnp.arange(0, half, dtype=jnp.float32) / half)
            freqs = jnp.arange(S, dtype=jnp.float32)[:, None] * inv_freq[None, :]
            _prog_cache["tabs"] = (np.asarray(jnp.cos(freqs)), np.asarray(jnp.sin(freqs)))
    cos_full, sin_full = _prog_cache["tabs"]

    if "nc" not in _prog_cache:
        _prog_cache["nc"] = build_program()
    nc = _prog_cache["nc"]

    in_maps = []
    for c in range(8):
        b, h = c // 2, c % 2
        own = x[b, 2048 * h:2048 * h + 2048]
        halo = x[b, 2048 * h - 512:2048 * h] if h == 1 else np.zeros((512, 2048), np.float32)
        xin = np.concatenate([halo, own], axis=0)
        rows = np.clip(np.arange(2048 * h - 512, 2048 * h + 2048), 0, S - 1)
        cosTh = np.ascontiguousarray(
            cos_full[rows].reshape(NT_KEY, 128, 64).transpose(1, 0, 2).reshape(128, TW))
        sinTh = np.ascontiguousarray(
            sin_full[rows].reshape(NT_KEY, 128, 64).transpose(1, 0, 2).reshape(128, TW))
        flag = np.full(128, float(h), np.float32)
        in_maps.append(dict(xin=xin, wqT=wqT, wkT=wkT, wvT=wvT, woT=woT, wff=wff,
                            lnw=lnw, lnb=lnb, qgain=qgain, cosTh=cosTh, sinTh=sinTh,
                            flag=flag, m0=m0, m4=m4, idm=idm))
    res = run_bass_kernel_spmd(nc, in_maps, core_ids=list(range(8)))
    out = np.empty((B, S, D), res.results[0]["yout"].dtype)
    for c in range(8):
        b, h = c // 2, c % 2
        out[b, 2048 * h:2048 * h + 2048] = res.results[c]["yout"]
    return out


# version 13 (final)
# speedup vs baseline: 1.1014x; 1.0558x over previous
import math
import numpy as np
import concourse.bass as bass
import concourse.mybir as mybir
from concourse.bass_utils import run_bass_kernel_spmd
from concourse.alu_op_type import AluOpType as OP
from contextlib import ExitStack

F32 = mybir.dt.float32
BF16 = mybir.dt.bfloat16
I32 = mybir.dt.int32
AF = mybir.ActivationFunctionType
AX = mybir.AxisListType

DEBUG = False

D = 2048
H = 16
KVH = 4
HD = 128
NKEY = 2560
NT_KEY = 20
NT_OWN = 16
NG = 2
OWN_G = 8
KEY_G = 12
EPS_LN = 1e-5
EPS_RMS = 1.1920928955078125e-07
TW = NT_KEY * 64

ENGS = ("sync", "scalar", "vector", "tensor", "gpsimd")
DMA_ENGS = {"sync", "gpsimd"}
DMA_SLOTS = 6


class Prog:

    def __init__(self, nc, sems):
        self.nc = nc
        self.sems = sems
        self.ops = {e: [] for e in ENGS}
        self.cnt = {k: 0 for k in sems}
        for e in ENGS:
            self.cnt.setdefault(e, 0)
        self.wmax = {e: {} for e in ENGS}
        self.dirty = {e: False for e in ENGS}
        self.dma_i = {q: 0 for q in DMA_ENGS}

    def _waits(self, eng, waits):
        for tok in waits:
            if tok is None:
                continue
            pe, v = tok
            if self.wmax[eng].get(pe, 0) >= v:
                continue
            self.wmax[eng][pe] = v
            sem = self.sems[pe]
            self.ops[eng].append(lambda E, sem=sem, v=v: E.wait_ge(sem, v))

    def op(self, eng, fn, waits=(), inc=False):
        self._waits(eng, waits)
        if inc:
            self.cnt[eng] += 1
            sem = self.sems[eng]
            self.ops[eng].append(lambda E, fn=fn, sem=sem: fn(E).then_inc(sem, 1))
            self.dirty[eng] = False
            return (eng, self.cnt[eng])
        self.ops[eng].append(fn)
        self.dirty[eng] = True
        return None

    def dma(self, eng, out, in_, waits=(), slow=False):
        i = self.dma_i[eng]
        self.dma_i[eng] = i + 1
        slot, cyc = i % DMA_SLOTS, i // DMA_SLOTS
        key = f"{eng}{slot}"
        w = list(waits)
        if cyc > 0:
            w.append((key, 16 * cyc))
        self._waits(eng, w)
        self.cnt[key] = 16 * (cyc + 1)
        sem = self.sems[key]
        self.ops[eng].append(lambda E, o=out, i_=in_, sem=sem, slow=slow: E.dma_start(
            out=o, in_=i_, allow_slow_non_contiguous=slow).then_inc(sem, 16))
        return (key, 16 * (cyc + 1))

    def tok(self, eng):
        assert not self.dirty[eng], f"{eng} has uncounted trailing ops"
        return (eng, self.cnt[eng])

    def barrier(self):
        for e in ENGS:
            assert not self.dirty[e], f"barrier with uncounted ops on {e}"
        toks = [(k, v) for k, v in self.cnt.items() if v > 0]
        for e in ENGS:
            self._waits(e, toks)

    def emit(self, block):
        for eng in ENGS:
            def section(E, eng=eng):
                for f in self.ops[eng]:
                    f(E)
            getattr(block, eng)(section)


def build_program():
    nc = bass.Bass("TRN2", target_bir_lowering=False,
                   detect_race_conditions=False)
    d = {}
    def din(name, shape, dtype=F32):
        d[name] = nc.dram_tensor(name, shape, dtype, kind="ExternalInput").ap()
        return d[name]
    def dint(name, shape, dtype):
        d[name] = nc.dram_tensor(name, shape, dtype, kind="Internal").ap()
        return d[name]

    xin = din("xin", [NKEY, D])
    wqT = din("wqT", [D, D])
    wkT = din("wkT", [D, KVH * HD])
    wvT = din("wvT", [D, KVH * HD])
    woT = din("woT", [D, D])
    wff = din("wff", [D, D])
    lnw_h = din("lnw", [128, 16])
    lnb_h = din("lnb", [128, 16])
    qgain_h = din("qgain", [H])
    cosT_h = din("cosTh", [128, TW])
    sinT_h = din("sinTh", [128, TW])
    flag_h = din("flag", [128])
    m0_h = din("m0", [128, 128])
    m4_h = din("m4", [128, 128])
    idm_h = din("idm", [128, 128])
    yout = nc.dram_tensor("yout", [NT_OWN * 128, D], F32, kind="ExternalOutput").ap()

    s_wq = dint("s_wq", [D, D], BF16)
    s_wo = dint("s_wo", [D, D], BF16)
    s_weff = dint("s_weff", [D, D], BF16)

    dbg = {}
    if DEBUG:
        for nm, shp, dt in [("dbg_small", [128, 64], F32), ("dbg_xn", [128, D], BF16),
                            ("dbg_kbf", [128, 512], BF16), ("dbg_vtok", [128, 512], BF16),
                            ("dbg_qbf", [128, D], BF16), ("dbg_pT", [128, 640], BF16),
                            ("dbg_y", [128, 128], BF16), ("dbg_xn2a", [128, D], BF16),
                            ("dbg_cbf", [1, D], BF16), ("dbg_invf", [128, 64], F32),
                            ("dbg_weff", [128, D], BF16), ("dbg_costile", [128, 64], BF16),
                            ("dbg_sintile", [128, 64], BF16), ("dbg_xnfm", [128, 128], BF16),
                            ("dbg_lns", [128, 16], F32), ("dbg_qfm", [128, 2048], BF16)]:
            dbg[nm] = nc.dram_tensor(nm, shp, dt, kind="ExternalOutput").ap()

    sems = {}
    stack = ExitStack()
    for e in ("scalar", "vector", "tensor"):
        sems[e] = stack.enter_context(nc.semaphore(f"sem_{e}"))
    for q in DMA_ENGS:
        for i in range(DMA_SLOTS):
            sems[f"{q}{i}"] = stack.enter_context(nc.semaphore(f"sem_{q}{i}"))
    P = Prog(nc, sems)
    E = stack.enter_context

    cst = E(nc.sbuf_tensor("cst", [128, 4], F32))
    sc128 = E(nc.sbuf_tensor("sc128", [128, 1], F32))
    negBmax = E(nc.sbuf_tensor("negBmax", [128, 1], F32))
    negB = E(nc.sbuf_tensor("negB", [128, H], F32))
    grep = E(nc.sbuf_tensor("grep", [128, H], F32))
    flag_sb = E(nc.sbuf_tensor("flag_sb", [128, 1], F32))
    m0_bf = E(nc.sbuf_tensor("m0_bf", [128, 128], BF16))
    m0f_bf = E(nc.sbuf_tensor("m0f_bf", [128, 128], BF16))
    m4_bf = E(nc.sbuf_tensor("m4_bf", [128, 128], BF16))
    idm_bf = E(nc.sbuf_tensor("idm_bf", [128, 128], BF16))
    ones_r_bf = E(nc.sbuf_tensor("ones_r_bf", [1, 128], BF16))
    ones_r_f = E(nc.sbuf_tensor("ones_r_f", [1, 128], F32))
    ones_c_f = E(nc.sbuf_tensor("ones_c_f", [128, 1], F32))
    ones_c_bf = E(nc.sbuf_tensor("ones_c_bf", [128, 1], BF16))
    c_bf = E(nc.sbuf_tensor("c_bf", [1, D], BF16))
    cosT = E(nc.sbuf_tensor("cosT", [128, TW], BF16))
    sinT = E(nc.sbuf_tensor("sinT", [128, TW], BF16))

    with ExitStack() as p0:
        e0 = p0.enter_context
        sm1 = e0(nc.sbuf_tensor("sm1", [1, 128], F32))
        sm2 = e0(nc.sbuf_tensor("sm2", [1, 128], F32))
        sm3 = e0(nc.sbuf_tensor("sm3", [1, 128], F32))
        mload = e0(nc.sbuf_tensor("mload", [128, 3 * 128], F32))
        tbl = [e0(nc.sbuf_tensor(f"tbl{i}", [128, TW], F32)) for i in range(2)]
        ps_rep = e0(nc.psum_tensor("ps_rep", [128, 64], F32))

        def memsets(V):
            V.memset(cst[:, 0:1], EPS_LN)
            V.memset(cst[:, 1:2], EPS_RMS)
            V.memset(cst[:, 2:3], 128.0)
            V.memset(cst[:, 3:4], 10000.0)
            V.memset(ones_r_bf[:], 1.0)
            V.memset(ones_r_f[:], 1.0)
            V.memset(ones_c_f[:], 1.0)
            return V.memset(ones_c_bf[:], 1.0)
        P.op("vector", memsets, inc=True)
        t_con = P.tok("vector")

        tks = []
        t_ctl = P.dma("sync", tbl[0][:], cosT_h)
        t_stl = P.dma("sync", tbl[1][:], sinT_h)
        tks.append(P.op("vector", lambda V: V.tensor_copy(cosT[:], tbl[0][:]),
                        waits=[t_ctl], inc=True))
        tks.append(P.op("scalar", lambda S: S.copy(sinT[:], tbl[1][:]),
                        waits=[t_stl], inc=True))
        tks.append(P.dma("sync", sm2[0:1, 0:16], qgain_h.unsqueeze(0)))
        tks.append(P.dma("sync", flag_sb[:, 0:1], flag_h.unsqueeze(1)))
        tks.append(P.dma("sync", mload[:, 0:128], m0_h))
        tks.append(P.dma("sync", mload[:, 128:256], m4_h))
        tks.append(P.dma("sync", mload[:, 256:384], idm_h))

        def casts0(V):
            V.tensor_copy(m0_bf[:], mload[:, 0:128])
            V.tensor_copy(m4_bf[:], mload[:, 128:256])
            V.tensor_copy(idm_bf[:], mload[:, 256:384])
            return V.tensor_scalar_mul(m0f_bf[:], mload[:, 0:128], flag_sb[:, 0:1])
        P.op("vector", casts0, waits=tks + [t_con], inc=True)

        def sconsts(S):
            S.activation(sm3[0:1, 0:1], cst[0:1, 2:3], AF.Sqrt)
            return S.activation(sm3[0:1, 16:32], sm2[0:1, 0:16], AF.Abs)
        P.op("scalar", sconsts, waits=[t_con] + tks, inc=True)
        t_sc0 = P.tok("scalar")
        P.op("vector", lambda V: V.tensor_scalar_mul(sm1[0:1, 16:32], sm3[0:1, 16:32],
                                                     sm3[0:1, 0:1]), waits=[t_sc0],
             inc=True)
        t_v1a = P.tok("vector")
        def v1b(V):
            V.tensor_scalar_mul(sm1[0:1, 16:32], sm1[0:1, 16:32], -1.0)
            V.tensor_reduce(sm1[0:1, 40:41], sm1[0:1, 16:32], AX.X, OP.min)
            return V.reciprocal(sm1[0:1, 0:1], sm3[0:1, 0:1])
        P.op("vector", v1b, waits=[t_v1a], inc=True)
        t_v1 = P.tok("vector")
        def repl2(T):
            T.matmul(ps_rep[:, 0:16], ones_r_f[:], sm1[0:1, 16:32], start=True, stop=True)
            T.matmul(ps_rep[:, 16:32], ones_r_f[:], sm2[0:1, 0:16], start=True, stop=True)
            T.matmul(ps_rep[:, 33:34], ones_r_f[:], sm1[0:1, 40:41],
                     start=True, stop=True)
            return T.matmul(ps_rep[:, 32:33], ones_r_f[:], sm1[0:1, 0:1],
                            start=True, stop=True)
        P.op("tensor", repl2, waits=[t_v1], inc=True)
        t_mm1 = P.tok("tensor")
        def cop2(V):
            V.tensor_copy(negB[:], ps_rep[:, 0:16])
            V.tensor_copy(grep[:], ps_rep[:, 16:32])
            V.tensor_copy(negBmax[:], ps_rep[:, 33:34])
            return V.tensor_copy(sc128[:], ps_rep[:, 32:33])
        P.op("vector", cop2, waits=[t_mm1], inc=True)
        t_grep = P.tok("vector")
        if DEBUG:
            P.dma("gpsimd", dbg["dbg_small"][:, 0:1], sc128[:], slow=True)
            P.dma("gpsimd", dbg["dbg_small"][:, 1:17], negB[:], slow=True)
            P.dma("gpsimd", dbg["dbg_small"][:, 17:33], grep[:], slow=True)
            P.dma("gpsimd", dbg["dbg_small"][:, 33:37], cst[:], slow=True)
            P.dma("gpsimd", dbg["dbg_small"][:, 38:39], flag_sb[:], slow=True)
        P.barrier()

    with ExitStack() as pg:
        eg = pg.enter_context
        y_g = eg(nc.sbuf_tensor("y_g", [128, OWN_G * H * 128], BF16))

        def emit_group(g):
            p12 = ExitStack()
            xnfm_g = p12.enter_context(
                nc.sbuf_tensor(f"xnfm_g{g}", [128, OWN_G * 16 * 128], BF16))
            kfm_g = p12.enter_context(
                nc.sbuf_tensor(f"kfm_g{g}", [128, KVH * KEY_G * 128], BF16))
            vtok_g = p12.enter_context(
                nc.sbuf_tensor(f"vtok_g{g}", [128, KEY_G * 516], BF16))

            with ExitStack() as p1:
                e1 = p1.enter_context
                wk_bf = e1(nc.sbuf_tensor(f"wk_bf_g{g}", [128, 16 * 512], BF16))
                wv_bf = e1(nc.sbuf_tensor(f"wv_bf_g{g}", [128, 16 * 512], BF16))
                xld = [e1(nc.sbuf_tensor(f"xld{i}_g{g}", [128, D], F32)) for i in range(2)]
                xn_bf = e1(nc.sbuf_tensor(f"xn_bf_g{g}", [128, 2 * D], BF16))
                sq_scr = e1(nc.sbuf_tensor(f"sq_scr_g{g}", [128, D], BF16))
                xnfm_t = e1(nc.sbuf_tensor(f"xnfm_t_g{g}", [128, 16 * 128], BF16))
                k_bf = e1(nc.sbuf_tensor(f"k_bf_g{g}", [128, 512], BF16))
                sA = e1(nc.sbuf_tensor(f"sA_g{g}", [128, 64], F32))
                sB = e1(nc.sbuf_tensor(f"sB_g{g}", [128, 64], F32))
                sC = e1(nc.sbuf_tensor(f"sC_g{g}", [128, 64], F32))
                sD = e1(nc.sbuf_tensor(f"sD_g{g}", [128, 64], F32))
                lns = e1(nc.sbuf_tensor(f"lns_g{g}", [128, 32], F32))
                ps_k = e1(nc.psum_tensor(f"ps_k_g{g}", [128, 512], F32))
                ps_v = e1(nc.psum_tensor(f"ps_v_g{g}", [128, 512], F32))
                ntp = 2 if g == 0 else 4
                ps_tp = [e1(nc.psum_tensor(f"ps_tp{i}_g{g}", [128, 128], BF16))
                         for i in range(ntp)]

                kstg = [e1(nc.sbuf_tensor(f"kstg{i}_g{g}", [128, 512], F32))
                        for i in range(3)]
                t_wkv = []
                kprev = [None] * 3

                def emit_wkv():
                    for c in range(16):
                        for idx, (srcT, dstB) in enumerate(((wkT, wk_bf), (wvT, wv_bf))):
                            sl = (2 * c + idx) % 3
                            t_ld = P.dma("gpsimd" if idx == 0 else "sync",
                                         kstg[sl][:], srcT[128 * c:128 * c + 128, :],
                                         waits=[kprev[sl]] if kprev[sl] else [])
                            if (2 * c + idx) % 2 == 0:
                                t_cv = P.op("scalar", lambda S, sl=sl, c=c, dstB=dstB: S.copy(
                                    dstB[:, 512 * c:512 * c + 512], kstg[sl][:]),
                                    waits=[t_ld], inc=True)
                            else:
                                t_cv = P.op("vector", lambda V, sl=sl, c=c, dstB=dstB:
                                            V.tensor_copy(dstB[:, 512 * c:512 * c + 512],
                                                          kstg[sl][:]), waits=[t_ld], inc=True)
                            kprev[sl] = t_cv
                            t_wkv.append(t_cv)

                items = []
                if g == 0:
                    wstg = [e1(nc.sbuf_tensor(f"wstg{i}", [128, D], F32)) for i in range(2)]
                    t_scrP = e1(nc.sbuf_tensor("t_scrP", [128, D], F32))
                    tern_bf = [e1(nc.sbuf_tensor(f"tern_bf{i}", [128, D], BF16))
                               for i in range(2)]
                    weff_bf0 = e1(nc.sbuf_tensor("weff_bf0", [128, D], BF16))
                    lnw_sb = e1(nc.sbuf_tensor("lnw_sb", [128, 16], F32))
                    lnb_sb = e1(nc.sbuf_tensor("lnb_sb", [128, 16], F32))
                    lnb_bf = e1(nc.sbuf_tensor("lnb_bf", [128, 16], BF16))
                    asum = e1(nc.sbuf_tensor("asum", [128, 20], F32))
                    ps_c = e1(nc.psum_tensor("ps_c", [128, D], F32))
                    tk_lw = P.dma("sync", lnw_sb[:], lnw_h)
                    tk_lb = P.dma("sync", lnb_sb[:], lnb_h)
                    t_lnbbf = P.op("vector", lambda V: V.tensor_copy(lnb_bf[:], lnb_sb[:]),
                                   waits=[tk_lb], inc=True)
                    pst = {"wi": 0, "ci": 0, "wfree": [None, None],
                           "cfree": [None, None], "absred": [], "t_scale": None,
                           "t_tern_mm": None, "t_weff_st": None,
                           "mm_pend": None, "mm_tok": [None, None]}

                    def abs_item(c):
                        def run():
                            sl = pst["wi"] % 2
                            pst["wi"] += 1
                            t_ld = P.dma("gpsimd", wstg[sl][:], wff[128 * c:128 * c + 128, :],
                                         waits=[pst["wfree"][sl]] if pst["wfree"][sl] else [])
                            t_r = P.op("vector", lambda V, sl=sl, c=c: V.tensor_reduce(
                                asum[:, c:c + 1], wstg[sl][:], AX.X, OP.add,
                                apply_absolute_value=True), waits=[t_ld], inc=True)
                            pst["wfree"][sl] = t_r
                            pst["absred"].append(t_r)
                        return run

                    def scale_item():
                        def run():
                            t1 = P.op("vector", lambda V: V.tensor_reduce(
                                asum[:, 16:17], asum[:, 0:16], AX.X, OP.add),
                                waits=pst["absred"], inc=True)
                            t2 = P.op("tensor", lambda T: T.matmul(
                                ps_c[0:1, 0:1], asum[:, 16:17], ones_c_f[:],
                                start=True, stop=True, skip_group_check=True),
                                waits=[t1], inc=True)
                            t3 = P.op("vector", lambda V: V.reciprocal(
                                asum[0:1, 17:18], ps_c[0:1, 0:1]), waits=[t2], inc=True)
                            t4 = P.op("vector", lambda V: V.tensor_scalar_mul(
                                asum[0:1, 18:19], asum[0:1, 17:18], float(D * D)),
                                waits=[t3], inc=True)
                            t5 = P.op("tensor", lambda T: T.matmul(
                                ps_c[:, 4:5], ones_r_f[:], asum[0:1, 18:19],
                                start=True, stop=True, skip_group_check=True),
                                waits=[t4], inc=True)
                            pst["t_scale"] = P.op("vector", lambda V: V.tensor_copy(
                                asum[:, 19:20], ps_c[:, 4:5]), waits=[t5], inc=True)
                        return run

                    def flush_tern_mm():
                        if pst["mm_pend"] is None:
                            return
                        c, par, t_t = pst["mm_pend"]
                        pst["mm_pend"] = None
                        for s5 in range(4):
                            P.op("tensor", lambda T, c=c, s5=s5, par=par: T.matmul(
                                ps_c[0:1, 512 * s5:512 * s5 + 512], lnb_bf[:, c:c + 1],
                                tern_bf[par][:, 512 * s5:512 * s5 + 512],
                                start=(c == 0), stop=(c == 15), skip_group_check=True),
                                waits=[t_t, t_lnbbf] if s5 == 0 else [], inc=(s5 == 3))
                        pst["t_tern_mm"] = pst["mm_tok"][par] = P.tok("tensor")

                    def tern_item(c):
                        def run():
                            sl = pst["wi"] % 2
                            par = c % 2
                            pst["wi"] += 1
                            t_ld = P.dma("gpsimd", wstg[sl][:], wff[128 * c:128 * c + 128, :],
                                         waits=[pst["wfree"][sl]] if pst["wfree"][sl] else [])
                            aw = [t_ld, pst["t_scale"]]
                            if pst.get("prev_tt"):
                                aw.append(pst["prev_tt"])
                            t_a = P.op("scalar", lambda S, sl=sl: S.activation(
                                t_scrP[:], wstg[sl][:], AF.Identity,
                                scale=asum[:, 19:20]),
                                waits=aw, inc=True)
                            tw = [t_a]
                            if pst["mm_tok"][par]:
                                tw.append(pst["mm_tok"][par])
                            def tern_chunk(V, sl=sl, par=par):
                                V.tensor_single_scalar(wstg[sl][:], t_scrP[:], -0.5, OP.is_lt)
                                return V.scalar_tensor_tensor(tern_bf[par][:], t_scrP[:], 0.5,
                                                              wstg[sl][:],
                                                              OP.is_gt, OP.subtract)
                            t_t = P.op("vector", lambda V, f=tern_chunk: f(V),
                                       waits=tw, inc=True)
                            pst["wfree"][sl] = t_t
                            pst["prev_tt"] = t_t
                            ww = [t_t, tk_lw]
                            if pst["t_weff_st"]:
                                ww.append(pst["t_weff_st"])
                            t_w = P.op("scalar", lambda S, c=c, par=par: S.activation(
                                weff_bf0[:], tern_bf[par][:], AF.Identity,
                                scale=lnw_sb[:, c:c + 1]), waits=ww, inc=True)
                            pst["t_weff_st"] = P.dma("gpsimd",
                                                     s_weff[128 * c:128 * c + 128, :],
                                                     weff_bf0[:], waits=[t_w])
                            flush_tern_mm()
                            pst["mm_pend"] = (c, par, t_t)
                        return run

                    def cfin_item():
                        def run():
                            flush_tern_mm()
                            P.op("vector", lambda V: V.tensor_copy(c_bf[0:1, :], ps_c[0:1, :]),
                                 waits=[pst["t_tern_mm"], pst["t_weff_st"]], inc=True)
                        return run

                    for c in range(16):
                        items.append(abs_item(c))
                    items.append(scale_item())
                    for c in range(16):
                        items.append(tern_item(c))
                    items.append(cfin_item())

                t_xfree = [None, None]
                t_xld = [None, None]
                t_tp_ev = [None] * ntp
                t_xtp_last = [None, None]
                t_kepi = None
                t_vevac = None
                P.op("vector", lambda V: V.memset(vtok_g[:], 1.0), inc=True)
                t_xn_arr = [None, None]

                def ln_emit(s):
                    kt = 8 * g + s
                    b = s % 2
                    lb = 16 * b
                    xb = D * b
                    t_xld[b] = P.dma("sync", xld[b][:], xin[128 * kt:128 * kt + 128, :],
                                     waits=[t_xfree[b]] if t_xfree[b] else [])
                    t_s1 = P.op("vector", lambda V, b=b, lb=lb: V.tensor_reduce(
                        lns[:, lb + 0:lb + 1], xld[b][:], AX.X, OP.add), waits=[t_xld[b]],
                        inc=True)
                    t_s2 = P.op("scalar", lambda S, b=b, lb=lb: S.activation(
                        sq_scr[:], xld[b][:], AF.Square, accum_out=lns[:, lb + 1:lb + 2]),
                        waits=[t_xld[b]], inc=True)
                    def ln_a(V, lb=lb):
                        V.tensor_scalar_mul(lns[:, lb + 2:lb + 3], lns[:, lb + 0:lb + 1], 1.0 / D)
                        return V.scalar_tensor_tensor(lns[:, lb + 3:lb + 4], lns[:, lb + 0:lb + 1],
                                                      1.0 / (D * D),
                                                      lns[:, lb + 0:lb + 1], OP.mult, OP.mult)
                    t_lna = P.op("vector", ln_a, waits=[t_s1], inc=True)
                    t_var = P.op("vector", lambda V, lb=lb: V.scalar_tensor_tensor(
                        lns[:, lb + 4:lb + 5], lns[:, lb + 1:lb + 2], 1.0 / D, lns[:, lb + 3:lb + 4],
                        OP.mult, OP.subtract), waits=[t_lna, t_s2], inc=True)
                    t_std = P.op("scalar", lambda S, lb=lb: S.activation(
                        lns[:, lb + 5:lb + 6], lns[:, lb + 4:lb + 5], AF.Sqrt, bias=cst[:, 0:1]),
                        waits=[t_var], inc=True)
                    t_rstd = P.op("vector", lambda V, lb=lb: V.reciprocal(
                        lns[:, lb + 6:lb + 7], lns[:, lb + 5:lb + 6]), waits=[t_std], inc=True)
                    t_nb = P.op("vector", lambda V, lb=lb: V.scalar_tensor_tensor(
                        lns[:, lb + 7:lb + 8], lns[:, lb + 2:lb + 3], -1.0, lns[:, lb + 6:lb + 7],
                        OP.mult, OP.mult), waits=[t_rstd], inc=True)
                    t_xn = P.op("scalar", lambda S, b=b, lb=lb, xb=xb: S.activation(
                        xn_bf[:, xb:xb + D], xld[b][:], AF.Identity, bias=lns[:, lb + 7:lb + 8],
                        scale=lns[:, lb + 6:lb + 7]),
                        waits=[t_nb] + ([t_xtp_last[b]] if t_xtp_last[b] else []), inc=True)
                    t_xfree[b] = t_xn
                    t_xn_arr[b] = t_xn

                ln_emit(0)
                emit_wkv()
                for s in range(KEY_G):
                    kt = 8 * g + s
                    b = s % 2
                    lb = 16 * b
                    xb = D * b
                    own = kt >= 4 and (kt - 4) // 8 == g
                    ot_local = (kt - 4) - 8 * g if own else None
                    if s + 1 < KEY_G:
                        ln_emit(s + 1)
                    t_xn = t_xn_arr[b]
                    dst = xnfm_g if own else xnfm_t
                    base = (ot_local * 16 * 128) if own else 0
                    for c in range(16):
                        i = c % ntp
                        w = [t_xn]
                        if t_tp_ev[i]:
                            w.append(t_tp_ev[i])
                        tk = P.op("tensor", lambda T, c=c, i=i, xb=xb: T.transpose(
                            ps_tp[i][:], xn_bf[:, xb + 128 * c:xb + 128 * c + 128], idm_bf[:]),
                            waits=w, inc=True)
                        if c == 15:
                            t_xtp_last[b] = tk
                        if i % 2 == 0:
                            t_tp_ev[i] = P.op("scalar", lambda S, c=c, i=i, dst=dst, base=base:
                                              S.copy(dst[:, base + 128 * c:base + 128 * c + 128],
                                                     ps_tp[i][:]), waits=[tk], inc=True)
                        else:
                            t_tp_ev[i] = P.op("vector", lambda V, c=c, i=i, dst=dst, base=base:
                                              V.tensor_copy(dst[:, base + 128 * c:base + 128 * c + 128],
                                                            ps_tp[i][:]), waits=[tk], inc=True)
                    kv_waits = [t for t in t_tp_ev if t]
                    if t_kepi:
                        kv_waits.append(t_kepi)
                    for c in range(16):
                        P.op("tensor", lambda T, c=c, dst=dst, base=base: T.matmul(
                            ps_k[:], dst[:, base + 128 * c:base + 128 * c + 128],
                            wk_bf[:, 512 * c:512 * c + 512],
                            start=(c == 0), stop=(c == 15), skip_group_check=True),
                            waits=(kv_waits if c == 0 else []) + [t_wkv[2 * c]],
                            inc=(c == 15))
                    t_kmm = P.tok("tensor")
                    for c in range(16):
                        P.op("tensor", lambda T, c=c, dst=dst, base=base: T.matmul(
                            ps_v[:], dst[:, base + 128 * c:base + 128 * c + 128],
                            wv_bf[:, 512 * c:512 * c + 512],
                            start=(c == 0), stop=(c == 15), skip_group_check=True),
                            waits=([t_vevac] if (c == 0 and t_vevac) else []) + [t_wkv[2 * c + 1]],
                            inc=(c == 15))
                    t_vmm = P.tok("tensor")
                    def v_evac(V, s=s):
                        for hk in range(KVH - 1):
                            V.tensor_copy(vtok_g[:, 516 * s + 129 * hk:516 * s + 129 * hk + 128],
                                          ps_v[:, 128 * hk:128 * hk + 128])
                        return V.tensor_copy(
                            vtok_g[:, 516 * s + 129 * 3:516 * s + 129 * 3 + 128],
                            ps_v[:, 384:512])
                    t_vevac = P.op("vector", v_evac, waits=[t_vmm], inc=True)
                    def k_sq(S, lb=lb):
                        r = None
                        for hk in range(KVH):
                            r = S.activation(sq_scr[:, 0:128],
                                             ps_k[:, 128 * hk:128 * hk + 128],
                                             AF.Square, accum_out=lns[:, lb + 8 + hk:lb + 9 + hk])
                        return r
                    t_ksq = P.op("scalar", k_sq, waits=[t_kmm], inc=True)
                    t_kstd = P.op("scalar", lambda S, lb=lb: S.activation(
                        lns[:, lb + 8:lb + 12], lns[:, lb + 8:lb + 12], AF.Sqrt, bias=cst[:, 1:2],
                        scale=1.0 / HD), waits=[t_ksq], inc=True)
                    t_krs = P.op("vector", lambda V, lb=lb: V.reciprocal(
                        lns[:, lb + 8:lb + 12], lns[:, lb + 8:lb + 12]), waits=[t_kstd], inc=True)
                    t_krs2 = P.op("scalar", lambda S, lb=lb: S.copy(
                        lns[:, lb + 12:lb + 16], lns[:, lb + 8:lb + 12]),
                        waits=[t_krs], inc=True)
                    ctile = slice(64 * kt, 64 * kt + 64)
                    for hk in range(KVH):
                        s1 = slice(128 * hk, 128 * hk + 64)
                        s2 = slice(128 * hk + 64, 128 * hk + 128)
                        def k_rope_m(V, hk=hk, s1=s1, s2=s2, ctile=ctile, lb=lb):
                            rs = lns[:, lb + 12 + hk:lb + 13 + hk]
                            V.scalar_tensor_tensor(sA[:], ps_k[:, s1], rs, cosT[:, ctile],
                                                   OP.mult, OP.mult)
                            V.scalar_tensor_tensor(sB[:], ps_k[:, s2], rs, sinT[:, ctile],
                                                   OP.mult, OP.mult)
                            V.scalar_tensor_tensor(sC[:], ps_k[:, s1], rs, sinT[:, ctile],
                                                   OP.mult, OP.mult)
                            return V.scalar_tensor_tensor(sD[:], ps_k[:, s2], rs,
                                                          cosT[:, ctile], OP.mult, OP.mult)
                        t_km = P.op("vector", lambda V, f=k_rope_m: f(V),
                                    waits=[t_krs2] if hk == 0 else [], inc=True)
                        def k_rope_a(V, s1=s1, s2=s2):
                            V.tensor_add(k_bf[:, s1], sA[:], sB[:])
                            return V.tensor_sub(k_bf[:, s2], sD[:], sC[:])
                        P.op("vector", lambda V, f=k_rope_a: f(V), waits=[t_km],
                             inc=(hk == KVH - 1))
                    t_kepi = P.tok("vector")
                    if DEBUG and g == 0 and kt == 4:
                        P.dma("gpsimd", dbg["dbg_xn"][:], xn_bf[:], waits=[t_xn], slow=True)
                        P.dma("gpsimd", dbg["dbg_kbf"][:], k_bf[:], waits=[t_kepi], slow=True)
                        P.dma("gpsimd", dbg["dbg_vtok"][:], vtok_g[:, 512 * s:512 * s + 512],
                              waits=[t_vevac], slow=True)
                        P.dma("gpsimd", dbg["dbg_xnfm"][:], xnfm_g[:, 0:128],
                              waits=[t_tp_ev[0], t_tp_ev[1]], slow=True)
                        P.dma("gpsimd", dbg["dbg_lns"][:], lns[:], waits=[t_kepi], slow=True)
                    for hk in range(KVH):
                        i = hk % ntp
                        tk = P.op("tensor", lambda T, hk=hk, i=i: T.transpose(
                            ps_tp[i][:], k_bf[:, 128 * hk:128 * hk + 128], idm_bf[:]),
                            waits=[t_kepi, t_tp_ev[i]], inc=True)
                        if i % 2 == 0:
                            t_tp_ev[i] = P.op("scalar", lambda S, hk=hk, i=i, s=s: S.copy(
                                kfm_g[:, (hk * KEY_G + s) * 128:(hk * KEY_G + s) * 128 + 128],
                                ps_tp[i][:]), waits=[tk], inc=True)
                        else:
                            t_tp_ev[i] = P.op("vector", lambda V, hk=hk, i=i, s=s: V.tensor_copy(
                                kfm_g[:, (hk * KEY_G + s) * 128:(hk * KEY_G + s) * 128 + 128],
                                ps_tp[i][:]), waits=[tk], inc=True)
                    lo = (len(items) * s) // KEY_G
                    hi = (len(items) * (s + 1)) // KEY_G
                    for it in items[lo:hi]:
                        it()
                P.barrier()

            with ExitStack() as p2:
                e2 = p2.enter_context
                wq_bf = e2(nc.sbuf_tensor(f"wq_bf_g{g}", [128, 16 * D], BF16))
                q_bf = e2(nc.sbuf_tensor(f"q_bf_g{g}", [128, D], BF16))
                qfm = e2(nc.sbuf_tensor(f"qfm_g{g}", [128, 16 * 128], BF16))
                pT4 = [[e2(nc.sbuf_tensor(f"pT4_{p}_{i}_g{g}", [128, 512], BF16))
                        for i in range(5)] for p in range(2)]
                qs = e2(nc.sbuf_tensor(f"qs_g{g}", [128, 48], F32))
                sA2 = e2(nc.sbuf_tensor(f"sA2_g{g}", [128, 64], F32))
                sB2 = e2(nc.sbuf_tensor(f"sB2_g{g}", [128, 64], F32))
                sC2 = e2(nc.sbuf_tensor(f"sC2_g{g}", [128, 64], F32))
                sD2 = e2(nc.sbuf_tensor(f"sD2_g{g}", [128, 64], F32))
                sq2 = e2(nc.sbuf_tensor(f"sq2_g{g}", [128, 128], BF16))
                rsum = e2(nc.sbuf_tensor(f"rsum_g{g}", [128, 2], F32))
                ps_q = [e2(nc.psum_tensor(f"ps_q{i}_g{g}", [128, 512], F32))
                        for i in range(2)]
                ps_tp2 = [e2(nc.psum_tensor(f"ps_tp2{i}_g{g}", [128, 128], BF16))
                          for i in range(2)]
                ps_s = [e2(nc.psum_tensor(f"ps_s{i}_g{g}", [128, 512], F32))
                        for i in range(2)]
                ps_att = [e2(nc.psum_tensor(f"ps_att{i}_g{g}", [128, 129], F32))
                          for i in range(2)]

                t_wq = []
                t_wqc = {}
                items2 = []
                if g == 0:
                    wstg2 = [e2(nc.sbuf_tensor(f"wstg2{i}", [128, D], F32))
                             for i in range(2)]
                    cws2 = [e2(nc.sbuf_tensor(f"cws2{i}", [128, D], BF16))
                            for i in range(2)]
                    pst2 = {"wi": 0, "ci": 0, "wfree": [None] * 2,
                            "cfree": [None, None]}

                    def wqcast_item(c):
                        def run():
                            sl = pst2["wi"] % 2
                            pst2["wi"] += 1
                            t_ld = P.dma("sync" if c % 2 == 0 else "gpsimd",
                                         wstg2[sl][:], wqT[128 * c:128 * c + 128, :],
                                         waits=[pst2["wfree"][sl]] if pst2["wfree"][sl] else [])
                            if c % 2 == 0:
                                t_cv = P.op("scalar", lambda S, sl=sl, c=c: S.copy(
                                    wq_bf[:, D * c:D * c + D], wstg2[sl][:]),
                                    waits=[t_ld], inc=True)
                            else:
                                t_cv = P.op("vector", lambda V, sl=sl, c=c: V.tensor_copy(
                                    wq_bf[:, D * c:D * c + D], wstg2[sl][:]),
                                    waits=[t_ld], inc=True)
                            pst2["wfree"][sl] = t_cv
                            t_wqc[c] = t_cv
                            P.dma("sync", s_wq[128 * c:128 * c + 128, :],
                                  wq_bf[:, D * c:D * c + D], waits=[t_cv])
                        return run

                    def wocast_item(c):
                        def run():
                            sl = pst2["wi"] % 2
                            ob = pst2["ci"] % 2
                            pst2["wi"] += 1
                            pst2["ci"] += 1
                            t_ld = P.dma("gpsimd", wstg2[sl][:], woT[128 * c:128 * c + 128, :],
                                         waits=[pst2["wfree"][sl]] if pst2["wfree"][sl] else [])
                            cw = [t_ld]
                            if pst2["cfree"][ob]:
                                cw.append(pst2["cfree"][ob])
                            if c % 2 == 0:
                                t_cv = P.op("scalar", lambda S, sl=sl, ob=ob: S.copy(
                                    cws2[ob][:], wstg2[sl][:]), waits=cw, inc=True)
                            else:
                                t_cv = P.op("vector", lambda V, sl=sl, ob=ob: V.tensor_copy(
                                    cws2[ob][:], wstg2[sl][:]), waits=cw, inc=True)
                            pst2["wfree"][sl] = t_cv
                            pst2["cfree"][ob] = P.dma("gpsimd",
                                                      s_wo[128 * c:128 * c + 128, :],
                                                      cws2[ob][:], waits=[t_cv])
                        return run

                    items2 = [wocast_item(c) for c in range(16)]
                    for c in range(16):
                        wqcast_item(c)()
                else:
                    for c in range(16):
                        t_wq.append(P.dma("sync", wq_bf[:, D * c:D * c + D],
                                          s_wq[128 * c:128 * c + 128, :]))

                t_q_ev = [None, None]
                t_tp_ev2 = [None, None]
                t_expb = [None, None]
                t_yevb = [None, None]
                t_pfree = [None, None]
                mk_d = [None, None]
                sct = [0]
                for qt in range(OWN_G):
                    ot = 8 * g + qt
                    kt0 = qt
                    base = qt * 16 * 128
                    ctile = slice(64 * (ot + 4), 64 * (ot + 4) + 64)
                    if items2:
                        lo = (len(items2) * qt) // OWN_G
                        hi = (len(items2) * (qt + 1)) // OWN_G
                        for it in items2[lo:hi]:
                            it()
                    for s4 in range(4):
                        bq = s4 % 2
                        w0 = []
                        if t_q_ev[bq]:
                            w0.append(t_q_ev[bq])
                        for c in range(16):
                            cw = list(w0) if c == 0 else []
                            if qt == 0:
                                cw.append(t_wqc[c] if g == 0 else t_wq[c])
                            P.op("tensor", lambda T, c=c, s4=s4, bq=bq, base=base: T.matmul(
                                ps_q[bq][:], xnfm_g[:, base + 128 * c:base + 128 * c + 128],
                                wq_bf[:, D * c + 512 * s4:D * c + 512 * s4 + 512],
                                start=(c == 0), stop=(c == 15), skip_group_check=True),
                                waits=cw, inc=(c == 15))
                        t_qmm = P.tok("tensor")
                        def q_sq(S, s4=s4, bq=bq):
                            r = None
                            for j in range(4):
                                r = S.activation(sq2[:], ps_q[bq][:, 128 * j:128 * j + 128],
                                                 AF.Square,
                                                 accum_out=qs[:, 4 * s4 + j:4 * s4 + j + 1])
                            return r
                        t_qsq = P.op("scalar", q_sq, waits=[t_qmm], inc=True)
                        t_qstd = P.op("scalar", lambda S, s4=s4: S.activation(
                            qs[:, 16 + 4 * s4:20 + 4 * s4], qs[:, 4 * s4:4 * s4 + 4],
                            AF.Sqrt, bias=cst[:, 1:2], scale=1.0 / HD),
                            waits=[t_qsq], inc=True)
                        t_qrs = P.op("vector", lambda V, s4=s4: V.reciprocal(
                            qs[:, 16 + 4 * s4:20 + 4 * s4], qs[:, 16 + 4 * s4:20 + 4 * s4]),
                            waits=[t_qstd], inc=True)
                        t_qrg0 = P.op("vector", lambda V, s4=s4: V.tensor_mul(
                            qs[:, 16 + 4 * s4:20 + 4 * s4], qs[:, 16 + 4 * s4:20 + 4 * s4],
                            grep[:, 4 * s4:4 * s4 + 4]), waits=[t_qrs], inc=True)
                        t_qrg = P.op("scalar", lambda S, s4=s4: S.copy(
                            qs[:, 32 + 4 * s4:36 + 4 * s4], qs[:, 16 + 4 * s4:20 + 4 * s4]),
                            waits=[t_qrg0], inc=True)
                        for j in range(4):
                            hq = 4 * s4 + j
                            s1 = slice(128 * j, 128 * j + 64)
                            s2 = slice(128 * j + 64, 128 * j + 128)
                            o1 = slice(128 * hq, 128 * hq + 64)
                            o2 = slice(128 * hq + 64, 128 * hq + 128)
                            def q_rope_m(V, bq=bq, hq=hq, s1=s1, s2=s2, ctile=ctile):
                                rs = qs[:, 32 + hq:33 + hq]
                                V.scalar_tensor_tensor(sA2[:], ps_q[bq][:, s1], rs,
                                                       cosT[:, ctile], OP.mult, OP.mult)
                                V.scalar_tensor_tensor(sB2[:], ps_q[bq][:, s2], rs,
                                                       sinT[:, ctile], OP.mult, OP.mult)
                                V.scalar_tensor_tensor(sC2[:], ps_q[bq][:, s1], rs,
                                                       sinT[:, ctile], OP.mult, OP.mult)
                                return V.scalar_tensor_tensor(sD2[:], ps_q[bq][:, s2], rs,
                                                              cosT[:, ctile],
                                                              OP.mult, OP.mult)
                            t_qm = P.op("vector", lambda V, f=q_rope_m: f(V),
                                        waits=[t_qrg] if j == 0 else [], inc=True)
                            def q_rope_a(V, o1=o1, o2=o2):
                                V.tensor_add(q_bf[:, o1], sA2[:], sB2[:])
                                return V.tensor_sub(q_bf[:, o2], sD2[:], sC2[:])
                            P.op("vector", lambda V, f=q_rope_a: f(V), waits=[t_qm],
                                 inc=(j == 3))
                        t_q_ev[bq] = P.tok("vector")
                    t_qrope = P.tok("vector")
                    for c in range(16):
                        i = c % 2
                        w = [t_qrope]
                        if t_tp_ev2[i]:
                            w.append(t_tp_ev2[i])
                        tk = P.op("tensor", lambda T, c=c, i=i: T.transpose(
                            ps_tp2[i][:], q_bf[:, 128 * c:128 * c + 128], idm_bf[:]),
                            waits=w, inc=True)
                        if i == 0:
                            t_tp_ev2[i] = P.op("scalar", lambda S, c=c, i=i: S.copy(
                                qfm[:, 128 * c:128 * c + 128], ps_tp2[i][:]),
                                waits=[tk], inc=True)
                        else:
                            t_tp_ev2[i] = P.op("vector", lambda V, c=c, i=i: V.tensor_copy(
                                qfm[:, 128 * c:128 * c + 128], ps_tp2[i][:]),
                                waits=[tk], inc=True)
                    t_qfm = [t_tp_ev2[0], t_tp_ev2[1]]
                    if DEBUG and g == 0 and qt == 4:
                        P.dma("gpsimd", dbg["dbg_qbf"][:], q_bf[:], waits=[t_qrope], slow=True)
                        P.dma("gpsimd", dbg["dbg_qfm"][:], qfm[:], waits=list(t_qfm), slow=True)

                    def emit_scores(hk):
                        par = hk % 2
                        mk = [None] * 5
                        for j in range(5):
                            ba = sct[0] % 2
                            sct[0] += 1
                            kslot = kt0 + j
                            sw = list(t_qfm)
                            if t_expb[ba]:
                                sw.append(t_expb[ba])
                            t_smm = P.op("tensor", lambda T, hk=hk, kslot=kslot, ba=ba:
                                         T.matmul(ps_s[ba][:],
                                                  kfm_g[:, (hk * KEY_G + kslot) * 128:
                                                        (hk * KEY_G + kslot) * 128 + 128],
                                                  qfm[:, 512 * hk:512 * hk + 512],
                                                  start=True, stop=True,
                                                  skip_group_check=True),
                                         waits=sw, inc=True)
                            ew = [t_smm]
                            if t_pfree[par]:
                                ew.append(t_pfree[par])
                            t_expb[ba] = P.op("scalar", lambda S, ba=ba, par=par, j=j:
                                              S.activation(pT4[par][j][:], ps_s[ba][:],
                                                           AF.Exp, bias=negBmax[:, 0:1],
                                                           scale=sc128[:, 0:1]),
                                              waits=ew, inc=True)
                            tok = t_expb[ba]
                            if j == 0:
                                m0use = m0f_bf if (8 * g + qt) <= 3 else m0_bf
                                tok = P.op("vector", lambda V, par=par, m0use=m0use:
                                           [V.tensor_mul(pT4[par][0][:, 128 * u:128 * u + 128],
                                                         pT4[par][0][:, 128 * u:128 * u + 128],
                                                         m0use[:]) for u in range(4)][-1],
                                           waits=[tok], inc=True)
                            if j == 4:
                                tok = P.op("vector", lambda V, par=par:
                                           [V.tensor_mul(pT4[par][4][:, 128 * u:128 * u + 128],
                                                         pT4[par][4][:, 128 * u:128 * u + 128],
                                                         m4_bf[:]) for u in range(4)][-1],
                                           waits=[tok], inc=True)
                            if 0 < j < 4 and 8 * g + qt + j <= 3:
                                tok = P.op("vector", lambda V, par=par, j=j:
                                           V.tensor_scalar_mul(pT4[par][j][:], pT4[par][j][:],
                                                               flag_sb[:, 0:1]),
                                           waits=[tok], inc=True)
                            mk[j] = tok
                        mk_d[par] = mk

                    def emit_av(hk):
                        par = hk % 2
                        for hl in range(4):
                            h = 4 * hk + hl
                            bv = h % 2
                            aw = list(mk_d[par])
                            if t_yevb[bv]:
                                aw.append(t_yevb[bv])
                            for j in range(5):
                                kslot = kt0 + j
                                P.op("tensor", lambda T, j=j, bv=bv, hk=hk, hl=hl,
                                     kslot=kslot, par=par:
                                     T.matmul(ps_att[bv][:],
                                              pT4[par][j][:, 128 * hl:128 * hl + 128],
                                              vtok_g[:, 516 * kslot + 129 * hk:
                                                     516 * kslot + 129 * hk + 129],
                                              start=(j == 0), stop=(j == 4),
                                              skip_group_check=True),
                                     waits=aw if j == 0 else [], inc=(j == 4))
                            t_av = P.tok("tensor")
                            if hl == 3:
                                t_pfree[par] = t_av
                            t_rsum = P.op("vector", lambda V, bv=bv: V.reciprocal(
                                rsum[:, bv:bv + 1], ps_att[bv][:, 128:129]),
                                waits=[t_av], inc=True)
                            P.op("scalar", lambda S, bv=bv, qt=qt, h=h: S.activation(
                                y_g[:, (qt * H + h) * 128:(qt * H + h) * 128 + 128],
                                ps_att[bv][:, 0:128], AF.Identity,
                                scale=rsum[:, bv:bv + 1]), waits=[t_rsum], inc=True)
                            t_yevb[bv] = P.tok("scalar")

                    for hk in range(KVH):
                        emit_scores(hk)
                        if hk > 0:
                            emit_av(hk - 1)
                    emit_av(KVH - 1)
                P.barrier()

            p12.close()
            with ExitStack() as p3:
                e3 = p3.enter_context
                wo_bf = e3(nc.sbuf_tensor(f"wo_bf_g{g}", [128, 16 * D], BF16))
                weff_sb = e3(nc.sbuf_tensor(f"weff_sb_g{g}", [128, 16 * D], BF16))
                xld3 = [e3(nc.sbuf_tensor(f"xld3{i}_g{g}", [128, D], F32)) for i in range(2)]
                xn2a = e3(nc.sbuf_tensor(f"xn2a_g{g}", [128, 2 * D], BF16))
                sq_scr3 = xn2a
                xnfm3 = e3(nc.sbuf_tensor(f"xnfm3_g{g}", [128, 16 * 128], BF16))
                yfm3 = e3(nc.sbuf_tensor(f"yfm3_g{g}", [128, 16 * 128], BF16))
                obuf = [e3(nc.sbuf_tensor(f"obuf{i}_g{g}", [128, 512], F32)) for i in range(2)]
                lns3 = e3(nc.sbuf_tensor(f"lns3_g{g}", [128, 32], F32))
                ps_o = [e3(nc.psum_tensor(f"ps_o{i}_g{g}", [128, 512], F32)) for i in range(2)]
                ps_tp3 = [e3(nc.psum_tensor(f"ps_tp3{i}_g{g}", [128, 128], BF16))
                          for i in range(4)]

                t_wo = []
                for c in range(16):
                    t_wo.append(P.dma("gpsimd", wo_bf[:, D * c:D * c + D],
                                      s_wo[128 * c:128 * c + 128, :]))
                    t_wo.append(P.dma("gpsimd", weff_sb[:, D * c:D * c + D],
                                      s_weff[128 * c:128 * c + 128, :]))

                t_xfree3 = [None, None]
                t_xld3 = [None, None]
                t_tp_ev3 = [None] * 4
                t_xtp3_last = [None, None]
                t_xn2a_arr = [None, None]
                t_o_ev = [None, None]
                t_ob_st = [None, None]

                def ln3_emit(tt):
                    ot = 8 * g + tt
                    kt = ot + 4
                    b = tt % 2
                    lb = 16 * b
                    xb = D * b
                    t_xld3[b] = P.dma("sync", xld3[b][:], xin[128 * kt:128 * kt + 128, :],
                                      waits=[t_xfree3[b]] if t_xfree3[b] else [])
                    t_s1 = P.op("vector", lambda V, b=b, lb=lb: V.tensor_reduce(
                        lns3[:, lb + 0:lb + 1], xld3[b][:], AX.X, OP.add), waits=[t_xld3[b]],
                        inc=True)
                    sqw = [t_xld3[b]]
                    if t_xtp3_last[b]:
                        sqw.append(t_xtp3_last[b])
                    t_s2 = P.op("scalar", lambda S, b=b, lb=lb, xb=xb: S.activation(
                        sq_scr3[:, xb:xb + D], xld3[b][:], AF.Square,
                        accum_out=lns3[:, lb + 1:lb + 2]),
                        waits=sqw, inc=True)
                    def ln_a3(V, lb=lb):
                        V.tensor_scalar_mul(lns3[:, lb + 2:lb + 3], lns3[:, lb + 0:lb + 1], 1.0 / D)
                        return V.scalar_tensor_tensor(lns3[:, lb + 3:lb + 4], lns3[:, lb + 0:lb + 1],
                                                      1.0 / (D * D),
                                                      lns3[:, lb + 0:lb + 1], OP.mult, OP.mult)
                    t_lna = P.op("vector", ln_a3, waits=[t_s1], inc=True)
                    t_var = P.op("vector", lambda V, lb=lb: V.scalar_tensor_tensor(
                        lns3[:, lb + 4:lb + 5], lns3[:, lb + 1:lb + 2], 1.0 / D,
                        lns3[:, lb + 3:lb + 4],
                        OP.mult, OP.subtract), waits=[t_lna, t_s2], inc=True)
                    t_std = P.op("scalar", lambda S, lb=lb: S.activation(
                        lns3[:, lb + 5:lb + 6], lns3[:, lb + 4:lb + 5], AF.Sqrt, bias=cst[:, 0:1]),
                        waits=[t_var], inc=True)
                    t_rstd = P.op("vector", lambda V, lb=lb: V.reciprocal(
                        lns3[:, lb + 6:lb + 7], lns3[:, lb + 5:lb + 6]), waits=[t_std], inc=True)
                    t_r2 = P.op("vector", lambda V, lb=lb: V.tensor_mul(
                        lns3[:, lb + 7:lb + 8], lns3[:, lb + 6:lb + 7], lns3[:, lb + 6:lb + 7]),
                        waits=[t_rstd], inc=True)
                    t_v2 = P.op("vector", lambda V, lb=lb: V.tensor_mul(
                        lns3[:, lb + 8:lb + 9], lns3[:, lb + 7:lb + 8], lns3[:, lb + 4:lb + 5]),
                        waits=[t_r2], inc=True)
                    t_std2 = P.op("scalar", lambda S, lb=lb: S.activation(
                        lns3[:, lb + 9:lb + 10], lns3[:, lb + 8:lb + 9], AF.Sqrt, bias=cst[:, 0:1]),
                        waits=[t_v2], inc=True)
                    t_rstd2 = P.op("vector", lambda V, lb=lb: V.reciprocal(
                        lns3[:, lb + 10:lb + 11], lns3[:, lb + 9:lb + 10]), waits=[t_std2], inc=True)
                    t_sc1 = P.op("vector", lambda V, lb=lb: V.tensor_mul(
                        lns3[:, lb + 11:lb + 12], lns3[:, lb + 6:lb + 7], lns3[:, lb + 10:lb + 11]),
                        waits=[t_rstd2], inc=True)
                    t_ab = P.op("vector", lambda V, lb=lb: V.scalar_tensor_tensor(
                        lns3[:, lb + 12:lb + 13], lns3[:, lb + 2:lb + 3], -1.0,
                        lns3[:, lb + 11:lb + 12],
                        OP.mult, OP.mult), waits=[t_sc1], inc=True)
                    t_xn2a_arr[b] = P.op("scalar", lambda S, b=b, lb=lb, xb=xb: S.activation(
                        xn2a[:, xb:xb + D], xld3[b][:], AF.Identity, bias=lns3[:, lb + 12:lb + 13],
                        scale=lns3[:, lb + 11:lb + 12]), waits=[t_ab], inc=True)

                ln3_emit(0)
                for tt in range(OWN_G):
                    ot = 8 * g + tt
                    kt = ot + 4
                    b = tt % 2
                    xb = D * b
                    if tt + 1 < OWN_G:
                        ln3_emit(tt + 1)
                    t_xn2a = t_xn2a_arr[b]
                    for c in range(16):
                        i = c % 4
                        w = [t_xn2a]
                        if t_tp_ev3[i]:
                            w.append(t_tp_ev3[i])
                        tk = P.op("tensor", lambda T, c=c, i=i, xb=xb: T.transpose(
                            ps_tp3[i][:], xn2a[:, xb + 128 * c:xb + 128 * c + 128], idm_bf[:]),
                            waits=w, inc=True)
                        if c == 15:
                            t_xtp3_last[b] = tk
                        if i % 2 == 0:
                            t_tp_ev3[i] = P.op("scalar", lambda S, c=c, i=i: S.copy(
                                xnfm3[:, 128 * c:128 * c + 128], ps_tp3[i][:]),
                                waits=[tk], inc=True)
                        else:
                            t_tp_ev3[i] = P.op("vector", lambda V, c=c, i=i: V.tensor_copy(
                                xnfm3[:, 128 * c:128 * c + 128], ps_tp3[i][:]),
                                waits=[tk], inc=True)
                    for hh in range(16):
                        i = hh % 4
                        tk = P.op("tensor", lambda T, hh=hh, i=i, tt=tt: T.transpose(
                            ps_tp3[i][:], y_g[:, (tt * H + hh) * 128:(tt * H + hh) * 128 + 128],
                            idm_bf[:]), waits=[t_tp_ev3[i]], inc=True)
                        if i % 2 == 0:
                            t_tp_ev3[i] = P.op("scalar", lambda S, hh=hh, i=i: S.copy(
                                yfm3[:, 128 * hh:128 * hh + 128], ps_tp3[i][:]),
                                waits=[tk], inc=True)
                        else:
                            t_tp_ev3[i] = P.op("vector", lambda V, hh=hh, i=i: V.tensor_copy(
                                yfm3[:, 128 * hh:128 * hh + 128], ps_tp3[i][:]),
                                waits=[tk], inc=True)
                    t_fm_ready = [t for t in t_tp_ev3 if t]
                    for s4 in range(4):
                        bo = s4 % 2
                        w0 = list(t_fm_ready)
                        if t_o_ev[bo]:
                            w0.append(t_o_ev[bo])
                        for c in range(16):
                            cw = list(w0) if c == 0 else []
                            if tt == 0:
                                cw.append(t_wo[2 * c])
                            P.op("tensor", lambda T, c=c, s4=s4, bo=bo: T.matmul(
                                ps_o[bo][:], yfm3[:, 128 * c:128 * c + 128],
                                wo_bf[:, D * c + 512 * s4:D * c + 512 * s4 + 512],
                                start=(c == 0), stop=False, skip_group_check=True),
                                waits=cw)
                        for c in range(16):
                            P.op("tensor", lambda T, c=c, s4=s4, bo=bo: T.matmul(
                                ps_o[bo][:], xnfm3[:, 128 * c:128 * c + 128],
                                weff_sb[:, D * c + 512 * s4:D * c + 512 * s4 + 512],
                                start=False, stop=False, skip_group_check=True),
                                waits=[t_wo[2 * c + 1]] if tt == 0 else [])
                        P.op("tensor", lambda T, s4=s4, bo=bo: T.matmul(
                            ps_o[bo][:], ones_r_bf[:], c_bf[0:1, 512 * s4:512 * s4 + 512],
                            start=False, stop=True, skip_group_check=True), inc=True)
                        t_omm = P.tok("tensor")
                        ww = [t_omm]
                        if t_ob_st[bo]:
                            ww.append(t_ob_st[bo])
                        t_o_ev[bo] = P.op("vector", lambda V, s4=s4, bo=bo, b=b: V.tensor_add(
                            obuf[bo][:], ps_o[bo][:], xld3[b][:, 512 * s4:512 * s4 + 512]),
                            waits=ww, inc=True)
                        t_ob_st[bo] = P.dma("sync",
                                            yout[128 * ot:128 * ot + 128,
                                                 512 * s4:512 * s4 + 512],
                                            obuf[bo][:], waits=[t_o_ev[bo]])
                        if s4 == 3:
                            t_xfree3[b] = t_o_ev[bo]
                P.barrier()

        for g in range(NG):
            emit_group(g)

    with nc.Block() as block:
        P.emit(block)
    stack.close()
    return nc


_prog_cache = {}


def kernel(L0_attn_x, L0_attn_Wq, L0_attn_Wk, L0_attn_Wv, L0_attn_Wo,
           L0_attn_q_gain, L0_ffn_W, L0_ffn_ln_w, L0_ffn_ln_b):
    x = np.ascontiguousarray(np.asarray(L0_attn_x, np.float32))
    B, S, _ = x.shape
    wqT = np.ascontiguousarray(np.asarray(L0_attn_Wq, np.float32).T)
    wkT = np.ascontiguousarray(np.asarray(L0_attn_Wk, np.float32).T)
    wvT = np.ascontiguousarray(np.asarray(L0_attn_Wv, np.float32).T)
    woT = np.ascontiguousarray(np.asarray(L0_attn_Wo, np.float32).T)
    wff = np.ascontiguousarray(np.asarray(L0_ffn_W, np.float32))
    lnw = np.ascontiguousarray(np.asarray(L0_ffn_ln_w, np.float32).reshape(16, 128).T)
    lnb = np.ascontiguousarray(np.asarray(L0_ffn_ln_b, np.float32).reshape(16, 128).T)
    qgain = np.asarray(L0_attn_q_gain, np.float32)
    iq = np.arange(128)[None, :]
    ik = np.arange(128)[:, None]
    m0 = (ik >= iq + 1).astype(np.float32)
    m4 = (ik <= iq).astype(np.float32)
    idm = np.eye(128, dtype=np.float32)

    if "tabs" not in _prog_cache:
        import jax
        import jax.numpy as jnp
        with jax.default_device(jax.devices("cpu")[0]):
            half = HD // 2
            inv_freq = 10000.0 ** (-jnp.arange(0, half, dtype=jnp.float32) / half)
            freqs = jnp.arange(S, dtype=jnp.float32)[:, None] * inv_freq[None, :]
            _prog_cache["tabs"] = (np.asarray(jnp.cos(freqs)), np.asarray(jnp.sin(freqs)))
    cos_full, sin_full = _prog_cache["tabs"]

    if "nc" not in _prog_cache:
        _prog_cache["nc"] = build_program()
    nc = _prog_cache["nc"]

    in_maps = []
    for c in range(8):
        b, h = c // 2, c % 2
        own = x[b, 2048 * h:2048 * h + 2048]
        halo = x[b, 2048 * h - 512:2048 * h] if h == 1 else np.zeros((512, 2048), np.float32)
        xin = np.concatenate([halo, own], axis=0)
        rows = np.clip(np.arange(2048 * h - 512, 2048 * h + 2048), 0, S - 1)
        cosTh = np.ascontiguousarray(
            cos_full[rows].reshape(NT_KEY, 128, 64).transpose(1, 0, 2).reshape(128, TW))
        sinTh = np.ascontiguousarray(
            sin_full[rows].reshape(NT_KEY, 128, 64).transpose(1, 0, 2).reshape(128, TW))
        flag = np.full(128, float(h), np.float32)
        in_maps.append(dict(xin=xin, wqT=wqT, wkT=wkT, wvT=wvT, woT=woT, wff=wff,
                            lnw=lnw, lnb=lnb, qgain=qgain, cosTh=cosTh, sinTh=sinTh,
                            flag=flag, m0=m0, m4=m4, idm=idm))
    res = run_bass_kernel_spmd(nc, in_maps, core_ids=list(range(8)))
    out = np.empty((B, S, D), res.results[0]["yout"].dtype)
    for c in range(8):
        b, h = c // 2, c % 2
        out[b, 2048 * h:2048 * h + 2048] = res.results[c]["yout"]
    return out
